# version 18
# speedup vs baseline: 1.1987x; 1.0997x over previous
import numpy as np
import ml_dtypes
import concourse.bass as bass
import concourse.mybir as mybir
from concourse.bass_utils import run_bass_kernel_spmd

F32 = mybir.dt.float32
BF16 = mybir.dt.bfloat16
AF = mybir.ActivationFunctionType
ALU = mybir.AluOpType
AX = mybir.AxisListType

D = 1024
DC = 8
S = 4096
B = 4
TOK = 2048
NSLOT = 4
SL = 512
NCORE = 8
POS = [[0, 3, 4, 7], [1, 2, 5, 6]]
LN_EPS = 1e-5

ENGINES = ("tensor", "vector", "scalar", "gpsimd", "sync")
DEBUG_A = False
DEBUG_F = False
_LAST_RES = None


class _Op:
    __slots__ = ("eng", "seq", "fn", "reads", "writes", "is_dma", "needs_inc",
                 "waits", "inc_count", "tag", "dma_slot", "dma_use")

    def __init__(self, eng, seq, fn, reads, writes, is_dma, tag):
        self.eng = eng
        self.seq = seq
        self.fn = fn
        self.reads = reads
        self.writes = writes
        self.is_dma = is_dma
        self.needs_inc = False
        self.waits = {}
        self.inc_count = 0
        self.tag = tag
        self.dma_slot = -1
        self.dma_use = 0


def _rkey(r):
    sub = None
    if isinstance(r, tuple):
        r, sub = r
    h = getattr(r, "tensor", r)
    name = getattr(h, "name", None)
    if name is None:
        name = id(h)
    return (name, sub)


NDMASEM = 8


class Emitter:

    def __init__(self, nc):
        self.nc = nc
        self.ops = []
        self.last_writer = {}
        self.readers = {}
        self.per_eng = {e: [] for e in ENGINES}
        self.alias = {}
        self.dma_count = {e: 0 for e in ENGINES}

    def set_alias(self, tensors, group):
        for t in tensors:
            self.alias[getattr(t, "name", t)] = group

    def set_pages(self, tensor, base, nbytes):
        self.alias[getattr(tensor, "name", tensor)] = ("PAGES", base, nbytes)

    def _keys(self, r):
        name, sub = _rkey(r)
        g = self.alias.get(name)
        if g is None:
            return [(name, sub)]
        if isinstance(g, tuple) and g and g[0] == "PAGES":
            base, nb = g[1], g[2]
            if isinstance(sub, tuple) and sub and sub[0] == "B":
                lo, hi = base + sub[1], base + sub[2]
            else:
                lo, hi = base, base + nb
            return [("pg", p) for p in range(lo // PGSZ, (hi + PGSZ - 1) // PGSZ)]
        return [(g, None)]

    def emit(self, eng, fn, reads=(), writes=(), dma=False, after=(), tag=""):
        op = _Op(eng, len(self.ops), fn,
                 [k for r in reads for k in self._keys(r)],
                 [k for r in writes for k in self._keys(r)], dma, tag)
        if dma:
            i = self.dma_count[eng]
            self.dma_count[eng] = i + 1
            op.dma_slot = i % NDMASEM
            op.dma_use = i // NDMASEM + 1
        deps = list(after)
        for r in op.reads:
            w = self.last_writer.get(r)
            if w is not None:
                deps.append(w)
        for r in op.writes:
            w = self.last_writer.get(r)
            if w is not None:
                deps.append(w)
            deps.extend(self.readers.get(r, ()))
        for d in deps:
            if d is op:
                continue
            if d.eng == eng == "tensor" and not d.is_dma and not dma:
                continue
            ch = ("d", d.eng, d.dma_slot) if d.is_dma else ("e", d.eng)
            cur = op.waits.get(ch)
            if cur is None or d.seq > cur.seq:
                op.waits[ch] = d
        for r in op.writes:
            self.last_writer[r] = op
            self.readers[r] = []
        for r in op.reads:
            self.readers.setdefault(r, []).append(op)
        self.ops.append(op)
        self.per_eng[eng].append(op)
        return op

    def build(self, block_name=None):
        nc = self.nc
        for op in self.ops:
            for d in op.waits.values():
                if not d.is_dma:
                    d.needs_inc = True
        sem_final = {}
        for e in ENGINES:
            c = 0
            for op in self.per_eng[e]:
                if op.needs_inc and not op.is_dma:
                    c += 1
                op.inc_count = c
            sem_final[e] = c
        sems = {e: nc.alloc_semaphore(f"em_{e}") for e in ENGINES}
        dsems = {e: [nc.alloc_semaphore(f"dm_{e}{i}") for i in range(NDMASEM)]
                 for e in ENGINES if self.dma_count[e] > 0}

        def _resolve(ch, dop):
            if ch[0] == "d":
                return dsems[ch[1]][ch[2]], 16 * dop.dma_use
            return sems[ch[1]], dop.inc_count

        def runner(e):
            def run(engine):
                clock = {}
                for op in self.per_eng[e]:
                    if op.is_dma and op.dma_use > 1:
                        ch = ("d", e, op.dma_slot)
                        v = 16 * (op.dma_use - 1)
                        if clock.get(ch, 0) < v:
                            engine.wait_ge(dsems[e][op.dma_slot], v)
                            clock[ch] = v
                    for ch in sorted(op.waits, key=str):
                        sem, v = _resolve(ch, op.waits[ch])
                        if v > clock.get(ch, 0):
                            engine.wait_ge(sem, v)
                            clock[ch] = v
                    inst = op.fn(engine)
                    if op.is_dma:
                        inst.then_inc(dsems[e][op.dma_slot], 16)
                    elif op.needs_inc:
                        inst.then_inc(sems[e], 1)
            return run

        with nc.Block(block_name) as blk:
            blk.tensor(runner("tensor"))
            blk.vector(runner("vector"))
            blk.scalar(runner("scalar"))
            blk.gpsimd(runner("gpsimd"))
            blk.sync(runner("sync"))
        return sems, sem_final


def _mk_nc():
    return bass.Bass("TRN2", target_bir_lowering=False)


def _in(nc, name, shape, dtype=F32):
    return nc.dram_tensor(name, shape, dtype, kind="ExternalInput").ap()


def _out(nc, name, shape, dtype=F32):
    return nc.dram_tensor(name, shape, dtype, kind="ExternalOutput").ap()


def load_fm(em, dst, src, chunks, width, eng="sync", tag="ld"):
    ap = src.rearrange("(c p) x -> p c x", p=128)
    dv = dst[:, :].rearrange("p (c x) -> p c x", c=chunks)
    em.emit(eng, lambda e: e.dma_start(out=dv, in_=ap), writes=[dst], dma=True, tag=tag)


def store_fm(em, dst, src_sb, chunks, width, col0=0, eng="sync", tag="st", reads=()):
    ap = dst.rearrange("(c p) x -> p c x", p=128)
    sv = src_sb[:, col0:col0 + chunks * width].rearrange("p (c x) -> p c x", c=chunks)
    return em.emit(eng, lambda e: e.dma_start(out=ap, in_=sv),
                   reads=list(reads) or [src_sb], dma=True, tag=tag)


class Common:

    def __init__(self, nc, em):
        self.nc, self.em = nc, em
        self.ones_mean = nc.alloc_sbuf_tensor("ones_mean", [128, 1], BF16)
        self.ones_row = nc.alloc_sbuf_tensor("ones_row", [1, 128], F32)
        self.ones_rowb = nc.alloc_sbuf_tensor("ones_rowb", [1, 128], BF16)
        self.ones_colb = nc.alloc_sbuf_tensor("ones_colb", [128, 1], BF16)
        em.emit("vector", lambda e: e.memset(self.ones_mean[:, :], 1.0 / D),
                writes=[self.ones_mean])
        em.emit("vector", lambda e: e.memset(self.ones_row[:, :], 1.0),
                writes=[self.ones_row])
        em.emit("vector", lambda e: e.memset(self.ones_rowb[:, :], 1.0),
                writes=[self.ones_rowb])
        em.emit("vector", lambda e: e.memset(self.ones_colb[:, :], 1.0),
                writes=[self.ones_colb])
        self.eps = nc.alloc_sbuf_tensor("eps_c", [1, 1], F32)
        em.emit("vector", lambda e: e.memset(self.eps[:, :], float(LN_EPS)),
                writes=[self.eps])
        self.xrb = self.x2b = self.rows = None
        self.Ab = nc.alloc_sbuf_tensor("ln_Ab", [128, SL], F32)
        self.Bb = nc.alloc_sbuf_tensor("ln_Bb", [128, SL], F32)
        self.tmp = nc.alloc_sbuf_tensor("ln_tmp", [128, SL], F32)

    def attach_ln_scratch(self, xrb, x2b, rows):
        self.xrb, self.x2b, self.rows = xrb, x2b, rows

    def layernorm_nt(self, xres, bk_stat, bk_bcast, out_f32=None, out_bf16=None):
        em = self.em
        xrb, x2b = self.xrb, self.x2b
        R = self.rows
        mrow, erow = R[0:1, 0:SL], R[0:1, SL:2 * SL]
        m2 = R[0:1, 2 * SL:3 * SL]
        var = R[0:1, SL:2 * SL]
        lnv = R[0:1, 2 * SL:3 * SL]
        rstd = R[0:1, SL:2 * SL]
        mr = negmr = R[0:1, 2 * SL:3 * SL]
        em.emit("scalar", lambda e: e.activation(x2b[:, 0:DC * SL], xres[:, 0:DC * SL], AF.Square),
                reads=[xres], writes=[x2b])
        for fc in range(DC):
            sl = slice(fc * SL, (fc + 1) * SL)
            bsl = ("B", fc * SL * 2, (fc + 1) * SL * 2)
            em.emit("vector", lambda e, sl=sl: e.tensor_copy(xrb[:, sl], xres[:, sl]),
                    reads=[xres], writes=[(xrb, bsl)])
            em.emit("tensor", lambda e, sl=sl, fc=fc: e.matmul(
                bk_stat[0:1, :], self.ones_mean[:, :], xrb[:, sl],
                start=(fc == 0), stop=(fc == DC - 1)),
                reads=[(xrb, bsl), self.ones_mean], writes=[bk_stat])
            em.emit("tensor", lambda e, sl=sl, fc=fc: e.matmul(
                bk_bcast[0:1, :], self.ones_mean[:, :], x2b[:, sl],
                start=(fc == 0), stop=(fc == DC - 1)),
                reads=[(x2b, bsl), self.ones_mean], writes=[bk_bcast])
        em.emit("vector", lambda e: e.tensor_copy(mrow, bk_stat[0:1, :]),
                reads=[bk_stat], writes=[R])
        em.emit("vector", lambda e: e.tensor_copy(erow, bk_bcast[0:1, :]),
                reads=[bk_bcast], writes=[R])
        em.emit("vector", lambda e: e.tensor_mul(m2, mrow, mrow), reads=[R], writes=[R])
        em.emit("vector", lambda e: e.tensor_sub(var, erow, m2), reads=[R], writes=[R])
        em.emit("scalar", lambda e: e.activation(lnv, var, AF.Ln, bias=self.eps[0:1, 0:1]),
                reads=[R, self.eps], writes=[R])
        em.emit("scalar", lambda e: e.activation(rstd, lnv, AF.Exp, scale=-0.5),
                reads=[R], writes=[R])
        em.emit("vector", lambda e: e.tensor_mul(mr, mrow, rstd), reads=[R], writes=[R])
        em.emit("vector", lambda e: e.tensor_scalar(negmr, mr, -1.0, None, ALU.mult),
                reads=[R], writes=[R])
        em.emit("tensor", lambda e: e.matmul(bk_stat[:, :], self.ones_row[:, :], rstd,
                                             start=True, stop=True),
                reads=[R, self.ones_row], writes=[bk_stat])
        em.emit("vector", lambda e: e.tensor_copy(self.Ab[:, :], bk_stat[:, :]),
                reads=[bk_stat], writes=[self.Ab])
        em.emit("tensor", lambda e: e.matmul(bk_bcast[:, :], self.ones_row[:, :], negmr,
                                             start=True, stop=True),
                reads=[R, self.ones_row], writes=[bk_bcast])
        em.emit("vector", lambda e: e.tensor_copy(self.Bb[:, :], bk_bcast[:, :]),
                reads=[bk_bcast], writes=[self.Bb])
        for fc in range(DC):
            sl = slice(fc * SL, (fc + 1) * SL)
            em.emit("vector", lambda e, sl=sl: e.tensor_mul(self.tmp[:, :], xres[:, sl], self.Ab[:, :]),
                    reads=[xres, self.Ab], writes=[self.tmp])
            if out_f32 is not None:
                em.emit("vector", lambda e, sl=sl: e.tensor_add(out_f32[:, sl], self.tmp[:, :], self.Bb[:, :]),
                        reads=[self.tmp, self.Bb], writes=[out_f32])
                if out_bf16 is not None:
                    em.emit("scalar", lambda e, sl=sl: e.activation(out_bf16[:, sl], out_f32[:, sl], AF.Copy),
                            reads=[out_f32], writes=[out_bf16])
            elif out_bf16 is not None:
                em.emit("vector", lambda e, sl=sl: e.tensor_add(out_bf16[:, sl], self.tmp[:, :], self.Bb[:, :]),
                        reads=[self.tmp, self.Bb], writes=[out_bf16])


def build_A():
    nc = _mk_nc()
    xseq = _in(nc, "xseq", [D, S])
    WvT = _in(nc, "WvT", [D, 256])
    WuT = _in(nc, "WuT", [256, 3 * D])
    WovT = _in(nc, "WovT", [D, 256])
    WouT = _in(nc, "WouT", [256, D])
    x1 = _out(nc, "x1", [D, TOK])

    em = Emitter(nc)
    cm = Common(nc, em)

    xb = nc.alloc_sbuf_tensor("xb", [128, DC * S], BF16)
    xb_off = nc.lookup_mloc(xb).addr
    xres = nc.alloc_sbuf_tensor_at("xres", [128, DC * SL], F32, offset=xb_off)
    xot = nc.alloc_sbuf_tensor_at("xot", [128, DC * SL], F32, offset=xb_off + 16384)
    xrb = nc.alloc_sbuf_tensor_at("ln_xrb", [128, DC * SL], BF16, offset=xb_off + 32768)
    x2b = nc.alloc_sbuf_tensor_at("ln_x2b", [128, DC * SL], BF16, offset=xb_off + 40960)
    lrows = nc.alloc_sbuf_tensor_at("ln_rows", [1, 4 * SL], F32, offset=xb_off + 49152)
    em.set_alias([xb, xres, xot, xrb, x2b, lrows], "R1")
    cm.attach_ln_scratch(xrb, x2b, lrows)

    wv = nc.alloc_sbuf_tensor("wv", [128, DC * 256], BF16)
    wu = nc.alloc_sbuf_tensor("wu", [128, 2 * 3 * D], BF16)
    wov = nc.alloc_sbuf_tensor("wov", [128, DC * 256], BF16)
    wou = nc.alloc_sbuf_tensor("wou", [128, 2 * D], BF16)
    t = nc.alloc_sbuf_tensor("t", [128, 2 * S], BF16)
    Qg = nc.alloc_sbuf_tensor("Qg", [128, S], BF16)
    Kg = nc.alloc_sbuf_tensor("Kg", [128, TOK], BF16)
    Vg = nc.alloc_sbuf_tensor("Vg", [128, 32 * 130], BF16)
    y = nc.alloc_sbuf_tensor("y", [128, DC * TOK], BF16)
    z1 = nc.alloc_sbuf_tensor("z1", [128, 2 * TOK], BF16)
    Pp = nc.alloc_sbuf_tensor("Pp", [128, 4 * SL], BF16)
    sq = nc.alloc_sbuf_tensor("sq", [128, S], BF16)
    r_sc = nc.alloc_sbuf_tensor("r_sc", [1, 528], F32)
    negc = nc.alloc_sbuf_tensor("negc", [128, 1], F32)
    rl = nc.alloc_sbuf_tensor("rl", [1, SL], F32)
    rlb = nc.alloc_sbuf_tensor("rlb", [64, SL], F32)

    bk = [nc.alloc_psum_tensor(f"bk{i}", [128, 512], F32) for i in range(8)]

    E = em.emit

    load_fm(em, xb, xseq, DC, S, eng="gpsimd")
    load_fm(em, wv, WvT, DC, 256, eng="gpsimd")
    load_fm(em, wu, WuT, 2, 3 * D, eng="gpsimd")
    load_fm(em, wov, WovT, DC, 256, eng="gpsimd")
    load_fm(em, wou, WouT, 2, D, eng="gpsimd")

    for rc in range(2):
        for ntile in range(S // 512):
            pb = bk[ntile % 2]
            for dc in range(DC):
                E("tensor", lambda e, rc=rc, nt=ntile, dc=dc, pb=pb: e.matmul(
                    pb[:, :], wv[:, dc * 256 + rc * 128: dc * 256 + rc * 128 + 128],
                    xb[:, dc * S + nt * 512: dc * S + nt * 512 + 512],
                    start=(dc == 0), stop=(dc == DC - 1)),
                    reads=[wv, xb], writes=[pb])
            E("scalar", lambda e, rc=rc, nt=ntile, pb=pb: e.activation(
                t[:, rc * S + nt * 512: rc * S + nt * 512 + 512], pb[:, :], AF.Copy),
                reads=[pb], writes=[(t, (rc, ntile))])

    for g in range(8):
        fq = D + g * 128
        fk = g * 128
        fv = 2 * D + g * 128
        for ntile in range(S // 512):
            pb = bk[ntile % 2]
            for rc in range(2):
                E("tensor", lambda e, nt=ntile, rc=rc, pb=pb, fq=fq: e.matmul(
                    pb[:, :], wu[:, rc * 3 * D + fq: rc * 3 * D + fq + 128],
                    t[:, rc * S + nt * 512: rc * S + nt * 512 + 512],
                    start=(rc == 0), stop=(rc == 1)),
                    reads=[wu, (t, (rc, ntile))], writes=[pb])
            E("scalar", lambda e, nt=ntile, pb=pb: e.activation(
                Qg[:, nt * 512: nt * 512 + 512], pb[:, :], AF.Copy),
                reads=[pb], writes=[(Qg, ntile)])
        for ntile in range(TOK // 512):
            pb = bk[2 + ntile % 2]
            for rc in range(2):
                E("tensor", lambda e, nt=ntile, rc=rc, pb=pb, fk=fk: e.matmul(
                    pb[:, :], wu[:, rc * 3 * D + fk: rc * 3 * D + fk + 128],
                    t[:, rc * S + nt * 512: rc * S + nt * 512 + 512],
                    start=(rc == 0), stop=(rc == 1)),
                    reads=[wu, (t, (rc, ntile))], writes=[pb])
            E("scalar", lambda e, nt=ntile, pb=pb: e.activation(
                Kg[:, nt * 512: nt * 512 + 512], pb[:, :], AF.Copy),
                reads=[pb], writes=[(Kg, ntile)])
        for jc in range(32):
            pb = bk[4 + jc % 2]
            for rc in range(2):
                E("tensor", lambda e, jc=jc, rc=rc, pb=pb, fv=fv: e.matmul(
                    pb[:, 0:128], t[:, rc * S + jc * 128: rc * S + jc * 128 + 128],
                    wu[:, rc * 3 * D + fv: rc * 3 * D + fv + 128],
                    start=(rc == 0), stop=(rc == 1)),
                    reads=[wu, (t, (rc, jc // 4))], writes=[pb])
            for h in range(2):
                E("vector", lambda e, jc=jc, h=h, pb=pb: e.tensor_copy(
                    Vg[:, jc * 130 + h * 65: jc * 130 + h * 65 + 64],
                    pb[:, h * 64: h * 64 + 64]),
                    reads=[pb], writes=[(Vg, jc)])
        E("vector", lambda e: e.memset(
            Vg[:, :].rearrange("p (j h c) -> p j h c", j=32, h=2)[:, :, :, 64:65], 1.0),
            writes=[(Vg, "ones")])

        scr = r_sc[0:1, 0:512]
        mk2 = r_sc[0:1, 512:513]
        mq2 = r_sc[0:1, 513:514]
        prd = r_sc[0:1, 514:515]
        sqr = r_sc[0:1, 515:516]
        ngc = r_sc[0:1, 516:517]
        cur = r_sc[0:1, 517:518]
        E("vector", lambda e: e.memset(mk2, 0.0), writes=[(r_sc, "mk2")])
        E("vector", lambda e: e.memset(mq2, 0.0), writes=[(r_sc, "mq2")])
        for src_, nmax, mrow, mkey in ((Kg, TOK // 512, mk2, "mk2"), (Qg, S // 512, mq2, "mq2")):
            for ntile in range(nmax):
                E("scalar", lambda e, nt=ntile, src_=src_: e.activation(
                    sq[:, nt * 512: nt * 512 + 512], src_[:, nt * 512: nt * 512 + 512], AF.Square),
                    reads=[(src_, ntile)], writes=[(sq, ntile)])
            for ntile in range(nmax):
                pb = bk[6]
                E("tensor", lambda e, nt=ntile, pb=pb: e.matmul(
                    pb[0:1, :], cm.ones_colb[:, :], sq[:, nt * 512: nt * 512 + 512],
                    start=True, stop=True),
                    reads=[(sq, ntile), cm.ones_colb], writes=[pb])
                E("vector", lambda e, pb=pb: e.tensor_copy(scr, pb[0:1, :]),
                    reads=[pb], writes=[(r_sc, "scr")])
                E("vector", lambda e: e.reduce_max(cur, scr, axis=AX.X),
                    reads=[(r_sc, "scr")], writes=[(r_sc, "cur")])
                E("vector", lambda e, mrow=mrow, mkey=mkey: e.tensor_max(mrow, mrow, cur),
                    reads=[(r_sc, "cur"), (r_sc, mkey)], writes=[(r_sc, mkey)])
        E("vector", lambda e: e.tensor_mul(prd, mk2, mq2),
          reads=[(r_sc, "mk2"), (r_sc, "mq2")], writes=[(r_sc, "prd")])
        E("scalar", lambda e: e.activation(sqr, prd, AF.Sqrt),
          reads=[(r_sc, "prd")], writes=[(r_sc, "sqr")])
        E("vector", lambda e: e.tensor_scalar(ngc, sqr, -0.125, None, ALU.mult),
          reads=[(r_sc, "sqr")], writes=[(r_sc, "ngc")])
        pb = bk[6]
        E("tensor", lambda e, pb=pb: e.matmul(pb[:, 0:1], cm.ones_row[:, :], ngc,
                                              start=True, stop=True),
          reads=[(r_sc, "ngc"), cm.ones_row], writes=[pb])
        E("vector", lambda e, pb=pb: e.tensor_copy(negc[:, :], pb[:, 0:1]),
          reads=[pb], writes=[negc])

        for it in range(NSLOT):
            nb = [bk[4], bk[5]]
            for h in range(2):
                pbase = h * 64
                nbank = nb[h]
                for jc in range(32):
                    sb_ = bk[2 * h + (jc % 2)]
                    E("tensor", lambda e, jc=jc, it=it, pbase=pbase, sb_=sb_: e.matmul(
                        sb_[:, :], Qg[pbase:pbase + 64, jc * 128: jc * 128 + 128],
                        Kg[pbase:pbase + 64, it * 512: it * 512 + 512],
                        start=True, stop=True),
                        reads=[(Qg, jc // 4), (Kg, it)], writes=[sb_])
                    psl = slice((2 * h + (jc % 2)) * SL, (2 * h + (jc % 2)) * SL + SL)
                    E("scalar", lambda e, psl=psl, sb_=sb_: e.activation(
                        Pp[:, psl], sb_[:, :], AF.Exp, bias=negc[:, 0:1], scale=0.125),
                        reads=[sb_, negc], writes=[(Pp, psl.start)])
                    E("tensor", lambda e, jc=jc, h=h, psl=psl, nbank=nbank: e.matmul(
                        nbank[0:65, :], Vg[:, jc * 130 + h * 65: jc * 130 + h * 65 + 65],
                        Pp[:, psl], start=(jc == 0), stop=(jc == 31)),
                        reads=[(Vg, jc), (Vg, "ones"), (Pp, psl.start)], writes=[nbank])
            for h in range(2):
                nbank = nb[h]
                E("vector", lambda e, nbank=nbank: e.reciprocal(rl[0:1, :], nbank[64:65, :]),
                  reads=[nbank], writes=[rl])
                pb2 = bk[6]
                E("tensor", lambda e, pb2=pb2: e.matmul(pb2[0:64, :], cm.ones_row[0:1, 0:64], rl[0:1, :],
                                                        start=True, stop=True),
                  reads=[rl, cm.ones_row], writes=[pb2])
                E("vector", lambda e, pb2=pb2: e.tensor_copy(rlb[:, :], pb2[0:64, :]),
                  reads=[pb2], writes=[rlb])
                hh = 2 * g + h
                fc_, pb_ = hh // 2, (hh % 2) * 64
                E("vector", lambda e, it=it, fc_=fc_, pb_=pb_, nbank=nbank: e.tensor_mul(
                    y[pb_:pb_ + 64, fc_ * TOK + it * 512: fc_ * TOK + it * 512 + 512],
                    nbank[0:64, :], rlb[:, :]),
                    reads=[nbank, rlb], writes=[(y, (fc_, pb_, it))])

    for rc in range(2):
        for ntile in range(NSLOT):
            pb = bk[ntile % 2]
            for dc in range(DC):
                E("tensor", lambda e, rc=rc, nt=ntile, dc=dc, pb=pb: e.matmul(
                    pb[:, :], wov[:, dc * 256 + rc * 128: dc * 256 + rc * 128 + 128],
                    y[:, dc * TOK + nt * 512: dc * TOK + nt * 512 + 512],
                    start=(dc == 0), stop=(dc == DC - 1)),
                    reads=[wov, (y, (dc, 0, ntile)), (y, (dc, 64, ntile))], writes=[pb])
            E("scalar", lambda e, rc=rc, nt=ntile, pb=pb: e.activation(
                z1[:, rc * TOK + nt * 512: rc * TOK + nt * 512 + 512], pb[:, :], AF.Copy),
                reads=[pb], writes=[(z1, (rc, ntile))])

    for ntile in range(NSLOT):
        xsl = xseq.rearrange("(c p) x -> p c x", p=128)[:, :, ntile * 512:(ntile + 1) * 512]
        E("sync", lambda e, xsl=xsl: e.dma_start(
            out=xot[:, :].rearrange("p (c x) -> p c x", c=DC), in_=xsl),
            writes=[xot], dma=True)
        for fc in range(DC):
            pb = bk[2 + fc % 2]
            for rc in range(2):
                E("tensor", lambda e, fc=fc, nt=ntile, rc=rc, pb=pb: e.matmul(
                    pb[:, :], wou[:, rc * D + fc * 128: rc * D + fc * 128 + 128],
                    z1[:, rc * TOK + nt * 512: rc * TOK + nt * 512 + 512],
                    start=(rc == 0), stop=(rc == 1)),
                    reads=[wou, (z1, (rc, ntile))], writes=[pb])
            E("vector", lambda e, fc=fc, pb=pb: e.tensor_add(
                xres[:, fc * SL:(fc + 1) * SL], pb[:, :], xot[:, fc * SL:(fc + 1) * SL]),
                reads=[pb, xot], writes=[xres])
        cm.layernorm_nt(xres, bk[6], bk[7], out_f32=xres)
        dstv = x1.rearrange("(c p) x -> p c x", p=128)[:, :, ntile * 512:(ntile + 1) * 512]
        srcv = xres[:, :].rearrange("p (c x) -> p c x", c=DC)
        E("sync", lambda e, dstv=dstv, srcv=srcv: e.dma_start(out=dstv, in_=srcv),
          reads=[xres], dma=True, tag="out")

    if DEBUG_A:
        dt_ = _out(nc, "dbg_t", [2 * 128, S], BF16)
        dy = _out(nc, "dbg_y", [D, TOK], BF16)
        dz1 = _out(nc, "dbg_z1", [2 * 128, TOK], BF16)
        store_fm(em, dt_, t, 2, S, eng="sync", tag="out", reads=[(t, (rc_, nt_)) for rc_ in range(2) for nt_ in range(8)])
        store_fm(em, dy, y, DC, TOK, eng="sync", tag="out",
                 reads=[(y, (fc_, pb_, it_)) for fc_ in range(DC) for pb_ in (0, 64) for it_ in range(4)])
        store_fm(em, dz1, z1, 2, TOK, eng="sync", tag="out",
                 reads=[(z1, (rc_, nt_)) for rc_ in range(2) for nt_ in range(4)])
    outs = [op for op in em.ops if op.tag == "out"]
    em.emit("sync", lambda e: e.nop(), after=outs, tag="final")
    em.build("A")
    return nc


def build_B():
    nc = _mk_nc()
    x1 = _in(nc, "x1", [D, TOK])
    w1T = _in(nc, "w1T", [D, 4096])
    w2T = _in(nc, "w2T", [4096, D])
    b1c = _in(nc, "b1c", [128, 32])
    b2c = _in(nc, "b2c", [128, 8])
    x2 = _out(nc, "x2", [D, TOK])
    s1s = nc.dram_tensor("s1s", [32, 128, 1024], BF16, kind="Internal").ap()
    s2s = nc.dram_tensor("s2s", [4096, D], BF16, kind="Internal").ap()

    em = Emitter(nc)
    cm = Common(nc, em)
    xrb = nc.alloc_sbuf_tensor("ln_xrb", [128, DC * SL], BF16)
    x2b = nc.alloc_sbuf_tensor("ln_x2b", [128, DC * SL], BF16)
    lrows = nc.alloc_sbuf_tensor("ln_rows", [1, 3 * SL], F32)
    cm.attach_ln_scratch(xrb, x2b, lrows)

    xsb = nc.alloc_sbuf_tensor("xsb", [128, DC * TOK], BF16)
    h = nc.alloc_sbuf_tensor("h", [128, 32 * SL], BF16)
    xot = nc.alloc_sbuf_tensor("xot", [128, DC * SL], F32)
    xres = nc.alloc_sbuf_tensor("xres", [128, DC * SL], F32)
    wblk = nc.alloc_sbuf_tensor("wblk", [128, 1024], F32)
    sgn = nc.alloc_sbuf_tensor("sgn", [128, 1024], BF16)
    ab = nc.alloc_sbuf_tensor("ab", [128, 1024], BF16)
    s1buf = [nc.alloc_sbuf_tensor(f"s1b{i}", [128, 1024], BF16) for i in range(2)]
    s2buf = [nc.alloc_sbuf_tensor(f"s2b{i}", [128, 1024], BF16) for i in range(2)]
    a1c = nc.alloc_sbuf_tensor("a1c", [128, 32], F32)
    a2c = nc.alloc_sbuf_tensor("a2c", [128, 8], F32)
    b1s = nc.alloc_sbuf_tensor("b1s", [128, 32], F32)
    b2s = nc.alloc_sbuf_tensor("b2s", [128, 8], F32)
    ot = nc.alloc_sbuf_tensor("ot", [128, SL], F32)
    ones4k = nc.alloc_sbuf_tensor("ones4k", [128, 1], BF16)
    bk = [nc.alloc_psum_tensor(f"bk{i}", [128, 512], F32) for i in range(8)]
    E = em.emit

    E("vector", lambda e: e.memset(ones4k[:, :], 1.0 / 4096.0), writes=[ones4k])
    E("sync", lambda e: e.dma_start(out=b1s[:, :], in_=b1c), writes=[b1s], dma=True)
    E("sync", lambda e: e.dma_start(out=b2s[:, :], in_=b2c), writes=[b2s], dma=True)
    load_fm(em, xsb, x1, DC, TOK, eng="gpsimd")

    for hfc in range(32):
        wv_ = w1T.rearrange("(dc p) c -> p dc c", p=128)[:, :, hfc * 128:(hfc + 1) * 128]
        E("sync", lambda e, wv_=wv_: e.dma_start(
            out=wblk[:, :].rearrange("p (dc c) -> p dc c", dc=8), in_=wv_),
            writes=[wblk], dma=True)
        E("scalar", lambda e: e.activation(sgn[:, :], wblk[:, :], AF.Sign),
          reads=[wblk], writes=[sgn])
        E("scalar", lambda e: e.activation(ab[:, :], wblk[:, :], AF.Abs),
          reads=[wblk], writes=[ab])
        E("gpsimd", lambda e, hfc=hfc: e.dma_start(out=s1s[hfc], in_=sgn[:, :]),
          reads=[sgn], writes=[(s1s, hfc)], dma=True)
        pb = bk[0]
        for dc in range(DC):
            E("tensor", lambda e, dc=dc, pb=pb: e.matmul(
                pb[:, 0:1], ab[:, dc * 128:(dc + 1) * 128], cm.ones_mean[:, :],
                start=(dc == 0), stop=(dc == DC - 1)),
                reads=[ab, cm.ones_mean], writes=[pb])
        E("vector", lambda e, hfc=hfc, pb=pb: e.tensor_copy(a1c[:, hfc:hfc + 1], pb[:, 0:1]),
          reads=[pb], writes=[a1c])

    pa = bk[1]
    E("vector", lambda e: e.memset(a2c[:, :], 0.0), writes=[a2c])
    for hfc in range(32):
        wv_ = w2T[hfc * 128:(hfc + 1) * 128, :]
        E("sync", lambda e, wv_=wv_: e.dma_start(out=wblk[:, :], in_=wv_),
          writes=[wblk], dma=True)
        E("scalar", lambda e: e.activation(sgn[:, :], wblk[:, :], AF.Sign),
          reads=[wblk], writes=[sgn])
        E("scalar", lambda e: e.activation(ab[:, :], wblk[:, :], AF.Abs),
          reads=[wblk], writes=[ab])
        E("gpsimd", lambda e, hfc=hfc: e.dma_start(out=s2s[hfc * 128:(hfc + 1) * 128, :], in_=sgn[:, :]),
          reads=[sgn], writes=[(s2s, hfc)], dma=True)
        for fc in range(DC):
            E("tensor", lambda e, fc=fc, pa=pa: e.matmul(
                pa[:, fc:fc + 1], ab[:, fc * 128:(fc + 1) * 128], ones4k[:, :],
                start=True, stop=True),
                reads=[ab, ones4k], writes=[pa])
        E("vector", lambda e, pa=pa: e.tensor_add(a2c[:, :], a2c[:, :], pa[:, 0:8]),
          reads=[pa, a2c], writes=[a2c])

    for st in range(4):
        xsl = x1.rearrange("(c p) x -> p c x", p=128)[:, :, st * 512:(st + 1) * 512]
        E("sync", lambda e, xsl=xsl: e.dma_start(
            out=xot[:, :].rearrange("p (c x) -> p c x", c=DC), in_=xsl),
            writes=[xot], dma=True)
        for hfc in range(32):
            sb_ = s1buf[hfc % 4]
            E("sync", lambda e, hfc=hfc, sb_=sb_: e.dma_start(out=sb_[:, :], in_=s1s[hfc]),
              reads=[(s1s, hfc)], writes=[sb_], dma=True)
            pb = bk[hfc % 2]
            for dc in range(DC):
                E("tensor", lambda e, dc=dc, st=st, sb_=sb_, pb=pb: e.matmul(
                    pb[:, :], sb_[:, dc * 128:(dc + 1) * 128],
                    xsb[:, dc * TOK + st * 512: dc * TOK + st * 512 + 512],
                    start=(dc == 0), stop=(dc == DC - 1)),
                    reads=[sb_, xsb], writes=[pb])
            E("scalar", lambda e, hfc=hfc, pb=pb: e.activation(
                h[:, hfc * SL:(hfc + 1) * SL], pb[:, :], AF.Gelu,
                bias=b1s[:, hfc:hfc + 1], scale=a1c[:, hfc:hfc + 1]),
                reads=[pb, b1s, a1c], writes=[(h, hfc)])
        for hfc in range(32):
            sb_ = s2buf[hfc % 4]
            E("scalar", lambda e, hfc=hfc, sb_=sb_: e.dma_start(
                out=sb_[:, :], in_=s2s[hfc * 128:(hfc + 1) * 128, :]),
                reads=[(s2s, hfc)], writes=[sb_], dma=True)
            for fc in range(DC):
                E("tensor", lambda e, fc=fc, hfc=hfc, sb_=sb_: e.matmul(
                    bk[fc][:, :], sb_[:, fc * 128:(fc + 1) * 128],
                    h[:, hfc * SL:(hfc + 1) * SL],
                    start=(hfc == 0), stop=(hfc == 31)),
                    reads=[sb_, (h, hfc)], writes=[bk[fc]])
        for fc in range(DC):
            E("vector", lambda e, fc=fc: e.tensor_scalar(
                ot[:, :], bk[fc][:, :], a2c[:, fc:fc + 1], b2s[:, fc:fc + 1],
                ALU.mult, ALU.add),
                reads=[bk[fc], b2s, a2c], writes=[ot])
            E("vector", lambda e, fc=fc: e.tensor_add(
                xres[:, fc * SL:(fc + 1) * SL], ot[:, :], xot[:, fc * SL:(fc + 1) * SL]),
                reads=[ot, xot], writes=[xres])
        cm.layernorm_nt(xres, bk[0], bk[1], out_f32=xres)
        dstv = x2.rearrange("(c p) x -> p c x", p=128)[:, :, st * 512:(st + 1) * 512]
        srcv = xres[:, :].rearrange("p (c x) -> p c x", c=DC)
        E("sync", lambda e, dstv=dstv, srcv=srcv: e.dma_start(out=dstv, in_=srcv),
          reads=[xres], dma=True, tag="out")

    outs = [op for op in em.ops if op.tag == "out"]
    em.emit("sync", lambda e: e.nop(), after=outs, tag="final")
    em.build("B")
    return nc


GPOS = POS[0] + POS[1]
INC_TILES = [list(range(0, it + 1)) + list(range(4, 4 + it + 1)) for it in range(4)]
MSK_TILES = [[it, 4 + it] for it in range(4)]


def build_C1():
    nc = _mk_nc()
    x2 = _in(nc, "x2", [D, TOK])
    wqT = _in(nc, "wqT", [D, D])
    wkT = _in(nc, "wkT", [D, D])
    wvT = _in(nc, "wvT", [D, D])
    Qo = _out(nc, "Qo", [D, TOK])
    Ko = _out(nc, "Ko", [D, TOK])
    Vo = _out(nc, "Vo", [TOK, D])
    q2o = _out(nc, "q2o", [1, TOK])

    em = Emitter(nc)
    cm = Common(nc, em)
    xsb = nc.alloc_sbuf_tensor("xsb", [128, DC * TOK], BF16)
    wT = nc.alloc_sbuf_tensor("wT", [128, DC * D], BF16)
    qt = nc.alloc_sbuf_tensor("qt", [128, DC * SL], F32)
    sqv = nc.alloc_sbuf_tensor("sqv", [128, SL], BF16)
    q2r = nc.alloc_sbuf_tensor("q2r", [1, SL], F32)
    bk = [nc.alloc_psum_tensor(f"bk{i}", [128, 512], F32) for i in range(4)]
    E = em.emit

    load_fm(em, xsb, x2, DC, TOK, eng="gpsimd")

    for wsrc, dst, with_q2 in ((wqT, Qo, True), (wkT, Ko, False)):
        load_fm(em, wT, wsrc, DC, D, eng="gpsimd", tag="ldw")
        for ntile in range(NSLOT):
            for fc in range(DC):
                pb = bk[fc % 2]
                for dc in range(DC):
                    E("tensor", lambda e, fc=fc, nt=ntile, dc=dc, pb=pb: e.matmul(
                        pb[:, :], wT[:, dc * D + fc * 128: dc * D + fc * 128 + 128],
                        xsb[:, dc * TOK + nt * 512: dc * TOK + nt * 512 + 512],
                        start=(dc == 0), stop=(dc == DC - 1)),
                        reads=[wT, xsb], writes=[pb])
                E("scalar", lambda e, fc=fc, pb=pb: e.activation(
                    qt[:, fc * SL:(fc + 1) * SL], pb[:, :], AF.Copy),
                    reads=[pb], writes=[(qt, fc)])
            dstv = dst.rearrange("(c p) x -> p c x", p=128)[:, :, ntile * 512:(ntile + 1) * 512]
            E("sync", lambda e, dstv=dstv: e.dma_start(
                out=dstv, in_=qt[:, :].rearrange("p (c x) -> p c x", c=DC)),
                reads=[(qt, fc_) for fc_ in range(DC)], dma=True, tag="out")
            if with_q2:
                pq = bk[2]
                for fc in range(DC):
                    E("scalar", lambda e, fc=fc: e.activation(
                        sqv[:, :], qt[:, fc * SL:(fc + 1) * SL], AF.Square),
                        reads=[(qt, fc)], writes=[sqv])
                    E("tensor", lambda e, fc=fc, pq=pq: e.matmul(
                        pq[0:1, :], cm.ones_colb[:, :], sqv[:, :],
                        start=(fc == 0), stop=(fc == DC - 1)),
                        reads=[sqv, cm.ones_colb], writes=[pq])
                E("vector", lambda e, pq=pq: e.tensor_copy(q2r[0:1, :], pq[0:1, :]),
                  reads=[pq], writes=[q2r])
                E("sync", lambda e, nt=ntile: e.dma_start(
                    out=q2o[0:1, nt * 512:(nt + 1) * 512], in_=q2r[0:1, :]),
                    reads=[q2r], dma=True, tag="out")

    load_fm(em, wT, wvT, DC, D, eng="gpsimd", tag="ldw")
    for tc in range(TOK // 128):
        for half in range(2):
            pb = bk[half]
            for dc in range(DC):
                E("tensor", lambda e, tc=tc, half=half, dc=dc, pb=pb: e.matmul(
                    pb[:, :], xsb[:, dc * TOK + tc * 128: dc * TOK + tc * 128 + 128],
                    wT[:, dc * D + half * 512: dc * D + half * 512 + 512],
                    start=(dc == 0), stop=(dc == DC - 1)),
                    reads=[wT, xsb], writes=[pb])
            E("scalar", lambda e, half=half, pb=pb: e.activation(
                qt[:, half * 512: half * 512 + 512], pb[:, :], AF.Copy),
                reads=[pb], writes=[(qt, half)])
        E("sync", lambda e, tc=tc: e.dma_start(
            out=Vo[tc * 128:(tc + 1) * 128, :], in_=qt[:, 0:D]),
            reads=[(qt, 0), (qt, 1)], dma=True, tag="out")

    outs = [op for op in em.ops if op.tag == "out"]
    em.emit("sync", lambda e: e.nop(), after=outs, tag="final")
    em.build("C1")
    return nc


def build_C2():
    nc = _mk_nc()
    x2 = _in(nc, "x2", [D, TOK])
    KoI = _in(nc, "Ko", [D, TOK])
    QfI = _in(nc, "Qf", [D, S])
    VfI = _in(nc, "Vf", [S, D])
    q2f = _in(nc, "q2f", [1, S])
    cmask = _in(nc, "cmask", [4, 8, 128, 512], BF16)
    x3 = _out(nc, "x3", [D, TOK])

    em = Emitter(nc)
    cm = Common(nc, em)
    xrb = nc.alloc_sbuf_tensor("ln_xrb", [128, DC * SL], BF16)
    x2b_ = nc.alloc_sbuf_tensor("ln_x2b", [128, DC * SL], BF16)
    lrows = nc.alloc_sbuf_tensor("ln_rows", [1, 3 * SL], F32)
    cm.attach_ln_scratch(xrb, x2b_, lrows)

    Qf = nc.alloc_sbuf_tensor("Qf_sb", [128, DC * S], BF16)
    Ko = nc.alloc_sbuf_tensor("Ko_sb", [128, DC * TOK], BF16)
    Pp = nc.alloc_sbuf_tensor("Pp", [128, 32 * SL], BF16)
    vbuf = [nc.alloc_sbuf_tensor(f"vb{i}", [128, D], BF16) for i in range(2)]
    mbuf = [nc.alloc_sbuf_tensor(f"mb{i}", [128, SL], BF16) for i in range(2)]
    xot = nc.alloc_sbuf_tensor("xot", [128, DC * SL], F32)
    xres = nc.alloc_sbuf_tensor("xres", [128, DC * SL], F32)
    sqv = nc.alloc_sbuf_tensor("sqv", [128, SL], BF16)
    scr = nc.alloc_sbuf_tensor("scr", [1, SL], F32)
    sca = nc.alloc_sbuf_tensor("sca", [1, 16], F32)
    negc = nc.alloc_sbuf_tensor("negc", [128, 1], F32)
    rl = nc.alloc_sbuf_tensor("rl", [1, SL], F32)
    rlb = nc.alloc_sbuf_tensor("rlb", [128, SL], F32)
    bk = [nc.alloc_psum_tensor(f"bk{i}", [128, 512], F32) for i in range(8)]
    E = em.emit

    load_fm(em, Qf, QfI, DC, S, eng="gpsimd")
    load_fm(em, Ko, KoI, DC, TOK, eng="gpsimd")

    mk2, mq2 = sca[0:1, 0:1], sca[0:1, 1:2]
    prd, sqr = sca[0:1, 2:3], sca[0:1, 3:4]
    ngc, cur = sca[0:1, 4:5], sca[0:1, 5:6]
    E("vector", lambda e: e.memset(mk2, 0.0), writes=[(sca, "mk2")])
    E("vector", lambda e: e.memset(mq2, 0.0), writes=[(sca, "mq2")])
    for ntile in range(NSLOT):
        pq = bk[0]
        for fc in range(DC):
            E("scalar", lambda e, fc=fc, nt=ntile: e.activation(
                sqv[:, :], Ko[:, fc * TOK + nt * 512: fc * TOK + nt * 512 + 512], AF.Square),
                reads=[Ko], writes=[sqv])
            E("tensor", lambda e, fc=fc, pq=pq: e.matmul(
                pq[0:1, :], cm.ones_colb[:, :], sqv[:, :],
                start=(fc == 0), stop=(fc == DC - 1)),
                reads=[sqv, cm.ones_colb], writes=[pq])
        E("vector", lambda e, pq=pq: e.tensor_copy(scr[0:1, :], pq[0:1, :]),
          reads=[pq], writes=[scr])
        E("vector", lambda e: e.reduce_max(cur, scr[0:1, :], axis=AX.X),
          reads=[scr], writes=[(sca, "cur")])
        E("vector", lambda e: e.tensor_max(mk2, mk2, cur),
          reads=[(sca, "cur"), (sca, "mk2")], writes=[(sca, "mk2")])
    for ntile in range(S // 512):
        E("sync", lambda e, nt=ntile: e.dma_start(
            out=scr[0:1, :], in_=q2f[0:1, nt * 512:(nt + 1) * 512]),
            writes=[scr], dma=True)
        E("vector", lambda e: e.reduce_max(cur, scr[0:1, :], axis=AX.X),
          reads=[scr], writes=[(sca, "cur")])
        E("vector", lambda e: e.tensor_max(mq2, mq2, cur),
          reads=[(sca, "cur"), (sca, "mq2")], writes=[(sca, "mq2")])
    E("vector", lambda e: e.tensor_mul(prd, mk2, mq2),
      reads=[(sca, "mk2"), (sca, "mq2")], writes=[(sca, "prd")])
    E("scalar", lambda e: e.activation(sqr, prd, AF.Sqrt),
      reads=[(sca, "prd")], writes=[(sca, "sqr")])
    E("vector", lambda e: e.tensor_scalar(ngc, sqr, -1.0 / 32.0, None, ALU.mult),
      reads=[(sca, "sqr")], writes=[(sca, "ngc")])
    E("tensor", lambda e: e.matmul(bk[3][:, 0:1], cm.ones_row[:, :], ngc, start=True, stop=True),
      reads=[(sca, "ngc"), cm.ones_row], writes=[bk[3]])
    E("vector", lambda e: e.tensor_copy(negc[:, :], bk[3][:, 0:1]),
      reads=[bk[3]], writes=[negc])

    for it in range(NSLOT):
        for jc in range(32):
            sb_ = bk[jc % 2]
            for dc in range(DC):
                E("tensor", lambda e, jc=jc, it=it, dc=dc, sb_=sb_: e.matmul(
                    sb_[:, :], Qf[:, dc * S + jc * 128: dc * S + jc * 128 + 128],
                    Ko[:, dc * TOK + it * 512: dc * TOK + it * 512 + 512],
                    start=(dc == 0), stop=(dc == DC - 1)),
                    reads=[Qf, Ko], writes=[sb_])
            E("scalar", lambda e, jc=jc, sb_=sb_: e.activation(
                Pp[:, jc * SL:(jc + 1) * SL], sb_[:, :], AF.Exp,
                bias=negc[:, 0:1], scale=1.0 / 32.0),
                reads=[sb_, negc], writes=[(Pp, jc)])
            E("tensor", lambda e, jc=jc: e.matmul(
                bk[2][0:1, :], cm.ones_colb[:, :], Pp[:, jc * SL:(jc + 1) * SL],
                start=(jc == 0), stop=(jc == 31)),
                reads=[(Pp, jc), cm.ones_colb], writes=[bk[2]])
        for mi, g in enumerate(MSK_TILES[it]):
            for cl in range(4):
                jc = g * 4 + cl
                mb_ = mbuf[(mi * 4 + cl) % 2]
                E("sync", lambda e, it=it, mi=mi, cl=cl, mb_=mb_: e.dma_start(
                    out=mb_[:, :], in_=cmask[it, mi * 4 + cl]),
                    writes=[mb_], dma=True)
                E("vector", lambda e, jc=jc, mb_=mb_: e.tensor_mul(
                    Pp[:, jc * SL:(jc + 1) * SL], Pp[:, jc * SL:(jc + 1) * SL], mb_[:, :]),
                    reads=[(Pp, jc), mb_], writes=[(Pp, jc)])
        E("vector", lambda e: e.tensor_copy(scr[0:1, :], bk[2][0:1, :]),
          reads=[bk[2]], writes=[scr])
        E("vector", lambda e: e.reciprocal(rl[0:1, :], scr[0:1, :]),
          reads=[scr], writes=[rl])
        E("tensor", lambda e: e.matmul(bk[3][:, :], cm.ones_row[:, :], rl[0:1, :],
                                       start=True, stop=True),
          reads=[rl, cm.ones_row], writes=[bk[3]])
        E("vector", lambda e: e.tensor_copy(rlb[:, :], bk[3][:, :]),
          reads=[bk[3]], writes=[rlb])
        xsl = x2d.rearrange("(c p) x -> p c x", p=128)[:, :, it * 512:(it + 1) * 512]
        E("sync", lambda e, xsl=xsl: e.dma_start(
            out=xot[:, :].rearrange("p (c x) -> p c x", c=DC), in_=xsl),
            writes=[xot], dma=True)
        chunks = [g * 4 + cl for g in INC_TILES[it] for cl in range(4)]
        for half in range(2):
            for ji, jc in enumerate(chunks):
                vb_ = vbuf[ji % 2]
                E("gpsimd", lambda e, jc=jc, vb_=vb_: e.dma_start(
                    out=vb_[:, :], in_=VfI[jc * 128:(jc + 1) * 128, :]),
                    writes=[vb_], dma=True)
                for dvl in range(4):
                    dv = half * 4 + dvl
                    E("tensor", lambda e, jc=jc, dv=dv, dvl=dvl, vb_=vb_, ji=ji, n=len(chunks): e.matmul(
                        bk[4 + dvl][:, :], vb_[:, dv * 128:(dv + 1) * 128],
                        Pp[:, jc * SL:(jc + 1) * SL],
                        start=(ji == 0), stop=(ji == n - 1)),
                        reads=[vb_, (Pp, jc)], writes=[bk[4 + dvl]])
            for dvl in range(4):
                dv = half * 4 + dvl
                E("vector", lambda e, dvl=dvl: e.tensor_mul(
                    cm.tmp[:, :], bk[4 + dvl][:, :], rlb[:, :]),
                    reads=[bk[4 + dvl], rlb], writes=[cm.tmp])
                E("vector", lambda e, dv=dv: e.tensor_add(
                    xres[:, dv * SL:(dv + 1) * SL], cm.tmp[:, :], xot[:, dv * SL:(dv + 1) * SL]),
                    reads=[cm.tmp, xot], writes=[xres])
        cm.layernorm_nt(xres, bk[0], bk[1], out_f32=xres)
        dstv = x3.rearrange("(c p) x -> p c x", p=128)[:, :, it * 512:(it + 1) * 512]
        srcv = xres[:, :].rearrange("p (c x) -> p c x", c=DC)
        E("sync", lambda e, dstv=dstv, srcv=srcv: e.dma_start(out=dstv, in_=srcv),
          reads=[xres], dma=True, tag="out")

    outs = [op for op in em.ops if op.tag == "out"]
    em.emit("sync", lambda e: e.nop(), after=outs, tag="final")
    em.build("C2")
    return nc


def build_D():
    nc = _mk_nc()
    x3 = _in(nc, "x3", [D, TOK])
    WgT = _in(nc, "WgT", [D, 8])
    lnsI = _in(nc, "lns", [D, 8])
    lnbI = _in(nc, "lnb", [D, 8])
    W1f = _in(nc, "W1f", [D, 8 * 256])
    b1f = _in(nc, "b1f", [256, 8])
    W2f = _in(nc, "W2f", [256, 8 * D])
    b2I = _in(nc, "b2", [8, D])
    idI = _in(nc, "id128", [128, 128])
    indI = _in(nc, "ind", [8, 1024], BF16)
    xout = _out(nc, "xout", [D, TOK])

    em = Emitter(nc)
    cm = Common(nc, em)
    xrb = nc.alloc_sbuf_tensor("ln_xrb", [128, DC * SL], BF16)
    x2b_ = nc.alloc_sbuf_tensor("ln_x2b", [128, DC * SL], BF16)
    lrows = nc.alloc_sbuf_tensor("ln_rows", [1, 3 * SL], F32)
    cm.attach_ln_scratch(xrb, x2b_, lrows)

    w1e = nc.alloc_sbuf_tensor("w1e", [128, DC * 8 * 256], BF16)
    w2e = nc.alloc_sbuf_tensor("w2e", [128, 2 * 8 * D], BF16)
    wg = nc.alloc_sbuf_tensor("wg", [128, DC * 8], F32)
    lns = nc.alloc_sbuf_tensor("lns_sb", [128, DC * 8], F32)
    lnb = nc.alloc_sbuf_tensor("lnb_sb", [128, DC * 8], F32)
    b1s = nc.alloc_sbuf_tensor("b1s", [128, 2 * 8], F32)
    b2b = nc.alloc_sbuf_tensor("b2b", [8, D], BF16)
    idt = nc.alloc_sbuf_tensor("idt", [128, 128], F32)
    indb = nc.alloc_sbuf_tensor("indb", [8, 1024], BF16)
    xot = nc.alloc_sbuf_tensor("xot", [128, DC * SL], F32)
    xtb = nc.alloc_sbuf_tensor("xtb", [128, DC * SL], BF16)
    xhat = nc.alloc_sbuf_tensor("xhat", [128, DC * SL], BF16)
    xeb = nc.alloc_sbuf_tensor("xeb", [128, DC * SL], BF16)
    hall = nc.alloc_sbuf_tensor("hall", [128, 16 * SL], BF16)
    xres = nc.alloc_sbuf_tensor("xres", [128, DC * SL], F32)
    rmx = nc.alloc_sbuf_tensor("rmx", [128, 1], F32)
    mtm = nc.alloc_sbuf_tensor("mtm", [128, 8], F32)
    maskb = nc.alloc_sbuf_tensor("maskb", [8, SL], BF16)
    bk = [nc.alloc_psum_tensor(f"bk{i}", [128, 512], F32) for i in range(8)]
    E = em.emit

    load_fm(em, wg, WgT, DC, 8, eng="sync")
    load_fm(em, lns, lnsI, DC, 8, eng="sync")
    load_fm(em, lnb, lnbI, DC, 8, eng="sync")
    load_fm(em, w1e, W1f, DC, 8 * 256, eng="gpsimd")
    load_fm(em, b1s, b1f, 2, 8, eng="sync")
    load_fm(em, w2e, W2f, 2, 8 * D, eng="gpsimd")
    E("gpsimd", lambda e: e.dma_start(out=b2b[:, :], in_=b2I), writes=[b2b], dma=True)
    E("sync", lambda e: e.dma_start(out=idt[:, :], in_=idI), writes=[idt], dma=True)
    E("sync", lambda e: e.dma_start(out=indb[:, :], in_=indI), writes=[indb], dma=True)

    for tt in range(NSLOT):
        xsl = x3.rearrange("(c p) x -> p c x", p=128)[:, :, tt * 512:(tt + 1) * 512]
        E("sync", lambda e, xsl=xsl: e.dma_start(
            out=xot[:, :].rearrange("p (c x) -> p c x", c=DC), in_=xsl),
            writes=[xot], dma=True)
        E("gpsimd", lambda e, xsl=xsl: e.dma_start(
            out=xtb[:, :].rearrange("p (c x) -> p c x", c=DC), in_=xsl),
            writes=[xtb], dma=True)
        for m4 in range(4):
            pg = bk[0]
            for dc in range(DC):
                E("tensor", lambda e, m4=m4, dc=dc, pg=pg: e.matmul(
                    pg[:, 0:8], xot[:, dc * SL + m4 * 128: dc * SL + m4 * 128 + 128],
                    wg[:, dc * 8:(dc + 1) * 8],
                    start=(dc == 0), stop=(dc == DC - 1)),
                    reads=[xot, wg], writes=[pg])
            E("vector", lambda e, pg=pg: e.reduce_max(rmx[:, :], pg[:, 0:8], axis=AX.X),
              reads=[pg], writes=[rmx])
            E("vector", lambda e, pg=pg: e.tensor_scalar(
                mtm[:, :], pg[:, 0:8], rmx[:, 0:1], None, ALU.is_ge),
                reads=[pg, rmx], writes=[mtm])
            E("tensor", lambda e: e.transpose(bk[3][0:8, 0:128], mtm[:, :], idt[:, :]),
              reads=[mtm, idt], writes=[bk[3]])
            E("vector", lambda e, m4=m4: e.tensor_copy(
                maskb[0:8, m4 * 128:(m4 + 1) * 128], bk[3][0:8, 0:128]),
                reads=[bk[3]], writes=[maskb])
        cm.layernorm_nt(xot, bk[6], bk[7], out_f32=None, out_bf16=xhat)
        for ex in range(8):
            for dc in range(DC):
                E("vector", lambda e, ex=ex, dc=dc: e.tensor_scalar(
                    xeb[:, dc * SL:(dc + 1) * SL], xhat[:, dc * SL:(dc + 1) * SL],
                    lns[:, dc * 8 + ex: dc * 8 + ex + 1], lnb[:, dc * 8 + ex: dc * 8 + ex + 1],
                    ALU.mult, ALU.add),
                    reads=[xhat, lns, lnb], writes=[xeb])
            E("tensor", lambda e, ex=ex: e.matmul(
                bk[3][:, :], indb[0:8, ex * 128:(ex + 1) * 128], maskb[0:8, :],
                start=True, stop=True),
                reads=[indb, maskb], writes=[bk[3]])
            for hfc in range(2):
                ph = bk[1 + hfc]
                for dc in range(DC):
                    E("tensor", lambda e, ex=ex, hfc=hfc, dc=dc, ph=ph: e.matmul(
                        ph[:, :], w1e[:, dc * 2048 + ex * 256 + hfc * 128: dc * 2048 + ex * 256 + hfc * 128 + 128],
                        xeb[:, dc * SL:(dc + 1) * SL],
                        start=(dc == 0), stop=(dc == DC - 1)),
                        reads=[w1e, xeb], writes=[ph])
                E("scalar", lambda e, ex=ex, hfc=hfc, ph=ph: e.activation(
                    hall[:, (ex * 2 + hfc) * SL:(ex * 2 + hfc + 1) * SL], ph[:, :], AF.Gelu,
                    bias=b1s[:, hfc * 8 + ex: hfc * 8 + ex + 1]),
                    reads=[ph, b1s], writes=[(hall, (ex, hfc))])
                E("vector", lambda e, ex=ex, hfc=hfc: e.tensor_mul(
                    hall[:, (ex * 2 + hfc) * SL:(ex * 2 + hfc + 1) * SL],
                    hall[:, (ex * 2 + hfc) * SL:(ex * 2 + hfc + 1) * SL], bk[3][:, :]),
                    reads=[(hall, (ex, hfc)), bk[3]], writes=[(hall, (ex, hfc))])
        for fc in range(DC):
            py = bk[4 + fc % 2]
            for ex in range(8):
                for hfc in range(2):
                    E("tensor", lambda e, fc=fc, ex=ex, hfc=hfc, py=py: e.matmul(
                        py[:, :], w2e[:, hfc * 8 * D + ex * D + fc * 128: hfc * 8 * D + ex * D + fc * 128 + 128],
                        hall[:, (ex * 2 + hfc) * SL:(ex * 2 + hfc + 1) * SL],
                        start=(ex == 0 and hfc == 0), stop=False),
                        reads=[w2e, (hall, (ex, hfc))], writes=[py])
            E("tensor", lambda e, fc=fc, py=py: e.matmul(
                py[:, :], b2b[0:8, fc * 128:(fc + 1) * 128], maskb[0:8, :],
                start=False, stop=True),
                reads=[b2b, maskb], writes=[py])
            E("vector", lambda e, fc=fc, py=py: e.tensor_add(
                xres[:, fc * SL:(fc + 1) * SL], py[:, :], xot[:, fc * SL:(fc + 1) * SL]),
                reads=[py, xot], writes=[xres])
        cm.layernorm_nt(xres, bk[6], bk[7], out_f32=xres)
        dstv = xout.rearrange("(c p) x -> p c x", p=128)[:, :, tt * 512:(tt + 1) * 512]
        srcv = xres[:, :].rearrange("p (c x) -> p c x", c=DC)
        E("sync", lambda e, dstv=dstv, srcv=srcv: e.dma_start(out=dstv, in_=srcv),
          reads=[xres], dma=True, tag="out")

    outs = [op for op in em.ops if op.tag == "out"]
    em.emit("sync", lambda e: e.nop(), after=outs, tag="final")
    em.build("D")
    return nc


PGSZ = 1024


class Arena:
    def __init__(self, nc, em, size):
        self.nc, self.em = nc, em
        self.base = nc.alloc_sbuf_tensor("arena", [128, size], mybir.dt.uint8)
        self.addr = nc.lookup_mloc(self.base).addr

    def at(self, name, off, shape, dtype):
        t = self.nc.alloc_sbuf_tensor_at(name, shape, dtype, offset=self.addr + off)
        nb = int(np.prod(shape[1:])) * (2 if dtype == BF16 else 4 if dtype in (F32,) else 1)
        self.em.set_pages(t, off, nb)
        return t


class PView:

    def __init__(self, t, c0, ncols, name):
        self.t, self.c0, self.ncols, self.name = t, c0, ncols, name

    def __getitem__(self, idx):
        p, c = idx
        a = 0 if c.start is None else c.start
        b = self.ncols if c.stop is None else c.stop
        return self.t[p, self.c0 + a:self.c0 + b]


PSUM_PGBASE = 1 << 20


def build_F():
    nc = _mk_nc()
    xseq = _in(nc, "xseq", [D, S])
    WvT = _in(nc, "WvT", [D, 256])
    WuT = _in(nc, "WuT", [256, 3 * D])
    WovT = _in(nc, "WovT", [D, 256])
    WouT = _in(nc, "WouT", [256, D])
    w1T = _in(nc, "w1T", [D, 4096])
    w2T = _in(nc, "w2T", [4096, D])
    b1c = _in(nc, "b1c", [128, 32])
    b2c = _in(nc, "b2c", [128, 8])
    wqT = _in(nc, "wqT", [D, D])
    wkT = _in(nc, "wkT", [D, D])
    wvT = _in(nc, "wvT", [D, D])
    cmask = _in(nc, "cmask", [4, 8, 128, 512], BF16)
    WgT = _in(nc, "WgT", [D, 8])
    lnsI = _in(nc, "lns", [D, 8])
    lnbI = _in(nc, "lnb", [D, 8])
    W1f = _in(nc, "W1f", [D, 8 * 256])
    b1f = _in(nc, "b1f", [256, 8])
    W2f = _in(nc, "W2f", [256, 8 * D])
    b2I = _in(nc, "b2", [8, D])
    idI = _in(nc, "id128", [128, 128])
    indI = _in(nc, "ind", [8, 1024], BF16)
    xout = _out(nc, "xout", [D, TOK])
    x1d = nc.dram_tensor("x1d", [D, TOK], F32, kind="Internal").ap()
    x2d = nc.dram_tensor("x2d", [D, TOK], F32, kind="Internal").ap()
    x3d = nc.dram_tensor("x3d", [D, TOK], F32, kind="Internal").ap()
    s1s = nc.dram_tensor("s1s", [32, 128, 1024], BF16, kind="Internal").ap()
    s2s = nc.dram_tensor("s2s", [4096, D], BF16, kind="Internal").ap()
    eQ = nc.dram_tensor("eQ", [TOK, D], BF16, kind="Internal")
    eV = nc.dram_tensor("eV", [TOK, D], BF16, kind="Internal")
    eq2 = nc.dram_tensor("eq2", [1, TOK], F32, kind="Internal")
    gQ = nc.dram_tensor("gQ", [2 * TOK, D], BF16, kind="Internal")
    gV = nc.dram_tensor("gV", [2 * TOK, D], BF16, kind="Internal")
    gq2 = nc.dram_tensor("gq2", [2, TOK], F32, kind="Internal")
    kd = nc.dram_tensor("kd", [D, TOK], BF16, kind="Internal").ap()

    em = Emitter(nc)
    cm = Common(nc, em)
    ar = Arena(nc, em, 196 * 1024)
    S4 = nc.alloc_psum_tensor("S4", [128, 4 * 512], F32)
    em.set_pages(S4, PSUM_PGBASE, 4 * 2048)
    bk = [PView(S4, i * 512, 512, f"S4v{i}") for i in range(4)] + \
         [nc.alloc_psum_tensor(f"bk{i}", [128, 512], F32) for i in range(4, 8)]
    for i in range(4):
        em.set_pages(bk[i], PSUM_PGBASE + i * 2048, 2048)
    E = em.emit


    xb = ar.at("xb", 0, [128, DC * S], BF16)
    xresA = ar.at("xresA", 0, [128, DC * SL], F32)
    xotA = ar.at("xotA", 16384, [128, DC * SL], F32)
    xrbA = ar.at("xrbA", 32768, [128, DC * SL], BF16)
    x2bA = ar.at("x2bA", 40960, [128, DC * SL], BF16)
    lrowsA = ar.at("lrowsA", 49152, [1, 3 * SL], F32)
    wv = ar.at("wv", 65536, [128, DC * 256], BF16)
    wu = ar.at("wu", 69632, [128, 2 * 3 * D], BF16)
    wov = ar.at("wov", 81920, [128, DC * 256], BF16)
    wou = ar.at("wou", 86016, [128, 2 * D], BF16)
    t = ar.at("t", 90112, [128, 2 * S], BF16)
    Qg = ar.at("Qg", 106496, [128, S], BF16)
    Kg = ar.at("Kg", 114688, [128, TOK], BF16)
    Vg = ar.at("Vg", 118784, [128, 32 * 130], BF16)
    y = ar.at("y", 128000, [128, DC * TOK], BF16)
    z1 = ar.at("z1", 160768, [128, 2 * TOK], BF16)
    Pp = ar.at("Pp", 168960, [128, 8 * SL], BF16)
    sq = ar.at("sq", 173056, [128, S], BF16)
    r_sc = ar.at("r_sc", 181248, [1, 528], F32)
    negc = ar.at("negc", 183424, [128, 1], F32)
    rl = ar.at("rl", 177152, [1, 2 * SL], F32)
    rlb = ar.at("rlb", 185536, [128, SL], F32)
    cm.attach_ln_scratch(xrbA, x2bA, lrowsA)

    load_fm(em, xb, xseq, DC, S, eng="gpsimd")
    load_fm(em, wv, WvT, DC, 256, eng="gpsimd")
    load_fm(em, wu, WuT, 2, 3 * D, eng="gpsimd")
    load_fm(em, wov, WovT, DC, 256, eng="gpsimd")
    load_fm(em, wou, WouT, 2, D, eng="gpsimd")

    wblk = ar.at("wblk", 188416, [128, 1024], F32)
    sgn = ar.at("sgn", 192512, [128, 1024], BF16)
    ab = ar.at("ab", 194560, [128, 1024], BF16)
    ab2 = ar.at("ab2", 197120, [128, 1024], BF16)
    a1c = ar.at("a1c", 196608, [128, 32], F32)
    a2c = ar.at("a2c", 196736, [128, 8], F32)
    ones4k = ar.at("ones4k", 196800, [128, 1], BF16)
    b1s = ar.at("b1s", 196864, [128, 32], F32)
    b2s = ar.at("b2s", 197024, [128, 8], F32)
    E("vector", lambda e: e.memset(ones4k[:, :], 1.0 / 4096.0), writes=[ones4k])
    E("sync", lambda e: e.dma_start(out=b1s[:, :], in_=b1c), writes=[b1s], dma=True)
    E("sync", lambda e: e.dma_start(out=b2s[:, :], in_=b2c), writes=[b2s], dma=True)
    E("vector", lambda e: e.memset(a2c[:, :], 0.0), writes=[a2c])
    def p1_w1_step(hfc):
            wv_ = w1T.rearrange("(dc p) c -> p dc c", p=128)[:, :, hfc * 128:(hfc + 1) * 128]
            E("sync", lambda e, wv_=wv_: e.dma_start(
                out=wblk[:, :].rearrange("p (dc c) -> p dc c", dc=8), in_=wv_),
                writes=[wblk], dma=True)
            E("scalar", lambda e: e.activation(sgn[:, :], wblk[:, :], AF.Sign),
              reads=[wblk], writes=[sgn])
            E("scalar", lambda e: e.activation(ab[:, :], wblk[:, :], AF.Abs),
              reads=[wblk], writes=[ab])
            E("gpsimd", lambda e, hfc=hfc: e.dma_start(out=s1s[hfc], in_=sgn[:, :]),
              reads=[sgn], writes=[(s1s, hfc)], dma=True)
            def reduce():
                pb = bk[7]
                for dc in range(DC):
                    E("tensor", lambda e, dc=dc, pb=pb: e.matmul(
                        pb[:, 0:1], ab[:, dc * 128:(dc + 1) * 128], cm.ones_mean[:, :],
                        start=(dc == 0), stop=(dc == DC - 1)),
                        reads=[ab, cm.ones_mean], writes=[pb])
                E("vector", lambda e, hfc=hfc, pb=pb: e.tensor_copy(a1c[:, hfc:hfc + 1], pb[:, 0:1]),
                  reads=[pb], writes=[a1c])
            return reduce


    def p1_w2_step(hfc):
            wv_ = w2T[hfc * 128:(hfc + 1) * 128, :]
            E("sync", lambda e, wv_=wv_: e.dma_start(out=wblk[:, :], in_=wv_),
              writes=[wblk], dma=True)
            E("scalar", lambda e: e.activation(sgn[:, :], wblk[:, :], AF.Sign),
              reads=[wblk], writes=[sgn])
            E("scalar", lambda e: e.activation(ab2[:, :], wblk[:, :], AF.Abs),
              reads=[wblk], writes=[ab2])
            E("gpsimd", lambda e, hfc=hfc: e.dma_start(out=s2s[hfc * 128:(hfc + 1) * 128, :], in_=sgn[:, :]),
              reads=[sgn], writes=[(s2s, hfc)], dma=True)
            def reduce():
                pa = bk[7]
                for fc in range(DC):
                    E("tensor", lambda e, fc=fc, pa=pa: e.matmul(
                        pa[:, fc:fc + 1], ab2[:, fc * 128:(fc + 1) * 128], ones4k[:, :],
                        start=True, stop=True),
                        reads=[ab2, ones4k], writes=[pa])
                E("vector", lambda e, pa=pa: e.tensor_add(a2c[:, :], a2c[:, :], pa[:, 0:8]),
                  reads=[pa, a2c], writes=[a2c])
            return reduce



    p1_iter = iter(range(32))
    pending_p1 = []

    for rc in range(2):
        for ntile in range(S // 512):
            pb = bk[ntile % 2]
            for dc in range(DC):
                E("tensor", lambda e, rc=rc, nt=ntile, dc=dc, pb=pb: e.matmul(
                    pb[:, :], wv[:, dc * 256 + rc * 128: dc * 256 + rc * 128 + 128],
                    xb[:, dc * S + nt * 512: dc * S + nt * 512 + 512],
                    start=(dc == 0), stop=(dc == DC - 1)),
                    reads=[wv, xb], writes=[pb])
            E("vector", lambda e, rc=rc, nt=ntile, pb=pb: e.tensor_copy(
                t[:, rc * S + nt * 512: rc * S + nt * 512 + 512], pb[:, :]),
                reads=[pb], writes=[t])

    pending_evac = [None]
    pending_avtail = [None]
    for g in range(8):
        if pending_avtail[0] is not None:
            pending_avtail[0]()
            pending_avtail[0] = None
        if pending_evac[0] is not None:
            pending_evac[0]()
            pending_evac[0] = None
        fq = D + g * 128
        fk = g * 128
        fv = 2 * D + g * 128
        scr = r_sc[0:1, 0:512]
        mk2 = r_sc[0:1, 512:513]
        mq2 = r_sc[0:1, 513:514]
        prd = r_sc[0:1, 514:515]
        sqr = r_sc[0:1, 515:516]
        ngc = r_sc[0:1, 516:517]
        cur = r_sc[0:1, 517:518]
        E("vector", lambda e, a=mk2: e.memset(a, 0.0), writes=[r_sc])
        E("vector", lambda e, a=mq2: e.memset(a, 0.0), writes=[r_sc])
        def csq_batch(src_, ntiles, mrow):
            nb = ntiles * 512 * 2
            E("scalar", lambda e, n=ntiles, src_=src_: e.activation(
                sq[:, 0:n * 512], src_[:, 0:n * 512], AF.Square),
                reads=[src_], writes=[(sq, ("B", 0, nb))])
            for nt in range(ntiles):
                pb = bk[6] if nt % 2 == 0 else bk[7]
                bs = ("B", nt * 1024, (nt + 1) * 1024)
                E("tensor", lambda e, nt=nt, pb=pb: e.matmul(
                    pb[0:1, :], cm.ones_colb[:, :], sq[:, nt * 512: nt * 512 + 512],
                    start=True, stop=True),
                    reads=[(sq, bs), cm.ones_colb], writes=[pb])
                E("vector", lambda e, pb=pb, a=cur: e.reduce_max(a, pb[0:1, :], axis=AX.X),
                    reads=[pb], writes=[r_sc])
                E("vector", lambda e, mrow=mrow, b=cur: e.tensor_max(mrow, mrow, b),
                    reads=[r_sc], writes=[r_sc])
        for ntile in range(S // 512):
            pb = bk[ntile % 2]
            for rc in range(2):
                E("tensor", lambda e, nt=ntile, rc=rc, pb=pb, fq=fq: e.matmul(
                    pb[:, :], wu[:, rc * 3 * D + fq: rc * 3 * D + fq + 128],
                    t[:, rc * S + nt * 512: rc * S + nt * 512 + 512],
                    start=(rc == 0), stop=(rc == 1)),
                    reads=[wu, t], writes=[pb])
            E("vector", lambda e, nt=ntile, pb=pb: e.tensor_copy(
                Qg[:, nt * 512: nt * 512 + 512], pb[:, :]),
                reads=[pb], writes=[Qg])
        csq_batch(Qg, S // 512, mq2)
        for ntile in range(TOK // 512):
            pb = bk[2 + ntile % 2]
            for rc in range(2):
                E("tensor", lambda e, nt=ntile, rc=rc, pb=pb, fk=fk: e.matmul(
                    pb[:, :], wu[:, rc * 3 * D + fk: rc * 3 * D + fk + 128],
                    t[:, rc * S + nt * 512: rc * S + nt * 512 + 512],
                    start=(rc == 0), stop=(rc == 1)),
                    reads=[wu, t], writes=[pb])
            E("vector", lambda e, nt=ntile, pb=pb: e.tensor_copy(
                Kg[:, nt * 512: nt * 512 + 512], pb[:, :]),
                reads=[pb], writes=[Kg])
        csq_batch(Kg, TOK // 512, mk2)
        for jc in range(32):
            pb = bk[4 + jc % 2]
            for rc in range(2):
                E("tensor", lambda e, jc=jc, rc=rc, pb=pb, fv=fv: e.matmul(
                    pb[:, 0:128], t[:, rc * S + jc * 128: rc * S + jc * 128 + 128],
                    wu[:, rc * 3 * D + fv: rc * 3 * D + fv + 128],
                    start=(rc == 0), stop=(rc == 1)),
                    reads=[wu, t], writes=[pb])
            for h in range(2):
                E("vector", lambda e, jc=jc, h=h, pb=pb: e.tensor_copy(
                    Vg[:, jc * 130 + h * 65: jc * 130 + h * 65 + 64],
                    pb[:, h * 64: h * 64 + 64]),
                    reads=[pb], writes=[Vg])
        E("vector", lambda e: e.memset(
            Vg[:, :].rearrange("p (j h c) -> p j h c", j=32, h=2)[:, :, :, 64:65], 1.0),
            writes=[Vg])

        E("vector", lambda e, a=prd, b=mk2, c=mq2: e.tensor_mul(a, b, c), reads=[r_sc], writes=[r_sc])
        E("scalar", lambda e, a=sqr, b=prd: e.activation(a, b, AF.Sqrt), reads=[r_sc], writes=[r_sc])
        E("vector", lambda e, a=ngc, b=sqr: e.tensor_scalar(a, b, -0.125, None, ALU.mult),
          reads=[r_sc], writes=[r_sc])
        pb = bk[6]
        E("tensor", lambda e, pb=pb, a=ngc: e.matmul(pb[:, 0:1], cm.ones_row[:, :], a,
                                                     start=True, stop=True),
          reads=[r_sc, cm.ones_row], writes=[pb])
        E("vector", lambda e, pb=pb: e.tensor_copy(negc[:, :], pb[:, 0:1]),
          reads=[pb], writes=[negc])

        for it in range(NSLOT):
            def sprime(jc, h):
                sb_ = bk[2 * (jc % 2) + h]
                pbase = h * 64
                E("tensor", lambda e, jc=jc, it=it, pbase=pbase, sb_=sb_: e.matmul(
                    sb_[:, :], Qg[pbase:pbase + 64, jc * 128: jc * 128 + 128],
                    Kg[pbase:pbase + 64, it * 512: it * 512 + 512],
                    start=True, stop=True),
                    reads=[Qg, Kg], writes=[sb_])
            def expw(jc):
                c0 = (jc % 2) * 2 * SL
                p0 = (((jc // 2) % 2) * 4 + (jc % 2) * 2) * SL
                E("scalar", lambda e, c0=c0, p0=p0: e.activation(
                    Pp[:, p0:p0 + 2 * SL], S4[:, c0:c0 + 2 * SL], AF.Exp,
                    bias=negc[:, 0:1], scale=0.125),
                    reads=[(S4, ("B", c0 * 4, (c0 + 2 * SL) * 4)), negc],
                    writes=[(Pp, ("B", p0 * 2, (p0 + 2 * SL) * 2))])
            def av(jc, h):
                s0 = (((jc // 2) % 2) * 4 + (jc % 2) * 2 + h) * SL
                nbank = bk[4 + h]
                E("tensor", lambda e, jc=jc, h=h, s0=s0, nbank=nbank: e.matmul(
                    nbank[0:65, :], Vg[:, jc * 130 + h * 65: jc * 130 + h * 65 + 65],
                    Pp[:, s0:s0 + SL], start=(jc == 0), stop=(jc == 31)),
                    reads=[Vg, (Pp, ("B", s0 * 2, (s0 + SL) * 2))], writes=[nbank])
            def mk_evac(g_, it_):
                def run():
                    for h in range(2):
                        nbank = bk[4 + h]
                        E("vector", lambda e, h=h, nbank=nbank: e.reciprocal(
                            rl[0:1, h * SL:(h + 1) * SL], nbank[64:65, :]),
                            reads=[nbank], writes=[(rl, ("B", h * SL * 4, (h + 1) * SL * 4))])
                    for h in range(2):
                        pbx = bk[6] if h == 0 else bk[7]
                        E("tensor", lambda e, h=h, pbx=pbx: e.matmul(
                            pbx[0:64, :], cm.ones_row[0:1, 0:64],
                            rl[0:1, h * SL:(h + 1) * SL], start=True, stop=True),
                            reads=[(rl, ("B", h * SL * 4, (h + 1) * SL * 4)), cm.ones_row],
                            writes=[pbx])
                    for h in range(2):
                        pbx = bk[6] if h == 0 else bk[7]
                        E("vector", lambda e, h=h, pbx=pbx: e.tensor_copy(
                            rlb[h * 64:h * 64 + 64, :], pbx[0:64, :]),
                            reads=[pbx], writes=[rlb])
                    for h in range(2):
                        nbank = bk[4 + h]
                        hh = 2 * g_ + h
                        fc_, pb_ = hh // 2, (hh % 2) * 64
                        E("vector", lambda e, h=h, it_=it_, fc_=fc_, pb_=pb_, nbank=nbank: e.tensor_mul(
                            y[pb_:pb_ + 64, fc_ * TOK + it_ * 512: fc_ * TOK + it_ * 512 + 512],
                            nbank[0:64, :], rlb[h * 64:h * 64 + 64, :]),
                            reads=[nbank, rlb], writes=[y])
                return run
            sprime(0, 0); sprime(0, 1); expw(0)
            if pending_avtail[0] is not None:
                pending_avtail[0]()
                pending_avtail[0] = None
            if pending_evac[0] is not None:
                pending_evac[0]()
                pending_evac[0] = None
            for f in pending_p1:
                f()
            pending_p1.clear()
            sprime(1, 0); sprime(1, 1); expw(1)
            for jc in range(2, 32):
                sprime(jc, 0); sprime(jc, 1)
                expw(jc)
                av(jc - 2, 0); av(jc - 2, 1)
            k_ = next(p1_iter, None)
            if k_ is not None:
                pending_p1.append(p1_w1_step(k_))
                pending_p1.append(p1_w2_step(k_))
            pending_avtail[0] = (lambda avf=av: (avf(30, 0), avf(30, 1),
                                                avf(31, 0), avf(31, 1)))
            pending_evac[0] = mk_evac(g, it)
    pending_avtail[0]()
    pending_avtail[0] = None
    pending_evac[0]()
    pending_evac[0] = None

    for f in pending_p1:
        f()
    pending_p1.clear()

    for rc in range(2):
        for ntile in range(NSLOT):
            pb = bk[ntile % 2]
            for dc in range(DC):
                E("tensor", lambda e, rc=rc, nt=ntile, dc=dc, pb=pb: e.matmul(
                    pb[:, :], wov[:, dc * 256 + rc * 128: dc * 256 + rc * 128 + 128],
                    y[:, dc * TOK + nt * 512: dc * TOK + nt * 512 + 512],
                    start=(dc == 0), stop=(dc == DC - 1)),
                    reads=[wov, y], writes=[pb])
            E("vector", lambda e, rc=rc, nt=ntile, pb=pb: e.tensor_copy(
                z1[:, rc * TOK + nt * 512: rc * TOK + nt * 512 + 512], pb[:, :]),
                reads=[pb], writes=[z1])

    for ntile in range(NSLOT):
        xsl = xseq.rearrange("(c p) x -> p c x", p=128)[:, :, ntile * 512:(ntile + 1) * 512]
        E("sync", lambda e, xsl=xsl: e.dma_start(
            out=xotA[:, :].rearrange("p (c x) -> p c x", c=DC), in_=xsl),
            writes=[xotA], dma=True)
        for fc in range(DC):
            pb = bk[2 + fc % 2]
            for rc in range(2):
                E("tensor", lambda e, fc=fc, nt=ntile, rc=rc, pb=pb: e.matmul(
                    pb[:, :], wou[:, rc * D + fc * 128: rc * D + fc * 128 + 128],
                    z1[:, rc * TOK + nt * 512: rc * TOK + nt * 512 + 512],
                    start=(rc == 0), stop=(rc == 1)),
                    reads=[wou, z1], writes=[pb])
            E("vector", lambda e, fc=fc, pb=pb: e.tensor_add(
                xresA[:, fc * SL:(fc + 1) * SL], pb[:, :], xotA[:, fc * SL:(fc + 1) * SL]),
                reads=[pb, xotA], writes=[xresA])
        cm.layernorm_nt(xresA, bk[6], bk[7], out_f32=xresA)
        dstv = x1d.rearrange("(c p) x -> p c x", p=128)[:, :, ntile * 512:(ntile + 1) * 512]
        E("sync", lambda e, dstv=dstv: e.dma_start(
            out=dstv, in_=xresA[:, :].rearrange("p (c x) -> p c x", c=DC)),
            reads=[xresA], writes=[(x1d, ntile)], dma=True)

    xsbB = ar.at("xsbB", 0, [128, DC * TOK], BF16)
    hB = ar.at("hB", 32768, [128, 32 * SL], BF16)
    xotB = ar.at("xotB", 65536, [128, DC * SL], F32)
    xresB = ar.at("xresB", 81920, [128, DC * SL], F32)
    s1buf = [ar.at(f"s1b{i}", 106496 + i * 2048, [128, 1024], BF16) for i in range(4)]
    s2buf = [ar.at(f"s2b{i}", 143360 + i * 2048, [128, 1024], BF16) for i in range(4)]
    otB = ar.at("otB", 115712, [128, SL], F32)
    xrbB = ar.at("xrbB", 118784, [128, DC * SL], BF16)
    x2bB = ar.at("x2bB", 126976, [128, DC * SL], BF16)
    lrowsB = ar.at("lrowsB", 135168, [1, 3 * SL], F32)
    cm.attach_ln_scratch(xrbB, x2bB, lrowsB)
    load_fm(em, xsbB, x1d, DC, TOK, eng="gpsimd", tag="ldx1")
    em.ops[-1].reads.extend([em._keys((x1d, nt))[0] for nt in range(NSLOT)])
    for nt in range(NSLOT):
        w = em.last_writer.get(em._keys((x1d, nt))[0])
        if w is not None:
            ch = ("d", w.eng, w.dma_slot) if w.is_dma else ("e", w.eng)
            cur = em.ops[-1].waits.get(ch)
            if cur is None or w.seq > cur.seq:
                em.ops[-1].waits[ch] = w

    for st in range(4):
        xsl = x1d.rearrange("(c p) x -> p c x", p=128)[:, :, st * 512:(st + 1) * 512]
        E("sync", lambda e, xsl=xsl: e.dma_start(
            out=xotB[:, :].rearrange("p (c x) -> p c x", c=DC), in_=xsl),
            reads=[(x1d, st)], writes=[xotB], dma=True)
        for hfc in range(32):
            sb_ = s1buf[hfc % 4]
            E("sync", lambda e, hfc=hfc, sb_=sb_: e.dma_start(out=sb_[:, :], in_=s1s[hfc]),
              reads=[(s1s, hfc)], writes=[sb_], dma=True)
            pb = bk[hfc % 2]
            for dc in range(DC):
                E("tensor", lambda e, dc=dc, st=st, sb_=sb_, pb=pb: e.matmul(
                    pb[:, :], sb_[:, dc * 128:(dc + 1) * 128],
                    xsbB[:, dc * TOK + st * 512: dc * TOK + st * 512 + 512],
                    start=(dc == 0), stop=(dc == DC - 1)),
                    reads=[sb_, xsbB], writes=[pb])
            E("scalar", lambda e, hfc=hfc, pb=pb: e.activation(
                hB[:, hfc * SL:(hfc + 1) * SL], pb[:, :], AF.Gelu,
                bias=b1s[:, hfc:hfc + 1], scale=a1c[:, hfc:hfc + 1]),
                reads=[pb, b1s, a1c], writes=[hB])
        for hfc in range(32):
            sb_ = s2buf[hfc % 4]
            E("scalar", lambda e, hfc=hfc, sb_=sb_: e.dma_start(
                out=sb_[:, :], in_=s2s[hfc * 128:(hfc + 1) * 128, :]),
                reads=[(s2s, hfc)], writes=[sb_], dma=True)
            for fc in range(DC):
                E("tensor", lambda e, fc=fc, hfc=hfc, sb_=sb_: e.matmul(
                    bk[fc][:, :], sb_[:, fc * 128:(fc + 1) * 128],
                    hB[:, hfc * SL:(hfc + 1) * SL],
                    start=(hfc == 0), stop=(hfc == 31)),
                    reads=[sb_, hB], writes=[bk[fc]])
        for fc in range(DC):
            E("vector", lambda e, fc=fc: e.tensor_scalar(
                otB[:, :], bk[fc][:, :], a2c[:, fc:fc + 1], b2s[:, fc:fc + 1],
                ALU.mult, ALU.add),
                reads=[bk[fc], b2s, a2c], writes=[otB])
            E("vector", lambda e, fc=fc: e.tensor_add(
                xresB[:, fc * SL:(fc + 1) * SL], otB[:, :], xotB[:, fc * SL:(fc + 1) * SL]),
                reads=[otB, xotB], writes=[xresB])
        cm.layernorm_nt(xresB, bk[0], bk[1], out_f32=xresB)
        dstv = x2d.rearrange("(c p) x -> p c x", p=128)[:, :, st * 512:(st + 1) * 512]
        E("sync", lambda e, dstv=dstv: e.dma_start(
            out=dstv, in_=xresB[:, :].rearrange("p (c x) -> p c x", c=DC)),
            reads=[xresB], writes=[(x2d, st)], dma=True)

    xsbC = ar.at("xsbC", 0, [128, DC * TOK], BF16)
    wTC = ar.at("wTC", 32768, [128, DC * D], BF16)
    qtC = ar.at("qtC", 49152, [128, DC * SL], BF16)
    sqvC = ar.at("sqvC", 57344, [128, SL], BF16)
    q2rC = ar.at("q2rC", 58368, [1, SL], F32)
    KoC = ar.at("KoC", 160768, [128, DC * TOK], BF16)

    load_fm(em, xsbC, x2d, DC, TOK, eng="gpsimd", tag="ldx2")
    em.ops[-1].reads.extend([em._keys((x2d, nt))[0] for nt in range(NSLOT)])
    for nt in range(NSLOT):
        w = em.last_writer.get(em._keys((x2d, nt))[0])
        if w is not None:
            ch = ("d", w.eng, w.dma_slot) if w.is_dma else ("e", w.eng)
            cur = em.ops[-1].waits.get(ch)
            if cur is None or w.seq > cur.seq:
                em.ops[-1].waits[ch] = w

    for wsrc, with_q2 in ((wqT, True), (wkT, False)):
        load_fm(em, wTC, wsrc, DC, D, eng="gpsimd", tag="ldw")
        for ntile in range(NSLOT):
            for fc in range(DC):
                pb = bk[fc % 2]
                for dc in range(DC):
                    E("tensor", lambda e, fc=fc, nt=ntile, dc=dc, pb=pb: e.matmul(
                        pb[:, :], wTC[:, dc * D + fc * 128: dc * D + fc * 128 + 128],
                        xsbC[:, dc * TOK + nt * 512: dc * TOK + nt * 512 + 512],
                        start=(dc == 0), stop=(dc == DC - 1)),
                        reads=[wTC, xsbC], writes=[pb])
                if with_q2:
                    E("scalar", lambda e, fc=fc, pb=pb: e.activation(
                        qtC[:, fc * SL:(fc + 1) * SL], pb[:, :], AF.Copy),
                        reads=[pb], writes=[qtC])
                else:
                    E("scalar", lambda e, fc=fc, nt=ntile, pb=pb: e.activation(
                        KoC[:, fc * TOK + nt * 512: fc * TOK + nt * 512 + 512], pb[:, :], AF.Copy),
                        reads=[pb], writes=[KoC])
            if with_q2:
                for j4 in range(4):
                    r0 = (ntile * 4 + j4) * 128
                    dstv = eQ.ap()[r0:r0 + 128, :].rearrange("p (dc c) -> p dc c", dc=DC)
                    srcv = qtC[:, :].rearrange("p (dc j c) -> p j dc c", dc=DC, j=4)[:, j4]
                    E("sync", lambda e, dstv=dstv, srcv=srcv: e.dma_start(out=dstv, in_=srcv),
                        reads=[qtC], writes=[(eQ, ntile)], dma=True)
                pq = bk[2]
                for fc in range(DC):
                    E("scalar", lambda e, fc=fc: e.activation(
                        sqvC[:, :], qtC[:, fc * SL:(fc + 1) * SL], AF.Square),
                        reads=[qtC], writes=[sqvC])
                    E("tensor", lambda e, fc=fc, pq=pq: e.matmul(
                        pq[0:1, :], cm.ones_colb[:, :], sqvC[:, :],
                        start=(fc == 0), stop=(fc == DC - 1)),
                        reads=[sqvC, cm.ones_colb], writes=[pq])
                E("vector", lambda e, pq=pq: e.tensor_copy(q2rC[0:1, :], pq[0:1, :]),
                  reads=[pq], writes=[q2rC])
                E("sync", lambda e, nt=ntile: e.dma_start(
                    out=eq2.ap()[0:1, nt * 512:(nt + 1) * 512], in_=q2rC[0:1, :]),
                    reads=[q2rC], writes=[(eq2, ntile)], dma=True)

        if with_q2:
            grp = [[0, 1], [2, 3], [4, 5], [6, 7]]
            E("gpsimd", lambda e: e.collective_compute(
                "AllGather", ALU.bypass, replica_groups=grp,
                ins=[eQ.ap().opt()], outs=[gQ.ap().opt()]),
                reads=[(eQ, nt) for nt in range(NSLOT)], writes=[gQ], tag="cc")
            E("gpsimd", lambda e: e.collective_compute(
                "AllGather", ALU.bypass, replica_groups=grp,
                ins=[eq2.ap().opt()], outs=[gq2.ap().opt()]),
                reads=[(eq2, nt) for nt in range(NSLOT)], writes=[gq2], tag="cc")

    load_fm(em, wTC, wvT, DC, D, eng="gpsimd", tag="ldw")
    for tc in range(TOK // 128):
        for half in range(2):
            pb = bk[half]
            for dc in range(DC):
                E("tensor", lambda e, tc=tc, half=half, dc=dc, pb=pb: e.matmul(
                    pb[:, :], xsbC[:, dc * TOK + tc * 128: dc * TOK + tc * 128 + 128],
                    wTC[:, dc * D + half * 512: dc * D + half * 512 + 512],
                    start=(dc == 0), stop=(dc == DC - 1)),
                    reads=[wTC, xsbC], writes=[pb])
            E("scalar", lambda e, half=half, pb=pb: e.activation(
                qtC[:, half * 512: half * 512 + 512], pb[:, :], AF.Copy),
                reads=[pb], writes=[qtC])
        E("sync", lambda e, tc=tc: e.dma_start(
            out=eV.ap()[tc * 128:(tc + 1) * 128, :], in_=qtC[:, 0:D]),
            reads=[qtC], writes=[(eV, tc)], dma=True)

    grp = [[0, 1], [2, 3], [4, 5], [6, 7]]
    ccv = E("gpsimd", lambda e: e.collective_compute(
        "AllGather", ALU.bypass, replica_groups=grp,
        ins=[eV.ap().opt()], outs=[gV.ap().opt()]),
        reads=[(eV, tc) for tc in range(TOK // 128)], writes=[gV], tag="cc")

    w1e = ar.at("w1e", 115712, [128, DC * 8 * 256], BF16)
    load_fm(em, w1e, W1f, DC, 8 * 256, eng="gpsimd")
    qbufC = [ar.at(f"qbC{i}", i * 2048, [128, DC * 128], BF16) for i in range(4)]
    vbufC = [ar.at(f"vbC{i}", 8192 + i * 2048, [128, D], BF16) for i in range(4)]
    mbufC = [ar.at(f"mbC{i}", 16384 + i * 1024, [128, SL], BF16) for i in range(2)]
    PpC = ar.at("PpC", 18432, [128, 32 * SL], BF16)
    xotC = ar.at("xotC", 51200, [128, DC * SL], F32)
    xresC = ar.at("xresC", 67584, [128, DC * SL], F32)
    xrbC = ar.at("xrbC", 83968, [128, DC * SL], BF16)
    x2bC = ar.at("x2bC", 92160, [128, DC * SL], BF16)
    lrowsC = ar.at("lrowsC", 100352, [1, 3 * SL], F32)
    sqv2 = ar.at("sqv2", 106496, [128, SL], BF16)
    scrC = ar.at("scrC", 107520, [1, SL], F32)
    scaC = ar.at("scaC", 109568, [1, 16], F32)
    negcC = ar.at("negcC", 109632, [128, 1], F32)
    rlC = ar.at("rlC", 109696, [1, SL], F32)
    rlbC = ar.at("rlbC", 111744, [128, SL], F32)
    cm.attach_ln_scratch(xrbC, x2bC, lrowsC)

    mk2, mq2 = scaC[0:1, 0:1], scaC[0:1, 1:2]
    prd, sqr = scaC[0:1, 2:3], scaC[0:1, 3:4]
    ngc, cur = scaC[0:1, 4:5], scaC[0:1, 5:6]
    E("vector", lambda e, a=mk2: e.memset(a, 0.0), writes=[scaC])
    E("vector", lambda e, a=mq2: e.memset(a, 0.0), writes=[scaC])
    for ntile in range(NSLOT):
        pq = bk[0]
        for fc in range(DC):
            E("scalar", lambda e, fc=fc, nt=ntile: e.activation(
                sqv2[:, :], KoC[:, fc * TOK + nt * 512: fc * TOK + nt * 512 + 512], AF.Square),
                reads=[KoC], writes=[sqv2])
            E("tensor", lambda e, fc=fc, pq=pq: e.matmul(
                pq[0:1, :], cm.ones_colb[:, :], sqv2[:, :],
                start=(fc == 0), stop=(fc == DC - 1)),
                reads=[sqv2, cm.ones_colb], writes=[pq])
        E("vector", lambda e, pq=pq: e.tensor_copy(scrC[0:1, :], pq[0:1, :]),
          reads=[pq], writes=[scrC])
        E("vector", lambda e, a=cur: e.reduce_max(a, scrC[0:1, :], axis=AX.X),
          reads=[scrC], writes=[scaC])
        E("vector", lambda e, a=mk2, b=cur: e.tensor_max(a, a, b), reads=[scaC], writes=[scaC])
    for ntile in range(S // 512):
        gsl = gq2.ap()[ntile // 4: ntile // 4 + 1, (ntile % 4) * 512:(ntile % 4) * 512 + 512]
        E("sync", lambda e, gsl=gsl: e.dma_start(out=scrC[0:1, :], in_=gsl),
          reads=[gq2], writes=[scrC], dma=True)
        E("vector", lambda e, a=cur: e.reduce_max(a, scrC[0:1, :], axis=AX.X),
          reads=[scrC], writes=[scaC])
        E("vector", lambda e, a=mq2, b=cur: e.tensor_max(a, a, b), reads=[scaC], writes=[scaC])
    E("vector", lambda e, a=prd, b=mk2, c=mq2: e.tensor_mul(a, b, c), reads=[scaC], writes=[scaC])
    E("scalar", lambda e, a=sqr, b=prd: e.activation(a, b, AF.Sqrt), reads=[scaC], writes=[scaC])
    E("vector", lambda e, a=ngc, b=sqr: e.tensor_scalar(a, b, -1.0 / 32.0, None, ALU.mult),
      reads=[scaC], writes=[scaC])
    E("tensor", lambda e, a=ngc: e.matmul(bk[3][:, 0:1], cm.ones_row[:, :], a, start=True, stop=True),
      reads=[scaC, cm.ones_row], writes=[bk[3]])
    E("vector", lambda e: e.tensor_copy(negcC[:, :], bk[3][:, 0:1]),
      reads=[bk[3]], writes=[negcC])

    for it in range(NSLOT):
        def qload(jc):
            qb_ = qbufC[jc % 4]
            r0 = jc * 128 if jc < 16 else TOK + (jc - 16) * 128
            E("sync", lambda e, r0=r0, qb_=qb_: e.dma_start(
                out=qb_[:, :], in_=gQ.ap()[r0:r0 + 128, :]),
                reads=[gQ], writes=[qb_], dma=True)
        def sgen(jc):
            qb_ = qbufC[jc % 4]
            sb_ = bk[jc % 2]
            for dc in range(DC):
                E("tensor", lambda e, jc=jc, it=it, dc=dc, sb_=sb_, qb_=qb_: e.matmul(
                    sb_[:, :], qb_[:, dc * 128:(dc + 1) * 128],
                    KoC[:, dc * TOK + it * 512: dc * TOK + it * 512 + 512],
                    start=(dc == 0), stop=(dc == DC - 1)),
                    reads=[qb_, KoC], writes=[sb_])
            E("scalar", lambda e, jc=jc, sb_=sb_: e.activation(
                PpC[:, jc * SL:(jc + 1) * SL], sb_[:, :], AF.Exp,
                bias=negcC[:, 0:1], scale=1.0 / 32.0),
                reads=[sb_, negcC], writes=[(PpC, ("B", jc * SL * 2, (jc + 1) * SL * 2))])
        def lacc(jc):
            E("tensor", lambda e, jc=jc: e.matmul(
                bk[2][0:1, :], cm.ones_colb[:, :], PpC[:, jc * SL:(jc + 1) * SL],
                start=(jc == 0), stop=(jc == 31)),
                reads=[(PpC, ("B", jc * SL * 2, (jc + 1) * SL * 2)), cm.ones_colb], writes=[bk[2]])
        inc_list = [g_ * 4 + cl for g_ in INC_TILES[it] for cl in range(4)]
        inc_pos = {jc: k for k, jc in enumerate(inc_list)}
        msk = {}
        for mi, g_ in enumerate(MSK_TILES[it]):
            for cl in range(4):
                msk[g_ * 4 + cl] = mi * 4 + cl
        nI = len(inc_list)
        def vload(k, half):
            if k >= nI:
                return
            jc = inc_list[k]
            vb_ = vbufC[k % 4]
            r0 = jc * 128 if jc < 16 else TOK + (jc - 16) * 128
            E("scalar", lambda e, r0=r0, vb_=vb_: e.dma_start(
                out=vb_[:, :], in_=gV.ap()[r0:r0 + 128, :]),
                reads=[gV], writes=[vb_], dma=True)
        def do_mask(jc):
            mb_ = mbufC[msk[jc] % 2]
            E("sync", lambda e, it=it, m=msk[jc], mb_=mb_: e.dma_start(
                out=mb_[:, :], in_=cmask[it, m]),
                writes=[mb_], dma=True)
            E("vector", lambda e, jc=jc, mb_=mb_: e.tensor_mul(
                PpC[:, jc * SL:(jc + 1) * SL], PpC[:, jc * SL:(jc + 1) * SL], mb_[:, :]),
                reads=[(PpC, ("B", jc * SL * 2, (jc + 1) * SL * 2)), mb_],
                writes=[(PpC, ("B", jc * SL * 2, (jc + 1) * SL * 2))])
        def avmm(jc, half, k):
            vb_ = vbufC[k % 4]
            for dvl in range(4):
                dv = half * 4 + dvl
                E("tensor", lambda e, jc=jc, dv=dv, dvl=dvl, vb_=vb_, k=k, nI=nI: e.matmul(
                    bk[4 + dvl][:, :], vb_[:, dv * 128:(dv + 1) * 128],
                    PpC[:, jc * SL:(jc + 1) * SL],
                    start=(k == 0), stop=(k == nI - 1)),
                    reads=[vb_, (PpC, ("B", jc * SL * 2, (jc + 1) * SL * 2))], writes=[bk[4 + dvl]])
        xsl = x2d.rearrange("(c p) x -> p c x", p=128)[:, :, it * 512:(it + 1) * 512]
        E("sync", lambda e, xsl=xsl: e.dma_start(
            out=xotC[:, :].rearrange("p (c x) -> p c x", c=DC), in_=xsl),
            reads=[(x2d, it)], writes=[xotC], dma=True)
        qload(0); qload(1); qload(2)
        vload(0, 0); vload(1, 0)
        sgen(0)
        for jc in range(1, 32):
            if jc + 2 < 32:
                qload(jc + 2)
            sgen(jc)
            lacc(jc - 1)
            d = jc - 2
            if d in inc_pos:
                if d in msk:
                    do_mask(d)
                k = inc_pos[d]
                vload(k + 2, 0)
                avmm(d, 0, k)
        lacc(31)
        for d in (30, 31):
            if d in inc_pos:
                if d in msk:
                    do_mask(d)
                avmm(d, 0, inc_pos[d])
        E("vector", lambda e: e.tensor_copy(scrC[0:1, :], bk[2][0:1, :]),
          reads=[bk[2]], writes=[scrC])
        E("vector", lambda e: e.reciprocal(rlC[0:1, :], scrC[0:1, :]),
          reads=[scrC], writes=[rlC])
        E("tensor", lambda e: e.matmul(bk[3][:, :], cm.ones_row[:, :], rlC[0:1, :],
                                       start=True, stop=True),
          reads=[rlC, cm.ones_row], writes=[bk[3]])
        E("vector", lambda e: e.tensor_copy(rlbC[:, :], bk[3][:, :]),
          reads=[bk[3]], writes=[rlbC])
        for dvl in range(4):
            E("vector", lambda e, dvl=dvl: e.tensor_mul(
                cm.tmp[:, :], bk[4 + dvl][:, :], rlbC[:, :]),
                reads=[bk[4 + dvl], rlbC], writes=[cm.tmp])
            E("vector", lambda e, dv=dvl: e.tensor_add(
                xresC[:, dv * SL:(dv + 1) * SL], cm.tmp[:, :], xotC[:, dv * SL:(dv + 1) * SL]),
                reads=[cm.tmp, xotC], writes=[xresC])
        vload(0, 1); vload(1, 1)
        for k, jc in enumerate(inc_list):
            vload(k + 2, 1)
            avmm(jc, 1, k)
        for dvl in range(4):
            dv = 4 + dvl
            E("vector", lambda e, dvl=dvl: e.tensor_mul(
                cm.tmp[:, :], bk[4 + dvl][:, :], rlbC[:, :]),
                reads=[bk[4 + dvl], rlbC], writes=[cm.tmp])
            E("vector", lambda e, dv=dv: e.tensor_add(
                xresC[:, dv * SL:(dv + 1) * SL], cm.tmp[:, :], xotC[:, dv * SL:(dv + 1) * SL]),
                reads=[cm.tmp, xotC], writes=[xresC])
        cm.layernorm_nt(xresC, bk[0], bk[1], out_f32=xresC)
        dstv = x3d.rearrange("(c p) x -> p c x", p=128)[:, :, it * 512:(it + 1) * 512]
        E("sync", lambda e, dstv=dstv: e.dma_start(
            out=dstv, in_=xresC[:, :].rearrange("p (c x) -> p c x", c=DC)),
            reads=[xresC], writes=[(x3d, it)], dma=True)

    w2e = ar.at("w2e", 32768, [128, 2 * 8 * D], BF16)
    xotD = ar.at("xotD", 65536, [128, DC * SL], F32)
    xresD = ar.at("xresD", 81920, [128, DC * SL], F32)
    xebD2 = ar.at("xebD2", 98304, [128, DC * SL], BF16)
    xhatD = ar.at("xhatD", 24576, [128, DC * SL], BF16)
    xebD = ar.at("xebD", 16384, [128, DC * SL], BF16)
    hallD = ar.at("hallD", 0, [128, 16 * SL], BF16)
    xrbD = ar.at("xrbD", 168960, [128, DC * SL], BF16)
    x2bD = ar.at("x2bD", 177152, [128, DC * SL], BF16)
    lrowsD = ar.at("lrowsD", 106496, [1, 3 * SL], F32)
    wgD = ar.at("wgD", 161792, [128, DC * 8], F32)
    lnsD = ar.at("lnsD", 162048, [128, DC * 8], F32)
    lnbD = ar.at("lnbD", 162304, [128, DC * 8], F32)
    b1sD = ar.at("b1sD", 162560, [128, 2 * 8], F32)
    b2bD = ar.at("b2bD", 162688, [8, D], BF16)
    idtD = ar.at("idtD", 164736, [128, 128], F32)
    indbD = ar.at("indbD", 165248, [8, 1024], BF16)
    rmxD = ar.at("rmxD", 167296, [128, 1], F32)
    mtmD = ar.at("mtmD", 167360, [128, 8], F32)
    maskbD = ar.at("maskbD", 167424, [8, SL], BF16)
    cm.attach_ln_scratch(xrbD, x2bD, lrowsD)

    load_fm(em, wgD, WgT, DC, 8, eng="sync")
    load_fm(em, lnsD, lnsI, DC, 8, eng="sync")
    load_fm(em, lnbD, lnbI, DC, 8, eng="sync")
    load_fm(em, b1sD, b1f, 2, 8, eng="sync")
    load_fm(em, w2e, W2f, 2, 8 * D, eng="gpsimd")
    E("gpsimd", lambda e: e.dma_start(out=b2bD[:, :], in_=b2I), writes=[b2bD], dma=True)
    E("sync", lambda e: e.dma_start(out=idtD[:, :], in_=idI), writes=[idtD], dma=True)
    E("sync", lambda e: e.dma_start(out=indbD[:, :], in_=indI), writes=[indbD], dma=True)

    for tt in range(NSLOT):
        xsl = x3d.rearrange("(c p) x -> p c x", p=128)[:, :, tt * 512:(tt + 1) * 512]
        E("sync", lambda e, xsl=xsl: e.dma_start(
            out=xotD[:, :].rearrange("p (c x) -> p c x", c=DC), in_=xsl),
            reads=[(x3d, tt)], writes=[xotD], dma=True)
        for m4 in range(4):
            pg = bk[0]
            for dc in range(DC):
                E("tensor", lambda e, m4=m4, dc=dc, pg=pg: e.matmul(
                    pg[:, 0:8], xotD[:, dc * SL + m4 * 128: dc * SL + m4 * 128 + 128],
                    wgD[:, dc * 8:(dc + 1) * 8],
                    start=(dc == 0), stop=(dc == DC - 1)),
                    reads=[xotD, wgD], writes=[pg])
            E("vector", lambda e, pg=pg: e.reduce_max(rmxD[:, :], pg[:, 0:8], axis=AX.X),
              reads=[pg], writes=[rmxD])
            E("vector", lambda e, pg=pg: e.tensor_scalar(
                mtmD[:, :], pg[:, 0:8], rmxD[:, 0:1], None, ALU.is_ge),
                reads=[pg, rmxD], writes=[mtmD])
            E("tensor", lambda e: e.transpose(bk[3][0:8, 0:128], mtmD[:, :], idtD[:, :]),
              reads=[mtmD, idtD], writes=[bk[3]])
            E("vector", lambda e, m4=m4: e.tensor_copy(
                maskbD[0:8, m4 * 128:(m4 + 1) * 128], bk[3][0:8, 0:128]),
                reads=[bk[3]], writes=[maskbD])
        cm.layernorm_nt(xotD, bk[6], bk[7], out_f32=None, out_bf16=xhatD)
        def xeb_chain(ex):
            xe_ = xebD if ex % 2 == 0 else xebD2
            for dc in range(DC):
                E("vector", lambda e, ex=ex, dc=dc, xe_=xe_: e.tensor_scalar(
                    xe_[:, dc * SL:(dc + 1) * SL], xhatD[:, dc * SL:(dc + 1) * SL],
                    lnsD[:, dc * 8 + ex: dc * 8 + ex + 1], lnbD[:, dc * 8 + ex: dc * 8 + ex + 1],
                    ALU.mult, ALU.add),
                    reads=[xhatD, lnsD, lnbD], writes=[xe_])
        xeb_chain(0)
        for ex in range(8):
            xe_ = xebD if ex % 2 == 0 else xebD2
            mbk = bk[3] if ex % 2 == 0 else bk[0]
            E("tensor", lambda e, ex=ex, mbk=mbk: e.matmul(
                mbk[:, :], indbD[0:8, ex * 128:(ex + 1) * 128], maskbD[0:8, :],
                start=True, stop=True),
                reads=[indbD, maskbD], writes=[mbk])
            for hfc in range(2):
                ph = bk[1 + hfc]
                for dc in range(DC):
                    E("tensor", lambda e, ex=ex, hfc=hfc, dc=dc, ph=ph, xe_=xe_: e.matmul(
                        ph[:, :], w1e[:, dc * 2048 + ex * 256 + hfc * 128: dc * 2048 + ex * 256 + hfc * 128 + 128],
                        xe_[:, dc * SL:(dc + 1) * SL],
                        start=(dc == 0), stop=(dc == DC - 1)),
                        reads=[w1e, xe_], writes=[ph])
                if hfc == 0 and ex < 7:
                    xeb_chain(ex + 1)
                E("scalar", lambda e, ex=ex, hfc=hfc, ph=ph: e.activation(
                    hallD[:, (ex * 2 + hfc) * SL:(ex * 2 + hfc + 1) * SL], ph[:, :], AF.Gelu,
                    bias=b1sD[:, hfc * 8 + ex: hfc * 8 + ex + 1]),
                    reads=[ph, b1sD], writes=[hallD])
                E("vector", lambda e, ex=ex, hfc=hfc, mbk=mbk: e.tensor_mul(
                    hallD[:, (ex * 2 + hfc) * SL:(ex * 2 + hfc + 1) * SL],
                    hallD[:, (ex * 2 + hfc) * SL:(ex * 2 + hfc + 1) * SL], mbk[:, :]),
                    reads=[hallD, mbk], writes=[hallD])
        for fc in range(DC):
            py = bk[4 + fc % 2]
            for ex in range(8):
                for hfc in range(2):
                    E("tensor", lambda e, fc=fc, ex=ex, hfc=hfc, py=py: e.matmul(
                        py[:, :], w2e[:, hfc * 8 * D + ex * D + fc * 128: hfc * 8 * D + ex * D + fc * 128 + 128],
                        hallD[:, (ex * 2 + hfc) * SL:(ex * 2 + hfc + 1) * SL],
                        start=(ex == 0 and hfc == 0), stop=False),
                        reads=[w2e, hallD], writes=[py])
            E("tensor", lambda e, fc=fc, py=py: e.matmul(
                py[:, :], b2bD[0:8, fc * 128:(fc + 1) * 128], maskbD[0:8, :],
                start=False, stop=True),
                reads=[b2bD, maskbD], writes=[py])
            E("vector", lambda e, fc=fc, py=py: e.tensor_add(
                xresD[:, fc * SL:(fc + 1) * SL], py[:, :], xotD[:, fc * SL:(fc + 1) * SL]),
                reads=[py, xotD], writes=[xresD])
        cm.layernorm_nt(xresD, bk[6], bk[7], out_f32=xresD)
        dstv = xout.rearrange("(c p) x -> p c x", p=128)[:, :, tt * 512:(tt + 1) * 512]
        E("sync", lambda e, dstv=dstv: e.dma_start(
            out=dstv, in_=xresD[:, :].rearrange("p (c x) -> p c x", c=DC)),
            reads=[xresD], dma=True, tag="out")

    if DEBUG_F:
        deq = _out(nc, "dbg_eQ", [TOK, D], BF16)
        E("sync", lambda e: e.dma_start(out=deq, in_=eQ.ap()),
          reads=[(eQ, nt) for nt in range(NSLOT)], dma=True, tag="out")
        dgq = _out(nc, "dbg_gQ", [2 * TOK, D], BF16)
        E("sync", lambda e: e.dma_start(out=dgq, in_=gQ.ap()), reads=[gQ], dma=True, tag="out")
        for nm, t_ in (("dbg_x1", x1d), ("dbg_x2", x2d), ("dbg_x3", x3d)):
            o_ = _out(nc, nm, [D, TOK])
            E("sync", lambda e, o_=o_, t_=t_: e.dma_start(out=o_, in_=t_),
              reads=[(t_, nt) for nt in range(NSLOT)], dma=True, tag="out")
    outs = [op for op in em.ops if op.tag == "out"]
    em.emit("sync", lambda e: e.nop(), after=outs, tag="final")
    em.build("F")
    return nc


_NC_CACHE = {}


def _get_nc(name):
    if name not in _NC_CACHE:
        _NC_CACHE[name] = {"A": build_A, "B": build_B, "C1": build_C1,
                           "C2": build_C2, "D": build_D, "F": build_F}[name]()
    return _NC_CACHE[name]


def _core_slices(x, c):
    b, par = c // 2, c % 2
    return np.concatenate([x[b, 512 * p:512 * p + 512, :] for p in POS[par]], 0)


def _build_cmask(par):
    m = np.zeros((4, 8, 128, 512), np.float32)
    tri = (np.arange(512)[None, :] >= np.arange(512)[:, None]).astype(np.float32)
    for it in range(4):
        p = POS[par][it]
        for mi, g in enumerate(MSK_TILES[it]):
            q = GPOS[g]
            if q < p:
                m[it, mi * 4:(mi + 1) * 4] = 1.0
            elif q == p:
                for cl in range(4):
                    m[it, mi * 4 + cl] = tri[cl * 128:(cl + 1) * 128, :]
    return m.astype(ml_dtypes.bfloat16)


FUSED = True


def kernel(**inputs):
    if FUSED:
        return kernel_fused(**inputs)
    return kernel_unfused(**inputs)


def kernel_fused(L0_attn_x, L0_attn_W_qkv_v, L0_attn_W_qkv_u, L0_attn_W_o_v,
                 L0_attn_W_o_u, L0_ffn_w1, L0_ffn_b1, L0_ffn_w2, L0_ffn_b2,
                 L1_attn_wq, L1_attn_wk, L1_attn_wv, L1_ffn_Wg, L1_ffn_ln_scale,
                 L1_ffn_ln_bias, L1_ffn_W1, L1_ffn_b1, L1_ffn_W2, L1_ffn_b2):
    f32 = np.float32
    x = np.asarray(L0_attn_x, f32)
    cores = list(range(NCORE))
    Wg = np.asarray(L1_ffn_Wg, f32)
    W1 = np.asarray(L1_ffn_W1, f32)
    W2 = np.asarray(L1_ffn_W2, f32)
    ind = np.zeros((8, 1024), f32)
    for e in range(8):
        ind[e, e * 128:(e + 1) * 128] = 1.0
    shared = {
        "WvT": np.ascontiguousarray(np.asarray(L0_attn_W_qkv_v, f32).T),
        "WuT": np.ascontiguousarray(np.asarray(L0_attn_W_qkv_u, f32).T),
        "WovT": np.ascontiguousarray(np.asarray(L0_attn_W_o_v, f32).T),
        "WouT": np.ascontiguousarray(np.asarray(L0_attn_W_o_u, f32).T),
        "w1T": np.ascontiguousarray(np.asarray(L0_ffn_w1, f32).T),
        "w2T": np.ascontiguousarray(np.asarray(L0_ffn_w2, f32).T),
        "b1c": np.ascontiguousarray(np.asarray(L0_ffn_b1, f32).reshape(32, 128).T),
        "b2c": np.ascontiguousarray(np.asarray(L0_ffn_b2, f32).reshape(8, 128).T),
        "wqT": np.ascontiguousarray(np.asarray(L1_attn_wq, f32).T),
        "wkT": np.ascontiguousarray(np.asarray(L1_attn_wk, f32).T),
        "wvT": np.ascontiguousarray(np.asarray(L1_attn_wv, f32).T),
        "WgT": np.ascontiguousarray(Wg.T),
        "lns": np.ascontiguousarray(np.asarray(L1_ffn_ln_scale, f32).T),
        "lnb": np.ascontiguousarray(np.asarray(L1_ffn_ln_bias, f32).T),
        "W1f": np.ascontiguousarray(W1.transpose(1, 0, 2).reshape(1024, 8 * 256)),
        "b1f": np.ascontiguousarray(np.asarray(L1_ffn_b1, f32).T),
        "W2f": np.ascontiguousarray(W2.transpose(1, 0, 2).reshape(256, 8 * 1024)),
        "b2": np.ascontiguousarray(np.asarray(L1_ffn_b2, f32)),
        "id128": np.eye(128, dtype=f32),
        "ind": ind.astype(ml_dtypes.bfloat16),
    }
    in_maps = []
    for c in cores:
        own = _core_slices(x, c)
        peer = _core_slices(x, c ^ 1)
        xseq = np.ascontiguousarray(np.concatenate([own, peer], 0).T)
        in_maps.append({"xseq": xseq, "cmask": _build_cmask(c % 2), **shared})
    res = run_bass_kernel_spmd(_get_nc("F"), in_maps, core_ids=cores)
    global _LAST_RES
    _LAST_RES = res
    out = np.empty((B, S, D), f32)
    for c in cores:
        b, par = c // 2, c % 2
        got = np.asarray(res.results[c]["xout"]).T
        for si, p in enumerate(POS[par]):
            out[b, 512 * p:512 * p + 512, :] = got[si * 512:(si + 1) * 512, :]
    return out


def kernel_unfused(L0_attn_x, L0_attn_W_qkv_v, L0_attn_W_qkv_u, L0_attn_W_o_v,
           L0_attn_W_o_u, L0_ffn_w1, L0_ffn_b1, L0_ffn_w2, L0_ffn_b2,
           L1_attn_wq, L1_attn_wk, L1_attn_wv, L1_ffn_Wg, L1_ffn_ln_scale,
           L1_ffn_ln_bias, L1_ffn_W1, L1_ffn_b1, L1_ffn_W2, L1_ffn_b2):
    f32 = np.float32
    x = np.asarray(L0_attn_x, f32)
    cores = list(range(NCORE))

    WvT = np.ascontiguousarray(np.asarray(L0_attn_W_qkv_v, f32).T)
    WuT = np.ascontiguousarray(np.asarray(L0_attn_W_qkv_u, f32).T)
    WovT = np.ascontiguousarray(np.asarray(L0_attn_W_o_v, f32).T)
    WouT = np.ascontiguousarray(np.asarray(L0_attn_W_o_u, f32).T)
    in_maps = []
    for c in cores:
        own = _core_slices(x, c)
        peer = _core_slices(x, c ^ 1)
        xseq = np.ascontiguousarray(np.concatenate([own, peer], 0).T)
        in_maps.append({"xseq": xseq, "WvT": WvT, "WuT": WuT, "WovT": WovT, "WouT": WouT})
    rA = run_bass_kernel_spmd(_get_nc("A"), in_maps, core_ids=cores)
    x1c = [np.asarray(rA.results[c]["x1"]) for c in cores]

    w1T = np.ascontiguousarray(np.asarray(L0_ffn_w1, f32).T)
    w2T = np.ascontiguousarray(np.asarray(L0_ffn_w2, f32).T)
    b1c = np.ascontiguousarray(np.asarray(L0_ffn_b1, f32).reshape(32, 128).T)
    b2c = np.ascontiguousarray(np.asarray(L0_ffn_b2, f32).reshape(8, 128).T)
    in_maps = [{"x1": x1c[c], "w1T": w1T, "w2T": w2T, "b1c": b1c, "b2c": b2c}
               for c in cores]
    rB = run_bass_kernel_spmd(_get_nc("B"), in_maps, core_ids=cores)
    x2c = [np.asarray(rB.results[c]["x2"]) for c in cores]

    wqT = np.ascontiguousarray(np.asarray(L1_attn_wq, f32).T)
    wkT = np.ascontiguousarray(np.asarray(L1_attn_wk, f32).T)
    wvT = np.ascontiguousarray(np.asarray(L1_attn_wv, f32).T)
    in_maps = [{"x2": x2c[c], "wqT": wqT, "wkT": wkT, "wvT": wvT} for c in cores]
    rC1 = run_bass_kernel_spmd(_get_nc("C1"), in_maps, core_ids=cores)

    in_maps = []
    for c in cores:
        b, par = c // 2, c % 2
        e, o = 2 * b, 2 * b + 1
        Qf = np.concatenate([rC1.results[e]["Qo"], rC1.results[o]["Qo"]], axis=1)
        Vf = np.concatenate([rC1.results[e]["Vo"], rC1.results[o]["Vo"]], axis=0)
        q2f = np.concatenate([rC1.results[e]["q2o"], rC1.results[o]["q2o"]], axis=1)
        in_maps.append({"x2": x2c[c], "Ko": rC1.results[c]["Ko"], "Qf": Qf,
                        "Vf": Vf, "q2f": q2f, "cmask": _build_cmask(par)})
    rC2 = run_bass_kernel_spmd(_get_nc("C2"), in_maps, core_ids=cores)
    x3c = [np.asarray(rC2.results[c]["x3"]) for c in cores]

    Wg = np.asarray(L1_ffn_Wg, f32)
    W1 = np.asarray(L1_ffn_W1, f32)
    W2 = np.asarray(L1_ffn_W2, f32)
    ind = np.zeros((8, 1024), f32)
    for e in range(8):
        ind[e, e * 128:(e + 1) * 128] = 1.0
    shared = {
        "WgT": np.ascontiguousarray(Wg.T),
        "lns": np.ascontiguousarray(np.asarray(L1_ffn_ln_scale, f32).T),
        "lnb": np.ascontiguousarray(np.asarray(L1_ffn_ln_bias, f32).T),
        "W1f": np.ascontiguousarray(W1.transpose(1, 0, 2).reshape(1024, 8 * 256)),
        "b1f": np.ascontiguousarray(np.asarray(L1_ffn_b1, f32).T),
        "W2f": np.ascontiguousarray(W2.transpose(1, 0, 2).reshape(256, 8 * 1024)),
        "b2": np.ascontiguousarray(np.asarray(L1_ffn_b2, f32)),
        "id128": np.eye(128, dtype=f32),
        "ind": ind.astype(ml_dtypes.bfloat16),
    }
    in_maps = [{"x3": x3c[c], **shared} for c in cores]
    rD = run_bass_kernel_spmd(_get_nc("D"), in_maps, core_ids=cores)

    out = np.empty((B, S, D), f32)
    for c in cores:
        b, par = c // 2, c % 2
        got = np.asarray(rD.results[c]["xout"]).T
        for si, p in enumerate(POS[par]):
            out[b, 512 * p:512 * p + 512, :] = got[si * 512:(si + 1) * 512, :]
    return out


# version 20
# speedup vs baseline: 1.2022x; 1.0029x over previous
import numpy as np
import ml_dtypes
import concourse.bass as bass
import concourse.mybir as mybir
from concourse.bass_utils import run_bass_kernel_spmd

F32 = mybir.dt.float32
BF16 = mybir.dt.bfloat16
AF = mybir.ActivationFunctionType
ALU = mybir.AluOpType
AX = mybir.AxisListType

D = 1024
DC = 8
S = 4096
B = 4
TOK = 2048
NSLOT = 4
SL = 512
NCORE = 8
POS = [[0, 3, 4, 7], [1, 2, 5, 6]]
LN_EPS = 1e-5

ENGINES = ("tensor", "vector", "scalar", "gpsimd", "sync")
DEBUG_A = False
DEBUG_F = False
_LAST_RES = None


class _Op:
    __slots__ = ("eng", "seq", "fn", "reads", "writes", "is_dma", "needs_inc",
                 "waits", "inc_count", "tag", "dma_slot", "dma_use")

    def __init__(self, eng, seq, fn, reads, writes, is_dma, tag):
        self.eng = eng
        self.seq = seq
        self.fn = fn
        self.reads = reads
        self.writes = writes
        self.is_dma = is_dma
        self.needs_inc = False
        self.waits = {}
        self.inc_count = 0
        self.tag = tag
        self.dma_slot = -1
        self.dma_use = 0


def _rkey(r):
    sub = None
    if isinstance(r, tuple):
        r, sub = r
    h = getattr(r, "tensor", r)
    name = getattr(h, "name", None)
    if name is None:
        name = id(h)
    return (name, sub)


NDMASEM = 8


class Emitter:

    def __init__(self, nc):
        self.nc = nc
        self.ops = []
        self.last_writer = {}
        self.readers = {}
        self.per_eng = {e: [] for e in ENGINES}
        self.alias = {}
        self.dma_count = {e: 0 for e in ENGINES}

    def set_alias(self, tensors, group):
        for t in tensors:
            self.alias[getattr(t, "name", t)] = group

    def set_pages(self, tensor, base, nbytes):
        self.alias[getattr(tensor, "name", tensor)] = ("PAGES", base, nbytes)

    def _keys(self, r):
        name, sub = _rkey(r)
        g = self.alias.get(name)
        if g is None:
            return [(name, sub)]
        if isinstance(g, tuple) and g and g[0] == "PAGES":
            base, nb = g[1], g[2]
            if isinstance(sub, tuple) and sub and sub[0] == "B":
                lo, hi = base + sub[1], base + sub[2]
            else:
                lo, hi = base, base + nb
            return [("pg", p) for p in range(lo // PGSZ, (hi + PGSZ - 1) // PGSZ)]
        return [(g, None)]

    def emit(self, eng, fn, reads=(), writes=(), dma=False, after=(), tag=""):
        op = _Op(eng, len(self.ops), fn,
                 [k for r in reads for k in self._keys(r)],
                 [k for r in writes for k in self._keys(r)], dma, tag)
        if dma:
            i = self.dma_count[eng]
            self.dma_count[eng] = i + 1
            op.dma_slot = i % NDMASEM
            op.dma_use = i // NDMASEM + 1
        deps = list(after)
        for r in op.reads:
            w = self.last_writer.get(r)
            if w is not None:
                deps.append(w)
        for r in op.writes:
            w = self.last_writer.get(r)
            if w is not None:
                deps.append(w)
            deps.extend(self.readers.get(r, ()))
        for d in deps:
            if d is op:
                continue
            if d.eng == eng == "tensor" and not d.is_dma and not dma:
                continue
            ch = ("d", d.eng, d.dma_slot) if d.is_dma else ("e", d.eng)
            cur = op.waits.get(ch)
            if cur is None or d.seq > cur.seq:
                op.waits[ch] = d
        for r in op.writes:
            self.last_writer[r] = op
            self.readers[r] = []
        for r in op.reads:
            self.readers.setdefault(r, []).append(op)
        self.ops.append(op)
        self.per_eng[eng].append(op)
        return op

    def build(self, block_name=None):
        nc = self.nc
        for op in self.ops:
            for d in op.waits.values():
                if not d.is_dma:
                    d.needs_inc = True
        sem_final = {}
        for e in ENGINES:
            c = 0
            for op in self.per_eng[e]:
                if op.needs_inc and not op.is_dma:
                    c += 1
                op.inc_count = c
            sem_final[e] = c
        sems = {e: nc.alloc_semaphore(f"em_{e}") for e in ENGINES}
        dsems = {e: [nc.alloc_semaphore(f"dm_{e}{i}") for i in range(NDMASEM)]
                 for e in ENGINES if self.dma_count[e] > 0}

        def _resolve(ch, dop):
            if ch[0] == "d":
                return dsems[ch[1]][ch[2]], 16 * dop.dma_use
            return sems[ch[1]], dop.inc_count

        def runner(e):
            def run(engine):
                clock = {}
                for op in self.per_eng[e]:
                    if op.is_dma and op.dma_use > 1:
                        ch = ("d", e, op.dma_slot)
                        v = 16 * (op.dma_use - 1)
                        if clock.get(ch, 0) < v:
                            engine.wait_ge(dsems[e][op.dma_slot], v)
                            clock[ch] = v
                    for ch in sorted(op.waits, key=str):
                        sem, v = _resolve(ch, op.waits[ch])
                        if v > clock.get(ch, 0):
                            engine.wait_ge(sem, v)
                            clock[ch] = v
                    inst = op.fn(engine)
                    if op.is_dma:
                        inst.then_inc(dsems[e][op.dma_slot], 16)
                    elif op.needs_inc:
                        inst.then_inc(sems[e], 1)
            return run

        with nc.Block(block_name) as blk:
            blk.tensor(runner("tensor"))
            blk.vector(runner("vector"))
            blk.scalar(runner("scalar"))
            blk.gpsimd(runner("gpsimd"))
            blk.sync(runner("sync"))
        return sems, sem_final


def _mk_nc():
    return bass.Bass("TRN2", target_bir_lowering=False)


def _in(nc, name, shape, dtype=F32):
    return nc.dram_tensor(name, shape, dtype, kind="ExternalInput").ap()


def _out(nc, name, shape, dtype=F32):
    return nc.dram_tensor(name, shape, dtype, kind="ExternalOutput").ap()


def load_fm(em, dst, src, chunks, width, eng="sync", tag="ld"):
    ap = src.rearrange("(c p) x -> p c x", p=128)
    dv = dst[:, :].rearrange("p (c x) -> p c x", c=chunks)
    em.emit(eng, lambda e: e.dma_start(out=dv, in_=ap), writes=[dst], dma=True, tag=tag)


def store_fm(em, dst, src_sb, chunks, width, col0=0, eng="sync", tag="st", reads=()):
    ap = dst.rearrange("(c p) x -> p c x", p=128)
    sv = src_sb[:, col0:col0 + chunks * width].rearrange("p (c x) -> p c x", c=chunks)
    return em.emit(eng, lambda e: e.dma_start(out=ap, in_=sv),
                   reads=list(reads) or [src_sb], dma=True, tag=tag)


class Common:

    def __init__(self, nc, em):
        self.nc, self.em = nc, em
        self.ones_mean = nc.alloc_sbuf_tensor("ones_mean", [128, 1], BF16)
        self.ones_row = nc.alloc_sbuf_tensor("ones_row", [1, 128], F32)
        self.ones_rowb = nc.alloc_sbuf_tensor("ones_rowb", [1, 128], BF16)
        self.ones_colb = nc.alloc_sbuf_tensor("ones_colb", [128, 1], BF16)
        em.emit("vector", lambda e: e.memset(self.ones_mean[:, :], 1.0 / D),
                writes=[self.ones_mean])
        em.emit("vector", lambda e: e.memset(self.ones_row[:, :], 1.0),
                writes=[self.ones_row])
        em.emit("vector", lambda e: e.memset(self.ones_rowb[:, :], 1.0),
                writes=[self.ones_rowb])
        em.emit("vector", lambda e: e.memset(self.ones_colb[:, :], 1.0),
                writes=[self.ones_colb])
        self.eps = nc.alloc_sbuf_tensor("eps_c", [1, 1], F32)
        em.emit("vector", lambda e: e.memset(self.eps[:, :], float(LN_EPS)),
                writes=[self.eps])
        self.xrb = self.x2b = self.rows = None
        self.Ab = nc.alloc_sbuf_tensor("ln_Ab", [128, SL], F32)
        self.Bb = nc.alloc_sbuf_tensor("ln_Bb", [128, SL], F32)
        self.tmp = nc.alloc_sbuf_tensor("ln_tmp", [128, SL], F32)

    def attach_ln_scratch(self, xrb, x2b, rows):
        self.xrb, self.x2b, self.rows = xrb, x2b, rows

    def layernorm_nt(self, xres, bk_stat, bk_bcast, out_f32=None, out_bf16=None):
        em = self.em
        xrb, x2b = self.xrb, self.x2b
        R = self.rows
        mrow, erow = R[0:1, 0:SL], R[0:1, SL:2 * SL]
        m2 = R[0:1, 2 * SL:3 * SL]
        var = R[0:1, SL:2 * SL]
        lnv = R[0:1, 2 * SL:3 * SL]
        rstd = R[0:1, SL:2 * SL]
        mr = negmr = R[0:1, 2 * SL:3 * SL]
        em.emit("scalar", lambda e: e.activation(x2b[:, 0:DC * SL], xres[:, 0:DC * SL], AF.Square),
                reads=[xres], writes=[x2b])
        for fc in range(DC):
            sl = slice(fc * SL, (fc + 1) * SL)
            bsl = ("B", fc * SL * 2, (fc + 1) * SL * 2)
            em.emit("vector", lambda e, sl=sl: e.tensor_copy(xrb[:, sl], xres[:, sl]),
                    reads=[xres], writes=[(xrb, bsl)])
            em.emit("tensor", lambda e, sl=sl, fc=fc: e.matmul(
                bk_stat[0:1, :], self.ones_mean[:, :], xrb[:, sl],
                start=(fc == 0), stop=(fc == DC - 1)),
                reads=[(xrb, bsl), self.ones_mean], writes=[bk_stat])
            em.emit("tensor", lambda e, sl=sl, fc=fc: e.matmul(
                bk_bcast[0:1, :], self.ones_mean[:, :], x2b[:, sl],
                start=(fc == 0), stop=(fc == DC - 1)),
                reads=[(x2b, bsl), self.ones_mean], writes=[bk_bcast])
        em.emit("vector", lambda e: e.tensor_copy(mrow, bk_stat[0:1, :]),
                reads=[bk_stat], writes=[R])
        em.emit("vector", lambda e: e.tensor_copy(erow, bk_bcast[0:1, :]),
                reads=[bk_bcast], writes=[R])
        em.emit("vector", lambda e: e.tensor_mul(m2, mrow, mrow), reads=[R], writes=[R])
        em.emit("vector", lambda e: e.tensor_sub(var, erow, m2), reads=[R], writes=[R])
        em.emit("scalar", lambda e: e.activation(lnv, var, AF.Ln, bias=self.eps[0:1, 0:1]),
                reads=[R, self.eps], writes=[R])
        em.emit("scalar", lambda e: e.activation(rstd, lnv, AF.Exp, scale=-0.5),
                reads=[R], writes=[R])
        em.emit("vector", lambda e: e.tensor_mul(mr, mrow, rstd), reads=[R], writes=[R])
        em.emit("vector", lambda e: e.tensor_scalar(negmr, mr, -1.0, None, ALU.mult),
                reads=[R], writes=[R])
        em.emit("tensor", lambda e: e.matmul(bk_stat[:, :], self.ones_row[:, :], rstd,
                                             start=True, stop=True),
                reads=[R, self.ones_row], writes=[bk_stat])
        em.emit("vector", lambda e: e.tensor_copy(self.Ab[:, :], bk_stat[:, :]),
                reads=[bk_stat], writes=[self.Ab])
        em.emit("tensor", lambda e: e.matmul(bk_bcast[:, :], self.ones_row[:, :], negmr,
                                             start=True, stop=True),
                reads=[R, self.ones_row], writes=[bk_bcast])
        em.emit("vector", lambda e: e.tensor_copy(self.Bb[:, :], bk_bcast[:, :]),
                reads=[bk_bcast], writes=[self.Bb])
        for fc in range(DC):
            sl = slice(fc * SL, (fc + 1) * SL)
            em.emit("vector", lambda e, sl=sl: e.tensor_mul(self.tmp[:, :], xres[:, sl], self.Ab[:, :]),
                    reads=[xres, self.Ab], writes=[self.tmp])
            if out_f32 is not None:
                em.emit("vector", lambda e, sl=sl: e.tensor_add(out_f32[:, sl], self.tmp[:, :], self.Bb[:, :]),
                        reads=[self.tmp, self.Bb], writes=[out_f32])
                if out_bf16 is not None:
                    em.emit("scalar", lambda e, sl=sl: e.activation(out_bf16[:, sl], out_f32[:, sl], AF.Copy),
                            reads=[out_f32], writes=[out_bf16])
            elif out_bf16 is not None:
                em.emit("vector", lambda e, sl=sl: e.tensor_add(out_bf16[:, sl], self.tmp[:, :], self.Bb[:, :]),
                        reads=[self.tmp, self.Bb], writes=[out_bf16])


def build_A():
    nc = _mk_nc()
    xseq = _in(nc, "xseq", [D, S])
    WvT = _in(nc, "WvT", [D, 256])
    WuT = _in(nc, "WuT", [256, 3 * D])
    WovT = _in(nc, "WovT", [D, 256])
    WouT = _in(nc, "WouT", [256, D])
    x1 = _out(nc, "x1", [D, TOK])

    em = Emitter(nc)
    cm = Common(nc, em)

    xb = nc.alloc_sbuf_tensor("xb", [128, DC * S], BF16)
    xb_off = nc.lookup_mloc(xb).addr
    xres = nc.alloc_sbuf_tensor_at("xres", [128, DC * SL], F32, offset=xb_off)
    xot = nc.alloc_sbuf_tensor_at("xot", [128, DC * SL], F32, offset=xb_off + 16384)
    xrb = nc.alloc_sbuf_tensor_at("ln_xrb", [128, DC * SL], BF16, offset=xb_off + 32768)
    x2b = nc.alloc_sbuf_tensor_at("ln_x2b", [128, DC * SL], BF16, offset=xb_off + 40960)
    lrows = nc.alloc_sbuf_tensor_at("ln_rows", [1, 4 * SL], F32, offset=xb_off + 49152)
    em.set_alias([xb, xres, xot, xrb, x2b, lrows], "R1")
    cm.attach_ln_scratch(xrb, x2b, lrows)

    wv = nc.alloc_sbuf_tensor("wv", [128, DC * 256], BF16)
    wu = nc.alloc_sbuf_tensor("wu", [128, 2 * 3 * D], BF16)
    wov = nc.alloc_sbuf_tensor("wov", [128, DC * 256], BF16)
    wou = nc.alloc_sbuf_tensor("wou", [128, 2 * D], BF16)
    t = nc.alloc_sbuf_tensor("t", [128, 2 * S], BF16)
    Qg = nc.alloc_sbuf_tensor("Qg", [128, S], BF16)
    Kg = nc.alloc_sbuf_tensor("Kg", [128, TOK], BF16)
    Vg = nc.alloc_sbuf_tensor("Vg", [128, 32 * 130], BF16)
    y = nc.alloc_sbuf_tensor("y", [128, DC * TOK], BF16)
    z1 = nc.alloc_sbuf_tensor("z1", [128, 2 * TOK], BF16)
    Pp = nc.alloc_sbuf_tensor("Pp", [128, 4 * SL], BF16)
    sq = nc.alloc_sbuf_tensor("sq", [128, S], BF16)
    r_sc = nc.alloc_sbuf_tensor("r_sc", [1, 528], F32)
    negc = nc.alloc_sbuf_tensor("negc", [128, 1], F32)
    rl = nc.alloc_sbuf_tensor("rl", [1, SL], F32)
    rlb = nc.alloc_sbuf_tensor("rlb", [64, SL], F32)

    bk = [nc.alloc_psum_tensor(f"bk{i}", [128, 512], F32) for i in range(8)]

    E = em.emit

    load_fm(em, xb, xseq, DC, S, eng="gpsimd")
    load_fm(em, wv, WvT, DC, 256, eng="gpsimd")
    load_fm(em, wu, WuT, 2, 3 * D, eng="gpsimd")
    load_fm(em, wov, WovT, DC, 256, eng="gpsimd")
    load_fm(em, wou, WouT, 2, D, eng="gpsimd")

    for rc in range(2):
        for ntile in range(S // 512):
            pb = bk[ntile % 2]
            for dc in range(DC):
                E("tensor", lambda e, rc=rc, nt=ntile, dc=dc, pb=pb: e.matmul(
                    pb[:, :], wv[:, dc * 256 + rc * 128: dc * 256 + rc * 128 + 128],
                    xb[:, dc * S + nt * 512: dc * S + nt * 512 + 512],
                    start=(dc == 0), stop=(dc == DC - 1)),
                    reads=[wv, xb], writes=[pb])
            E("scalar", lambda e, rc=rc, nt=ntile, pb=pb: e.activation(
                t[:, rc * S + nt * 512: rc * S + nt * 512 + 512], pb[:, :], AF.Copy),
                reads=[pb], writes=[(t, (rc, ntile))])

    for g in range(8):
        fq = D + g * 128
        fk = g * 128
        fv = 2 * D + g * 128
        for ntile in range(S // 512):
            pb = bk[ntile % 2]
            for rc in range(2):
                E("tensor", lambda e, nt=ntile, rc=rc, pb=pb, fq=fq: e.matmul(
                    pb[:, :], wu[:, rc * 3 * D + fq: rc * 3 * D + fq + 128],
                    t[:, rc * S + nt * 512: rc * S + nt * 512 + 512],
                    start=(rc == 0), stop=(rc == 1)),
                    reads=[wu, (t, (rc, ntile))], writes=[pb])
            E("scalar", lambda e, nt=ntile, pb=pb: e.activation(
                Qg[:, nt * 512: nt * 512 + 512], pb[:, :], AF.Copy),
                reads=[pb], writes=[(Qg, ntile)])
        for ntile in range(TOK // 512):
            pb = bk[2 + ntile % 2]
            for rc in range(2):
                E("tensor", lambda e, nt=ntile, rc=rc, pb=pb, fk=fk: e.matmul(
                    pb[:, :], wu[:, rc * 3 * D + fk: rc * 3 * D + fk + 128],
                    t[:, rc * S + nt * 512: rc * S + nt * 512 + 512],
                    start=(rc == 0), stop=(rc == 1)),
                    reads=[wu, (t, (rc, ntile))], writes=[pb])
            E("scalar", lambda e, nt=ntile, pb=pb: e.activation(
                Kg[:, nt * 512: nt * 512 + 512], pb[:, :], AF.Copy),
                reads=[pb], writes=[(Kg, ntile)])
        for jc in range(32):
            pb = bk[4 + jc % 2]
            for rc in range(2):
                E("tensor", lambda e, jc=jc, rc=rc, pb=pb, fv=fv: e.matmul(
                    pb[:, 0:128], t[:, rc * S + jc * 128: rc * S + jc * 128 + 128],
                    wu[:, rc * 3 * D + fv: rc * 3 * D + fv + 128],
                    start=(rc == 0), stop=(rc == 1)),
                    reads=[wu, (t, (rc, jc // 4))], writes=[pb])
            for h in range(2):
                E("vector", lambda e, jc=jc, h=h, pb=pb: e.tensor_copy(
                    Vg[:, jc * 130 + h * 65: jc * 130 + h * 65 + 64],
                    pb[:, h * 64: h * 64 + 64]),
                    reads=[pb], writes=[(Vg, jc)])
        E("vector", lambda e: e.memset(
            Vg[:, :].rearrange("p (j h c) -> p j h c", j=32, h=2)[:, :, :, 64:65], 1.0),
            writes=[(Vg, "ones")])

        scr = r_sc[0:1, 0:512]
        mk2 = r_sc[0:1, 512:513]
        mq2 = r_sc[0:1, 513:514]
        prd = r_sc[0:1, 514:515]
        sqr = r_sc[0:1, 515:516]
        ngc = r_sc[0:1, 516:517]
        cur = r_sc[0:1, 517:518]
        E("vector", lambda e: e.memset(mk2, 0.0), writes=[(r_sc, "mk2")])
        E("vector", lambda e: e.memset(mq2, 0.0), writes=[(r_sc, "mq2")])
        for src_, nmax, mrow, mkey in ((Kg, TOK // 512, mk2, "mk2"), (Qg, S // 512, mq2, "mq2")):
            for ntile in range(nmax):
                E("scalar", lambda e, nt=ntile, src_=src_: e.activation(
                    sq[:, nt * 512: nt * 512 + 512], src_[:, nt * 512: nt * 512 + 512], AF.Square),
                    reads=[(src_, ntile)], writes=[(sq, ntile)])
            for ntile in range(nmax):
                pb = bk[6]
                E("tensor", lambda e, nt=ntile, pb=pb: e.matmul(
                    pb[0:1, :], cm.ones_colb[:, :], sq[:, nt * 512: nt * 512 + 512],
                    start=True, stop=True),
                    reads=[(sq, ntile), cm.ones_colb], writes=[pb])
                E("vector", lambda e, pb=pb: e.tensor_copy(scr, pb[0:1, :]),
                    reads=[pb], writes=[(r_sc, "scr")])
                E("vector", lambda e: e.reduce_max(cur, scr, axis=AX.X),
                    reads=[(r_sc, "scr")], writes=[(r_sc, "cur")])
                E("vector", lambda e, mrow=mrow, mkey=mkey: e.tensor_max(mrow, mrow, cur),
                    reads=[(r_sc, "cur"), (r_sc, mkey)], writes=[(r_sc, mkey)])
        E("vector", lambda e: e.tensor_mul(prd, mk2, mq2),
          reads=[(r_sc, "mk2"), (r_sc, "mq2")], writes=[(r_sc, "prd")])
        E("scalar", lambda e: e.activation(sqr, prd, AF.Sqrt),
          reads=[(r_sc, "prd")], writes=[(r_sc, "sqr")])
        E("vector", lambda e: e.tensor_scalar(ngc, sqr, -0.125, None, ALU.mult),
          reads=[(r_sc, "sqr")], writes=[(r_sc, "ngc")])
        pb = bk[6]
        E("tensor", lambda e, pb=pb: e.matmul(pb[:, 0:1], cm.ones_row[:, :], ngc,
                                              start=True, stop=True),
          reads=[(r_sc, "ngc"), cm.ones_row], writes=[pb])
        E("vector", lambda e, pb=pb: e.tensor_copy(negc[:, :], pb[:, 0:1]),
          reads=[pb], writes=[negc])

        for it in range(NSLOT):
            nb = [bk[4], bk[5]]
            for h in range(2):
                pbase = h * 64
                nbank = nb[h]
                for jc in range(32):
                    sb_ = bk[2 * h + (jc % 2)]
                    E("tensor", lambda e, jc=jc, it=it, pbase=pbase, sb_=sb_: e.matmul(
                        sb_[:, :], Qg[pbase:pbase + 64, jc * 128: jc * 128 + 128],
                        Kg[pbase:pbase + 64, it * 512: it * 512 + 512],
                        start=True, stop=True),
                        reads=[(Qg, jc // 4), (Kg, it)], writes=[sb_])
                    psl = slice((2 * h + (jc % 2)) * SL, (2 * h + (jc % 2)) * SL + SL)
                    E("scalar", lambda e, psl=psl, sb_=sb_: e.activation(
                        Pp[:, psl], sb_[:, :], AF.Exp, bias=negc[:, 0:1], scale=0.125),
                        reads=[sb_, negc], writes=[(Pp, psl.start)])
                    E("tensor", lambda e, jc=jc, h=h, psl=psl, nbank=nbank: e.matmul(
                        nbank[0:65, :], Vg[:, jc * 130 + h * 65: jc * 130 + h * 65 + 65],
                        Pp[:, psl], start=(jc == 0), stop=(jc == 31)),
                        reads=[(Vg, jc), (Vg, "ones"), (Pp, psl.start)], writes=[nbank])
            for h in range(2):
                nbank = nb[h]
                E("vector", lambda e, nbank=nbank: e.reciprocal(rl[0:1, :], nbank[64:65, :]),
                  reads=[nbank], writes=[rl])
                pb2 = bk[6]
                E("tensor", lambda e, pb2=pb2: e.matmul(pb2[0:64, :], cm.ones_row[0:1, 0:64], rl[0:1, :],
                                                        start=True, stop=True),
                  reads=[rl, cm.ones_row], writes=[pb2])
                E("vector", lambda e, pb2=pb2: e.tensor_copy(rlb[:, :], pb2[0:64, :]),
                  reads=[pb2], writes=[rlb])
                hh = 2 * g + h
                fc_, pb_ = hh // 2, (hh % 2) * 64
                E("vector", lambda e, it=it, fc_=fc_, pb_=pb_, nbank=nbank: e.tensor_mul(
                    y[pb_:pb_ + 64, fc_ * TOK + it * 512: fc_ * TOK + it * 512 + 512],
                    nbank[0:64, :], rlb[:, :]),
                    reads=[nbank, rlb], writes=[(y, (fc_, pb_, it))])

    for rc in range(2):
        for ntile in range(NSLOT):
            pb = bk[ntile % 2]
            for dc in range(DC):
                E("tensor", lambda e, rc=rc, nt=ntile, dc=dc, pb=pb: e.matmul(
                    pb[:, :], wov[:, dc * 256 + rc * 128: dc * 256 + rc * 128 + 128],
                    y[:, dc * TOK + nt * 512: dc * TOK + nt * 512 + 512],
                    start=(dc == 0), stop=(dc == DC - 1)),
                    reads=[wov, (y, (dc, 0, ntile)), (y, (dc, 64, ntile))], writes=[pb])
            E("scalar", lambda e, rc=rc, nt=ntile, pb=pb: e.activation(
                z1[:, rc * TOK + nt * 512: rc * TOK + nt * 512 + 512], pb[:, :], AF.Copy),
                reads=[pb], writes=[(z1, (rc, ntile))])

    for ntile in range(NSLOT):
        xsl = xseq.rearrange("(c p) x -> p c x", p=128)[:, :, ntile * 512:(ntile + 1) * 512]
        E("sync", lambda e, xsl=xsl: e.dma_start(
            out=xot[:, :].rearrange("p (c x) -> p c x", c=DC), in_=xsl),
            writes=[xot], dma=True)
        for fc in range(DC):
            pb = bk[2 + fc % 2]
            for rc in range(2):
                E("tensor", lambda e, fc=fc, nt=ntile, rc=rc, pb=pb: e.matmul(
                    pb[:, :], wou[:, rc * D + fc * 128: rc * D + fc * 128 + 128],
                    z1[:, rc * TOK + nt * 512: rc * TOK + nt * 512 + 512],
                    start=(rc == 0), stop=(rc == 1)),
                    reads=[wou, (z1, (rc, ntile))], writes=[pb])
            E("vector", lambda e, fc=fc, pb=pb: e.tensor_add(
                xres[:, fc * SL:(fc + 1) * SL], pb[:, :], xot[:, fc * SL:(fc + 1) * SL]),
                reads=[pb, xot], writes=[xres])
        cm.layernorm_nt(xres, bk[6], bk[7], out_f32=xres)
        dstv = x1.rearrange("(c p) x -> p c x", p=128)[:, :, ntile * 512:(ntile + 1) * 512]
        srcv = xres[:, :].rearrange("p (c x) -> p c x", c=DC)
        E("sync", lambda e, dstv=dstv, srcv=srcv: e.dma_start(out=dstv, in_=srcv),
          reads=[xres], dma=True, tag="out")

    if DEBUG_A:
        dt_ = _out(nc, "dbg_t", [2 * 128, S], BF16)
        dy = _out(nc, "dbg_y", [D, TOK], BF16)
        dz1 = _out(nc, "dbg_z1", [2 * 128, TOK], BF16)
        store_fm(em, dt_, t, 2, S, eng="sync", tag="out", reads=[(t, (rc_, nt_)) for rc_ in range(2) for nt_ in range(8)])
        store_fm(em, dy, y, DC, TOK, eng="sync", tag="out",
                 reads=[(y, (fc_, pb_, it_)) for fc_ in range(DC) for pb_ in (0, 64) for it_ in range(4)])
        store_fm(em, dz1, z1, 2, TOK, eng="sync", tag="out",
                 reads=[(z1, (rc_, nt_)) for rc_ in range(2) for nt_ in range(4)])
    outs = [op for op in em.ops if op.tag == "out"]
    em.emit("sync", lambda e: e.nop(), after=outs, tag="final")
    em.build("A")
    return nc


def build_B():
    nc = _mk_nc()
    x1 = _in(nc, "x1", [D, TOK])
    w1T = _in(nc, "w1T", [D, 4096])
    w2T = _in(nc, "w2T", [4096, D])
    b1c = _in(nc, "b1c", [128, 32])
    b2c = _in(nc, "b2c", [128, 8])
    x2 = _out(nc, "x2", [D, TOK])
    s1s = nc.dram_tensor("s1s", [32, 128, 1024], BF16, kind="Internal").ap()
    s2s = nc.dram_tensor("s2s", [4096, D], BF16, kind="Internal").ap()

    em = Emitter(nc)
    cm = Common(nc, em)
    xrb = nc.alloc_sbuf_tensor("ln_xrb", [128, DC * SL], BF16)
    x2b = nc.alloc_sbuf_tensor("ln_x2b", [128, DC * SL], BF16)
    lrows = nc.alloc_sbuf_tensor("ln_rows", [1, 3 * SL], F32)
    cm.attach_ln_scratch(xrb, x2b, lrows)

    xsb = nc.alloc_sbuf_tensor("xsb", [128, DC * TOK], BF16)
    h = nc.alloc_sbuf_tensor("h", [128, 32 * SL], BF16)
    xot = nc.alloc_sbuf_tensor("xot", [128, DC * SL], F32)
    xres = nc.alloc_sbuf_tensor("xres", [128, DC * SL], F32)
    wblk = nc.alloc_sbuf_tensor("wblk", [128, 1024], F32)
    sgn = nc.alloc_sbuf_tensor("sgn", [128, 1024], BF16)
    ab = nc.alloc_sbuf_tensor("ab", [128, 1024], BF16)
    s1buf = [nc.alloc_sbuf_tensor(f"s1b{i}", [128, 1024], BF16) for i in range(2)]
    s2buf = [nc.alloc_sbuf_tensor(f"s2b{i}", [128, 1024], BF16) for i in range(2)]
    a1c = nc.alloc_sbuf_tensor("a1c", [128, 32], F32)
    a2c = nc.alloc_sbuf_tensor("a2c", [128, 8], F32)
    b1s = nc.alloc_sbuf_tensor("b1s", [128, 32], F32)
    b2s = nc.alloc_sbuf_tensor("b2s", [128, 8], F32)
    ot = nc.alloc_sbuf_tensor("ot", [128, SL], F32)
    ones4k = nc.alloc_sbuf_tensor("ones4k", [128, 1], BF16)
    bk = [nc.alloc_psum_tensor(f"bk{i}", [128, 512], F32) for i in range(8)]
    E = em.emit

    E("vector", lambda e: e.memset(ones4k[:, :], 1.0 / 4096.0), writes=[ones4k])
    E("sync", lambda e: e.dma_start(out=b1s[:, :], in_=b1c), writes=[b1s], dma=True)
    E("sync", lambda e: e.dma_start(out=b2s[:, :], in_=b2c), writes=[b2s], dma=True)
    load_fm(em, xsb, x1, DC, TOK, eng="gpsimd")

    for hfc in range(32):
        wv_ = w1T.rearrange("(dc p) c -> p dc c", p=128)[:, :, hfc * 128:(hfc + 1) * 128]
        E("sync", lambda e, wv_=wv_: e.dma_start(
            out=wblk[:, :].rearrange("p (dc c) -> p dc c", dc=8), in_=wv_),
            writes=[wblk], dma=True)
        E("scalar", lambda e: e.activation(sgn[:, :], wblk[:, :], AF.Sign),
          reads=[wblk], writes=[sgn])
        E("scalar", lambda e: e.activation(ab[:, :], wblk[:, :], AF.Abs),
          reads=[wblk], writes=[ab])
        E("gpsimd", lambda e, hfc=hfc: e.dma_start(out=s1s[hfc], in_=sgn[:, :]),
          reads=[sgn], writes=[(s1s, hfc)], dma=True)
        pb = bk[0]
        for dc in range(DC):
            E("tensor", lambda e, dc=dc, pb=pb: e.matmul(
                pb[:, 0:1], ab[:, dc * 128:(dc + 1) * 128], cm.ones_mean[:, :],
                start=(dc == 0), stop=(dc == DC - 1)),
                reads=[ab, cm.ones_mean], writes=[pb])
        E("vector", lambda e, hfc=hfc, pb=pb: e.tensor_copy(a1c[:, hfc:hfc + 1], pb[:, 0:1]),
          reads=[pb], writes=[a1c])

    pa = bk[1]
    E("vector", lambda e: e.memset(a2c[:, :], 0.0), writes=[a2c])
    for hfc in range(32):
        wv_ = w2T[hfc * 128:(hfc + 1) * 128, :]
        E("sync", lambda e, wv_=wv_: e.dma_start(out=wblk[:, :], in_=wv_),
          writes=[wblk], dma=True)
        E("scalar", lambda e: e.activation(sgn[:, :], wblk[:, :], AF.Sign),
          reads=[wblk], writes=[sgn])
        E("scalar", lambda e: e.activation(ab[:, :], wblk[:, :], AF.Abs),
          reads=[wblk], writes=[ab])
        E("gpsimd", lambda e, hfc=hfc: e.dma_start(out=s2s[hfc * 128:(hfc + 1) * 128, :], in_=sgn[:, :]),
          reads=[sgn], writes=[(s2s, hfc)], dma=True)
        for fc in range(DC):
            E("tensor", lambda e, fc=fc, pa=pa: e.matmul(
                pa[:, fc:fc + 1], ab[:, fc * 128:(fc + 1) * 128], ones4k[:, :],
                start=True, stop=True),
                reads=[ab, ones4k], writes=[pa])
        E("vector", lambda e, pa=pa: e.tensor_add(a2c[:, :], a2c[:, :], pa[:, 0:8]),
          reads=[pa, a2c], writes=[a2c])

    for st in range(4):
        xsl = x1.rearrange("(c p) x -> p c x", p=128)[:, :, st * 512:(st + 1) * 512]
        E("sync", lambda e, xsl=xsl: e.dma_start(
            out=xot[:, :].rearrange("p (c x) -> p c x", c=DC), in_=xsl),
            writes=[xot], dma=True)
        for hfc in range(32):
            sb_ = s1buf[hfc % 4]
            E("sync", lambda e, hfc=hfc, sb_=sb_: e.dma_start(out=sb_[:, :], in_=s1s[hfc]),
              reads=[(s1s, hfc)], writes=[sb_], dma=True)
            pb = bk[hfc % 2]
            for dc in range(DC):
                E("tensor", lambda e, dc=dc, st=st, sb_=sb_, pb=pb: e.matmul(
                    pb[:, :], sb_[:, dc * 128:(dc + 1) * 128],
                    xsb[:, dc * TOK + st * 512: dc * TOK + st * 512 + 512],
                    start=(dc == 0), stop=(dc == DC - 1)),
                    reads=[sb_, xsb], writes=[pb])
            E("scalar", lambda e, hfc=hfc, pb=pb: e.activation(
                h[:, hfc * SL:(hfc + 1) * SL], pb[:, :], AF.Gelu,
                bias=b1s[:, hfc:hfc + 1], scale=a1c[:, hfc:hfc + 1]),
                reads=[pb, b1s, a1c], writes=[(h, hfc)])
        for hfc in range(32):
            sb_ = s2buf[hfc % 4]
            E("scalar", lambda e, hfc=hfc, sb_=sb_: e.dma_start(
                out=sb_[:, :], in_=s2s[hfc * 128:(hfc + 1) * 128, :]),
                reads=[(s2s, hfc)], writes=[sb_], dma=True)
            for fc in range(DC):
                E("tensor", lambda e, fc=fc, hfc=hfc, sb_=sb_: e.matmul(
                    bk[fc][:, :], sb_[:, fc * 128:(fc + 1) * 128],
                    h[:, hfc * SL:(hfc + 1) * SL],
                    start=(hfc == 0), stop=(hfc == 31)),
                    reads=[sb_, (h, hfc)], writes=[bk[fc]])
        for fc in range(DC):
            E("vector", lambda e, fc=fc: e.tensor_scalar(
                ot[:, :], bk[fc][:, :], a2c[:, fc:fc + 1], b2s[:, fc:fc + 1],
                ALU.mult, ALU.add),
                reads=[bk[fc], b2s, a2c], writes=[ot])
            E("vector", lambda e, fc=fc: e.tensor_add(
                xres[:, fc * SL:(fc + 1) * SL], ot[:, :], xot[:, fc * SL:(fc + 1) * SL]),
                reads=[ot, xot], writes=[xres])
        cm.layernorm_nt(xres, bk[0], bk[1], out_f32=xres)
        dstv = x2.rearrange("(c p) x -> p c x", p=128)[:, :, st * 512:(st + 1) * 512]
        srcv = xres[:, :].rearrange("p (c x) -> p c x", c=DC)
        E("sync", lambda e, dstv=dstv, srcv=srcv: e.dma_start(out=dstv, in_=srcv),
          reads=[xres], dma=True, tag="out")

    outs = [op for op in em.ops if op.tag == "out"]
    em.emit("sync", lambda e: e.nop(), after=outs, tag="final")
    em.build("B")
    return nc


GPOS = POS[0] + POS[1]
INC_TILES = [list(range(0, it + 1)) + list(range(4, 4 + it + 1)) for it in range(4)]
MSK_TILES = [[it, 4 + it] for it in range(4)]


def build_C1():
    nc = _mk_nc()
    x2 = _in(nc, "x2", [D, TOK])
    wqT = _in(nc, "wqT", [D, D])
    wkT = _in(nc, "wkT", [D, D])
    wvT = _in(nc, "wvT", [D, D])
    Qo = _out(nc, "Qo", [D, TOK])
    Ko = _out(nc, "Ko", [D, TOK])
    Vo = _out(nc, "Vo", [TOK, D])
    q2o = _out(nc, "q2o", [1, TOK])

    em = Emitter(nc)
    cm = Common(nc, em)
    xsb = nc.alloc_sbuf_tensor("xsb", [128, DC * TOK], BF16)
    wT = nc.alloc_sbuf_tensor("wT", [128, DC * D], BF16)
    qt = nc.alloc_sbuf_tensor("qt", [128, DC * SL], F32)
    sqv = nc.alloc_sbuf_tensor("sqv", [128, SL], BF16)
    q2r = nc.alloc_sbuf_tensor("q2r", [1, SL], F32)
    bk = [nc.alloc_psum_tensor(f"bk{i}", [128, 512], F32) for i in range(4)]
    E = em.emit

    load_fm(em, xsb, x2, DC, TOK, eng="gpsimd")

    for wsrc, dst, with_q2 in ((wqT, Qo, True), (wkT, Ko, False)):
        load_fm(em, wT, wsrc, DC, D, eng="gpsimd", tag="ldw")
        for ntile in range(NSLOT):
            for fc in range(DC):
                pb = bk[fc % 2]
                for dc in range(DC):
                    E("tensor", lambda e, fc=fc, nt=ntile, dc=dc, pb=pb: e.matmul(
                        pb[:, :], wT[:, dc * D + fc * 128: dc * D + fc * 128 + 128],
                        xsb[:, dc * TOK + nt * 512: dc * TOK + nt * 512 + 512],
                        start=(dc == 0), stop=(dc == DC - 1)),
                        reads=[wT, xsb], writes=[pb])
                E("scalar", lambda e, fc=fc, pb=pb: e.activation(
                    qt[:, fc * SL:(fc + 1) * SL], pb[:, :], AF.Copy),
                    reads=[pb], writes=[(qt, fc)])
            dstv = dst.rearrange("(c p) x -> p c x", p=128)[:, :, ntile * 512:(ntile + 1) * 512]
            E("sync", lambda e, dstv=dstv: e.dma_start(
                out=dstv, in_=qt[:, :].rearrange("p (c x) -> p c x", c=DC)),
                reads=[(qt, fc_) for fc_ in range(DC)], dma=True, tag="out")
            if with_q2:
                pq = bk[2]
                for fc in range(DC):
                    E("scalar", lambda e, fc=fc: e.activation(
                        sqv[:, :], qt[:, fc * SL:(fc + 1) * SL], AF.Square),
                        reads=[(qt, fc)], writes=[sqv])
                    E("tensor", lambda e, fc=fc, pq=pq: e.matmul(
                        pq[0:1, :], cm.ones_colb[:, :], sqv[:, :],
                        start=(fc == 0), stop=(fc == DC - 1)),
                        reads=[sqv, cm.ones_colb], writes=[pq])
                E("vector", lambda e, pq=pq: e.tensor_copy(q2r[0:1, :], pq[0:1, :]),
                  reads=[pq], writes=[q2r])
                E("sync", lambda e, nt=ntile: e.dma_start(
                    out=q2o[0:1, nt * 512:(nt + 1) * 512], in_=q2r[0:1, :]),
                    reads=[q2r], dma=True, tag="out")

    load_fm(em, wT, wvT, DC, D, eng="gpsimd", tag="ldw")
    for tc in range(TOK // 128):
        for half in range(2):
            pb = bk[half]
            for dc in range(DC):
                E("tensor", lambda e, tc=tc, half=half, dc=dc, pb=pb: e.matmul(
                    pb[:, :], xsb[:, dc * TOK + tc * 128: dc * TOK + tc * 128 + 128],
                    wT[:, dc * D + half * 512: dc * D + half * 512 + 512],
                    start=(dc == 0), stop=(dc == DC - 1)),
                    reads=[wT, xsb], writes=[pb])
            E("scalar", lambda e, half=half, pb=pb: e.activation(
                qt[:, half * 512: half * 512 + 512], pb[:, :], AF.Copy),
                reads=[pb], writes=[(qt, half)])
        E("sync", lambda e, tc=tc: e.dma_start(
            out=Vo[tc * 128:(tc + 1) * 128, :], in_=qt[:, 0:D]),
            reads=[(qt, 0), (qt, 1)], dma=True, tag="out")

    outs = [op for op in em.ops if op.tag == "out"]
    em.emit("sync", lambda e: e.nop(), after=outs, tag="final")
    em.build("C1")
    return nc


def build_C2():
    nc = _mk_nc()
    x2 = _in(nc, "x2", [D, TOK])
    KoI = _in(nc, "Ko", [D, TOK])
    QfI = _in(nc, "Qf", [D, S])
    VfI = _in(nc, "Vf", [S, D])
    q2f = _in(nc, "q2f", [1, S])
    cmask = _in(nc, "cmask", [4, 8, 128, 512], BF16)
    x3 = _out(nc, "x3", [D, TOK])

    em = Emitter(nc)
    cm = Common(nc, em)
    xrb = nc.alloc_sbuf_tensor("ln_xrb", [128, DC * SL], BF16)
    x2b_ = nc.alloc_sbuf_tensor("ln_x2b", [128, DC * SL], BF16)
    lrows = nc.alloc_sbuf_tensor("ln_rows", [1, 3 * SL], F32)
    cm.attach_ln_scratch(xrb, x2b_, lrows)

    Qf = nc.alloc_sbuf_tensor("Qf_sb", [128, DC * S], BF16)
    Ko = nc.alloc_sbuf_tensor("Ko_sb", [128, DC * TOK], BF16)
    Pp = nc.alloc_sbuf_tensor("Pp", [128, 32 * SL], BF16)
    vbuf = [nc.alloc_sbuf_tensor(f"vb{i}", [128, D], BF16) for i in range(2)]
    mbuf = [nc.alloc_sbuf_tensor(f"mb{i}", [128, SL], BF16) for i in range(2)]
    xot = nc.alloc_sbuf_tensor("xot", [128, DC * SL], F32)
    xres = nc.alloc_sbuf_tensor("xres", [128, DC * SL], F32)
    sqv = nc.alloc_sbuf_tensor("sqv", [128, SL], BF16)
    scr = nc.alloc_sbuf_tensor("scr", [1, SL], F32)
    sca = nc.alloc_sbuf_tensor("sca", [1, 16], F32)
    negc = nc.alloc_sbuf_tensor("negc", [128, 1], F32)
    rl = nc.alloc_sbuf_tensor("rl", [1, SL], F32)
    rlb = nc.alloc_sbuf_tensor("rlb", [128, SL], F32)
    bk = [nc.alloc_psum_tensor(f"bk{i}", [128, 512], F32) for i in range(8)]
    E = em.emit

    load_fm(em, Qf, QfI, DC, S, eng="gpsimd")
    load_fm(em, Ko, KoI, DC, TOK, eng="gpsimd")

    mk2, mq2 = sca[0:1, 0:1], sca[0:1, 1:2]
    prd, sqr = sca[0:1, 2:3], sca[0:1, 3:4]
    ngc, cur = sca[0:1, 4:5], sca[0:1, 5:6]
    E("vector", lambda e: e.memset(mk2, 0.0), writes=[(sca, "mk2")])
    E("vector", lambda e: e.memset(mq2, 0.0), writes=[(sca, "mq2")])
    for ntile in range(NSLOT):
        pq = bk[0]
        for fc in range(DC):
            E("scalar", lambda e, fc=fc, nt=ntile: e.activation(
                sqv[:, :], Ko[:, fc * TOK + nt * 512: fc * TOK + nt * 512 + 512], AF.Square),
                reads=[Ko], writes=[sqv])
            E("tensor", lambda e, fc=fc, pq=pq: e.matmul(
                pq[0:1, :], cm.ones_colb[:, :], sqv[:, :],
                start=(fc == 0), stop=(fc == DC - 1)),
                reads=[sqv, cm.ones_colb], writes=[pq])
        E("vector", lambda e, pq=pq: e.tensor_copy(scr[0:1, :], pq[0:1, :]),
          reads=[pq], writes=[scr])
        E("vector", lambda e: e.reduce_max(cur, scr[0:1, :], axis=AX.X),
          reads=[scr], writes=[(sca, "cur")])
        E("vector", lambda e: e.tensor_max(mk2, mk2, cur),
          reads=[(sca, "cur"), (sca, "mk2")], writes=[(sca, "mk2")])
    for ntile in range(S // 512):
        E("sync", lambda e, nt=ntile: e.dma_start(
            out=scr[0:1, :], in_=q2f[0:1, nt * 512:(nt + 1) * 512]),
            writes=[scr], dma=True)
        E("vector", lambda e: e.reduce_max(cur, scr[0:1, :], axis=AX.X),
          reads=[scr], writes=[(sca, "cur")])
        E("vector", lambda e: e.tensor_max(mq2, mq2, cur),
          reads=[(sca, "cur"), (sca, "mq2")], writes=[(sca, "mq2")])
    E("vector", lambda e: e.tensor_mul(prd, mk2, mq2),
      reads=[(sca, "mk2"), (sca, "mq2")], writes=[(sca, "prd")])
    E("scalar", lambda e: e.activation(sqr, prd, AF.Sqrt),
      reads=[(sca, "prd")], writes=[(sca, "sqr")])
    E("vector", lambda e: e.tensor_scalar(ngc, sqr, -1.0 / 32.0, None, ALU.mult),
      reads=[(sca, "sqr")], writes=[(sca, "ngc")])
    E("tensor", lambda e: e.matmul(bk[3][:, 0:1], cm.ones_row[:, :], ngc, start=True, stop=True),
      reads=[(sca, "ngc"), cm.ones_row], writes=[bk[3]])
    E("vector", lambda e: e.tensor_copy(negc[:, :], bk[3][:, 0:1]),
      reads=[bk[3]], writes=[negc])

    for it in range(NSLOT):
        for jc in range(32):
            sb_ = bk[jc % 2]
            for dc in range(DC):
                E("tensor", lambda e, jc=jc, it=it, dc=dc, sb_=sb_: e.matmul(
                    sb_[:, :], Qf[:, dc * S + jc * 128: dc * S + jc * 128 + 128],
                    Ko[:, dc * TOK + it * 512: dc * TOK + it * 512 + 512],
                    start=(dc == 0), stop=(dc == DC - 1)),
                    reads=[Qf, Ko], writes=[sb_])
            E("scalar", lambda e, jc=jc, sb_=sb_: e.activation(
                Pp[:, jc * SL:(jc + 1) * SL], sb_[:, :], AF.Exp,
                bias=negc[:, 0:1], scale=1.0 / 32.0),
                reads=[sb_, negc], writes=[(Pp, jc)])
            E("tensor", lambda e, jc=jc: e.matmul(
                bk[2][0:1, :], cm.ones_colb[:, :], Pp[:, jc * SL:(jc + 1) * SL],
                start=(jc == 0), stop=(jc == 31)),
                reads=[(Pp, jc), cm.ones_colb], writes=[bk[2]])
        for mi, g in enumerate(MSK_TILES[it]):
            for cl in range(4):
                jc = g * 4 + cl
                mb_ = mbuf[(mi * 4 + cl) % 2]
                E("sync", lambda e, it=it, mi=mi, cl=cl, mb_=mb_: e.dma_start(
                    out=mb_[:, :], in_=cmask[it, mi * 4 + cl]),
                    writes=[mb_], dma=True)
                E("vector", lambda e, jc=jc, mb_=mb_: e.tensor_mul(
                    Pp[:, jc * SL:(jc + 1) * SL], Pp[:, jc * SL:(jc + 1) * SL], mb_[:, :]),
                    reads=[(Pp, jc), mb_], writes=[(Pp, jc)])
        E("vector", lambda e: e.tensor_copy(scr[0:1, :], bk[2][0:1, :]),
          reads=[bk[2]], writes=[scr])
        E("vector", lambda e: e.reciprocal(rl[0:1, :], scr[0:1, :]),
          reads=[scr], writes=[rl])
        E("tensor", lambda e: e.matmul(bk[3][:, :], cm.ones_row[:, :], rl[0:1, :],
                                       start=True, stop=True),
          reads=[rl, cm.ones_row], writes=[bk[3]])
        E("vector", lambda e: e.tensor_copy(rlb[:, :], bk[3][:, :]),
          reads=[bk[3]], writes=[rlb])
        xsl = x2d.rearrange("(c p) x -> p c x", p=128)[:, :, it * 512:(it + 1) * 512]
        E("sync", lambda e, xsl=xsl: e.dma_start(
            out=xot[:, :].rearrange("p (c x) -> p c x", c=DC), in_=xsl),
            writes=[xot], dma=True)
        chunks = [g * 4 + cl for g in INC_TILES[it] for cl in range(4)]
        for half in range(2):
            for ji, jc in enumerate(chunks):
                vb_ = vbuf[ji % 2]
                E("gpsimd", lambda e, jc=jc, vb_=vb_: e.dma_start(
                    out=vb_[:, :], in_=VfI[jc * 128:(jc + 1) * 128, :]),
                    writes=[vb_], dma=True)
                for dvl in range(4):
                    dv = half * 4 + dvl
                    E("tensor", lambda e, jc=jc, dv=dv, dvl=dvl, vb_=vb_, ji=ji, n=len(chunks): e.matmul(
                        bk[4 + dvl][:, :], vb_[:, dv * 128:(dv + 1) * 128],
                        Pp[:, jc * SL:(jc + 1) * SL],
                        start=(ji == 0), stop=(ji == n - 1)),
                        reads=[vb_, (Pp, jc)], writes=[bk[4 + dvl]])
            for dvl in range(4):
                dv = half * 4 + dvl
                E("vector", lambda e, dvl=dvl: e.tensor_mul(
                    cm.tmp[:, :], bk[4 + dvl][:, :], rlb[:, :]),
                    reads=[bk[4 + dvl], rlb], writes=[cm.tmp])
                E("vector", lambda e, dv=dv: e.tensor_add(
                    xres[:, dv * SL:(dv + 1) * SL], cm.tmp[:, :], xot[:, dv * SL:(dv + 1) * SL]),
                    reads=[cm.tmp, xot], writes=[xres])
        cm.layernorm_nt(xres, bk[0], bk[1], out_f32=xres)
        dstv = x3.rearrange("(c p) x -> p c x", p=128)[:, :, it * 512:(it + 1) * 512]
        srcv = xres[:, :].rearrange("p (c x) -> p c x", c=DC)
        E("sync", lambda e, dstv=dstv, srcv=srcv: e.dma_start(out=dstv, in_=srcv),
          reads=[xres], dma=True, tag="out")

    outs = [op for op in em.ops if op.tag == "out"]
    em.emit("sync", lambda e: e.nop(), after=outs, tag="final")
    em.build("C2")
    return nc


def build_D():
    nc = _mk_nc()
    x3 = _in(nc, "x3", [D, TOK])
    WgT = _in(nc, "WgT", [D, 8])
    lnsI = _in(nc, "lns", [D, 8])
    lnbI = _in(nc, "lnb", [D, 8])
    W1f = _in(nc, "W1f", [D, 8 * 256])
    b1f = _in(nc, "b1f", [256, 8])
    W2f = _in(nc, "W2f", [256, 8 * D])
    b2I = _in(nc, "b2", [8, D])
    idI = _in(nc, "id128", [128, 128])
    indI = _in(nc, "ind", [8, 1024], BF16)
    xout = _out(nc, "xout", [D, TOK])

    em = Emitter(nc)
    cm = Common(nc, em)
    xrb = nc.alloc_sbuf_tensor("ln_xrb", [128, DC * SL], BF16)
    x2b_ = nc.alloc_sbuf_tensor("ln_x2b", [128, DC * SL], BF16)
    lrows = nc.alloc_sbuf_tensor("ln_rows", [1, 3 * SL], F32)
    cm.attach_ln_scratch(xrb, x2b_, lrows)

    w1e = nc.alloc_sbuf_tensor("w1e", [128, DC * 8 * 256], BF16)
    w2e = nc.alloc_sbuf_tensor("w2e", [128, 2 * 8 * D], BF16)
    wg = nc.alloc_sbuf_tensor("wg", [128, DC * 8], F32)
    lns = nc.alloc_sbuf_tensor("lns_sb", [128, DC * 8], F32)
    lnb = nc.alloc_sbuf_tensor("lnb_sb", [128, DC * 8], F32)
    b1s = nc.alloc_sbuf_tensor("b1s", [128, 2 * 8], F32)
    b2b = nc.alloc_sbuf_tensor("b2b", [8, D], BF16)
    idt = nc.alloc_sbuf_tensor("idt", [128, 128], F32)
    indb = nc.alloc_sbuf_tensor("indb", [8, 1024], BF16)
    xot = nc.alloc_sbuf_tensor("xot", [128, DC * SL], F32)
    xtb = nc.alloc_sbuf_tensor("xtb", [128, DC * SL], BF16)
    xhat = nc.alloc_sbuf_tensor("xhat", [128, DC * SL], BF16)
    xeb = nc.alloc_sbuf_tensor("xeb", [128, DC * SL], BF16)
    hall = nc.alloc_sbuf_tensor("hall", [128, 16 * SL], BF16)
    xres = nc.alloc_sbuf_tensor("xres", [128, DC * SL], F32)
    rmx = nc.alloc_sbuf_tensor("rmx", [128, 1], F32)
    mtm = nc.alloc_sbuf_tensor("mtm", [128, 8], F32)
    maskb = nc.alloc_sbuf_tensor("maskb", [8, SL], BF16)
    bk = [nc.alloc_psum_tensor(f"bk{i}", [128, 512], F32) for i in range(8)]
    E = em.emit

    load_fm(em, wg, WgT, DC, 8, eng="sync")
    load_fm(em, lns, lnsI, DC, 8, eng="sync")
    load_fm(em, lnb, lnbI, DC, 8, eng="sync")
    load_fm(em, w1e, W1f, DC, 8 * 256, eng="gpsimd")
    load_fm(em, b1s, b1f, 2, 8, eng="sync")
    load_fm(em, w2e, W2f, 2, 8 * D, eng="gpsimd")
    E("gpsimd", lambda e: e.dma_start(out=b2b[:, :], in_=b2I), writes=[b2b], dma=True)
    E("sync", lambda e: e.dma_start(out=idt[:, :], in_=idI), writes=[idt], dma=True)
    E("sync", lambda e: e.dma_start(out=indb[:, :], in_=indI), writes=[indb], dma=True)

    for tt in range(NSLOT):
        xsl = x3.rearrange("(c p) x -> p c x", p=128)[:, :, tt * 512:(tt + 1) * 512]
        E("sync", lambda e, xsl=xsl: e.dma_start(
            out=xot[:, :].rearrange("p (c x) -> p c x", c=DC), in_=xsl),
            writes=[xot], dma=True)
        E("gpsimd", lambda e, xsl=xsl: e.dma_start(
            out=xtb[:, :].rearrange("p (c x) -> p c x", c=DC), in_=xsl),
            writes=[xtb], dma=True)
        for m4 in range(4):
            pg = bk[0]
            for dc in range(DC):
                E("tensor", lambda e, m4=m4, dc=dc, pg=pg: e.matmul(
                    pg[:, 0:8], xot[:, dc * SL + m4 * 128: dc * SL + m4 * 128 + 128],
                    wg[:, dc * 8:(dc + 1) * 8],
                    start=(dc == 0), stop=(dc == DC - 1)),
                    reads=[xot, wg], writes=[pg])
            E("vector", lambda e, pg=pg: e.reduce_max(rmx[:, :], pg[:, 0:8], axis=AX.X),
              reads=[pg], writes=[rmx])
            E("vector", lambda e, pg=pg: e.tensor_scalar(
                mtm[:, :], pg[:, 0:8], rmx[:, 0:1], None, ALU.is_ge),
                reads=[pg, rmx], writes=[mtm])
            E("tensor", lambda e: e.transpose(bk[3][0:8, 0:128], mtm[:, :], idt[:, :]),
              reads=[mtm, idt], writes=[bk[3]])
            E("vector", lambda e, m4=m4: e.tensor_copy(
                maskb[0:8, m4 * 128:(m4 + 1) * 128], bk[3][0:8, 0:128]),
                reads=[bk[3]], writes=[maskb])
        cm.layernorm_nt(xot, bk[6], bk[7], out_f32=None, out_bf16=xhat)
        for ex in range(8):
            for dc in range(DC):
                E("vector", lambda e, ex=ex, dc=dc: e.tensor_scalar(
                    xeb[:, dc * SL:(dc + 1) * SL], xhat[:, dc * SL:(dc + 1) * SL],
                    lns[:, dc * 8 + ex: dc * 8 + ex + 1], lnb[:, dc * 8 + ex: dc * 8 + ex + 1],
                    ALU.mult, ALU.add),
                    reads=[xhat, lns, lnb], writes=[xeb])
            E("tensor", lambda e, ex=ex: e.matmul(
                bk[3][:, :], indb[0:8, ex * 128:(ex + 1) * 128], maskb[0:8, :],
                start=True, stop=True),
                reads=[indb, maskb], writes=[bk[3]])
            for hfc in range(2):
                ph = bk[1 + hfc]
                for dc in range(DC):
                    E("tensor", lambda e, ex=ex, hfc=hfc, dc=dc, ph=ph: e.matmul(
                        ph[:, :], w1e[:, dc * 2048 + ex * 256 + hfc * 128: dc * 2048 + ex * 256 + hfc * 128 + 128],
                        xeb[:, dc * SL:(dc + 1) * SL],
                        start=(dc == 0), stop=(dc == DC - 1)),
                        reads=[w1e, xeb], writes=[ph])
                E("scalar", lambda e, ex=ex, hfc=hfc, ph=ph: e.activation(
                    hall[:, (ex * 2 + hfc) * SL:(ex * 2 + hfc + 1) * SL], ph[:, :], AF.Gelu,
                    bias=b1s[:, hfc * 8 + ex: hfc * 8 + ex + 1]),
                    reads=[ph, b1s], writes=[(hall, (ex, hfc))])
                E("vector", lambda e, ex=ex, hfc=hfc: e.tensor_mul(
                    hall[:, (ex * 2 + hfc) * SL:(ex * 2 + hfc + 1) * SL],
                    hall[:, (ex * 2 + hfc) * SL:(ex * 2 + hfc + 1) * SL], bk[3][:, :]),
                    reads=[(hall, (ex, hfc)), bk[3]], writes=[(hall, (ex, hfc))])
        for fc in range(DC):
            py = bk[4 + fc % 2]
            for ex in range(8):
                for hfc in range(2):
                    E("tensor", lambda e, fc=fc, ex=ex, hfc=hfc, py=py: e.matmul(
                        py[:, :], w2e[:, hfc * 8 * D + ex * D + fc * 128: hfc * 8 * D + ex * D + fc * 128 + 128],
                        hall[:, (ex * 2 + hfc) * SL:(ex * 2 + hfc + 1) * SL],
                        start=(ex == 0 and hfc == 0), stop=False),
                        reads=[w2e, (hall, (ex, hfc))], writes=[py])
            E("tensor", lambda e, fc=fc, py=py: e.matmul(
                py[:, :], b2b[0:8, fc * 128:(fc + 1) * 128], maskb[0:8, :],
                start=False, stop=True),
                reads=[b2b, maskb], writes=[py])
            E("vector", lambda e, fc=fc, py=py: e.tensor_add(
                xres[:, fc * SL:(fc + 1) * SL], py[:, :], xot[:, fc * SL:(fc + 1) * SL]),
                reads=[py, xot], writes=[xres])
        cm.layernorm_nt(xres, bk[6], bk[7], out_f32=xres)
        dstv = xout.rearrange("(c p) x -> p c x", p=128)[:, :, tt * 512:(tt + 1) * 512]
        srcv = xres[:, :].rearrange("p (c x) -> p c x", c=DC)
        E("sync", lambda e, dstv=dstv, srcv=srcv: e.dma_start(out=dstv, in_=srcv),
          reads=[xres], dma=True, tag="out")

    outs = [op for op in em.ops if op.tag == "out"]
    em.emit("sync", lambda e: e.nop(), after=outs, tag="final")
    em.build("D")
    return nc


PGSZ = 1024


class Arena:
    def __init__(self, nc, em, size):
        self.nc, self.em = nc, em
        self.base = nc.alloc_sbuf_tensor("arena", [128, size], mybir.dt.uint8)
        self.addr = nc.lookup_mloc(self.base).addr

    def at(self, name, off, shape, dtype):
        t = self.nc.alloc_sbuf_tensor_at(name, shape, dtype, offset=self.addr + off)
        nb = int(np.prod(shape[1:])) * (2 if dtype == BF16 else 4 if dtype in (F32,) else 1)
        self.em.set_pages(t, off, nb)
        return t


class PView:

    def __init__(self, t, c0, ncols, name):
        self.t, self.c0, self.ncols, self.name = t, c0, ncols, name

    def __getitem__(self, idx):
        p, c = idx
        a = 0 if c.start is None else c.start
        b = self.ncols if c.stop is None else c.stop
        return self.t[p, self.c0 + a:self.c0 + b]


PSUM_PGBASE = 1 << 20


def build_F():
    nc = _mk_nc()
    xseq = _in(nc, "xseq", [D, S])
    WvT = _in(nc, "WvT", [D, 256])
    WuT = _in(nc, "WuT", [256, 3 * D])
    WovT = _in(nc, "WovT", [D, 256])
    WouT = _in(nc, "WouT", [256, D])
    w1T = _in(nc, "w1T", [D, 4096])
    w2T = _in(nc, "w2T", [4096, D])
    b1c = _in(nc, "b1c", [128, 32])
    b2c = _in(nc, "b2c", [128, 8])
    wqT = _in(nc, "wqT", [D, D])
    wkT = _in(nc, "wkT", [D, D])
    wvT = _in(nc, "wvT", [D, D])
    cmask = _in(nc, "cmask", [4, 8, 128, 512], BF16)
    WgT = _in(nc, "WgT", [D, 8])
    lnsI = _in(nc, "lns", [D, 8])
    lnbI = _in(nc, "lnb", [D, 8])
    W1f = _in(nc, "W1f", [D, 8 * 256])
    b1f = _in(nc, "b1f", [256, 8])
    W2f = _in(nc, "W2f", [256, 8 * D])
    b2I = _in(nc, "b2", [8, D])
    idI = _in(nc, "id128", [128, 128])
    indI = _in(nc, "ind", [8, 1024], BF16)
    xout = _out(nc, "xout", [D, TOK])
    x1d = nc.dram_tensor("x1d", [D, TOK], F32, kind="Internal").ap()
    x2d = nc.dram_tensor("x2d", [D, TOK], F32, kind="Internal").ap()
    x3d = nc.dram_tensor("x3d", [D, TOK], F32, kind="Internal").ap()
    s1s = nc.dram_tensor("s1s", [32, 128, 1024], BF16, kind="Internal").ap()
    s2s = nc.dram_tensor("s2s", [4096, D], BF16, kind="Internal").ap()
    eQ = nc.dram_tensor("eQ", [TOK, D], BF16, kind="Internal")
    eV = nc.dram_tensor("eV", [TOK, D], BF16, kind="Internal")
    eq2 = nc.dram_tensor("eq2", [1, TOK], F32, kind="Internal")
    gQ = nc.dram_tensor("gQ", [2 * TOK, D], BF16, kind="Internal")
    gV = nc.dram_tensor("gV", [2 * TOK, D], BF16, kind="Internal")
    gq2 = nc.dram_tensor("gq2", [2, TOK], F32, kind="Internal")
    kd = nc.dram_tensor("kd", [D, TOK], BF16, kind="Internal").ap()

    em = Emitter(nc)
    cm = Common(nc, em)
    ar = Arena(nc, em, 196 * 1024)
    S4 = nc.alloc_psum_tensor("S4", [128, 4 * 512], F32)
    em.set_pages(S4, PSUM_PGBASE, 4 * 2048)
    bk = [PView(S4, i * 512, 512, f"S4v{i}") for i in range(4)] + \
         [nc.alloc_psum_tensor(f"bk{i}", [128, 512], F32) for i in range(4, 8)]
    for i in range(4):
        em.set_pages(bk[i], PSUM_PGBASE + i * 2048, 2048)
    E = em.emit


    xb = ar.at("xb", 0, [128, DC * S], BF16)
    xresA = ar.at("xresA", 0, [128, DC * SL], F32)
    xotA = ar.at("xotA", 16384, [128, DC * SL], F32)
    xrbA = ar.at("xrbA", 32768, [128, DC * SL], BF16)
    x2bA = ar.at("x2bA", 40960, [128, DC * SL], BF16)
    lrowsA = ar.at("lrowsA", 49152, [1, 3 * SL], F32)
    wv = ar.at("wv", 65536, [128, DC * 256], BF16)
    wu = ar.at("wu", 69632, [128, 2 * 3 * D], BF16)
    wov = ar.at("wov", 81920, [128, DC * 256], BF16)
    wou = ar.at("wou", 86016, [128, 2 * D], BF16)
    t = ar.at("t", 90112, [128, 2 * S], BF16)
    Qg = ar.at("Qg", 106496, [128, S], BF16)
    Kg = ar.at("Kg", 114688, [128, TOK], BF16)
    Vg = ar.at("Vg", 118784, [128, 32 * 130], BF16)
    y = ar.at("y", 128000, [128, DC * TOK], BF16)
    z1 = ar.at("z1", 160768, [128, 2 * TOK], BF16)
    Pp = ar.at("Pp", 168960, [128, 8 * SL], BF16)
    sq = ar.at("sq", 173056, [128, S], BF16)
    r_sc = ar.at("r_sc", 181248, [1, 528], F32)
    negc = ar.at("negc", 183424, [128, 1], F32)
    rl = ar.at("rl", 177152, [1, 2 * SL], F32)
    rlb = ar.at("rlb", 185536, [128, SL], F32)
    cm.attach_ln_scratch(xrbA, x2bA, lrowsA)

    load_fm(em, xb, xseq, DC, S, eng="gpsimd")
    load_fm(em, wv, WvT, DC, 256, eng="gpsimd")
    load_fm(em, wu, WuT, 2, 3 * D, eng="gpsimd")
    load_fm(em, wov, WovT, DC, 256, eng="gpsimd")
    load_fm(em, wou, WouT, 2, D, eng="gpsimd")

    wblk = ar.at("wblk", 188416, [128, 1024], F32)
    sgn = ar.at("sgn", 192512, [128, 1024], BF16)
    ab = ar.at("ab", 194560, [128, 1024], BF16)
    ab2 = ar.at("ab2", 197120, [128, 1024], BF16)
    a1c = ar.at("a1c", 196608, [128, 32], F32)
    a2c = ar.at("a2c", 196736, [128, 8], F32)
    ones4k = ar.at("ones4k", 196800, [128, 1], BF16)
    b1s = ar.at("b1s", 196864, [128, 32], F32)
    b2s = ar.at("b2s", 197024, [128, 8], F32)
    E("vector", lambda e: e.memset(ones4k[:, :], 1.0 / 4096.0), writes=[ones4k])
    E("sync", lambda e: e.dma_start(out=b1s[:, :], in_=b1c), writes=[b1s], dma=True)
    E("sync", lambda e: e.dma_start(out=b2s[:, :], in_=b2c), writes=[b2s], dma=True)
    E("vector", lambda e: e.memset(a2c[:, :], 0.0), writes=[a2c])
    def p1_w1_step(hfc):
            wv_ = w1T.rearrange("(dc p) c -> p dc c", p=128)[:, :, hfc * 128:(hfc + 1) * 128]
            E("sync", lambda e, wv_=wv_: e.dma_start(
                out=wblk[:, :].rearrange("p (dc c) -> p dc c", dc=8), in_=wv_),
                writes=[wblk], dma=True)
            E("scalar", lambda e: e.activation(sgn[:, :], wblk[:, :], AF.Sign),
              reads=[wblk], writes=[sgn])
            E("scalar", lambda e: e.activation(ab[:, :], wblk[:, :], AF.Abs),
              reads=[wblk], writes=[ab])
            E("gpsimd", lambda e, hfc=hfc: e.dma_start(out=s1s[hfc], in_=sgn[:, :]),
              reads=[sgn], writes=[(s1s, hfc)], dma=True)
            def reduce():
                pb = bk[7]
                for dc in range(DC):
                    E("tensor", lambda e, dc=dc, pb=pb: e.matmul(
                        pb[:, 0:1], ab[:, dc * 128:(dc + 1) * 128], cm.ones_mean[:, :],
                        start=(dc == 0), stop=(dc == DC - 1)),
                        reads=[ab, cm.ones_mean], writes=[pb])
                E("vector", lambda e, hfc=hfc, pb=pb: e.tensor_copy(a1c[:, hfc:hfc + 1], pb[:, 0:1]),
                  reads=[pb], writes=[a1c])
            return reduce


    def p1_w2_step(hfc):
            wv_ = w2T[hfc * 128:(hfc + 1) * 128, :]
            E("sync", lambda e, wv_=wv_: e.dma_start(out=wblk[:, :], in_=wv_),
              writes=[wblk], dma=True)
            E("scalar", lambda e: e.activation(sgn[:, :], wblk[:, :], AF.Sign),
              reads=[wblk], writes=[sgn])
            E("scalar", lambda e: e.activation(ab2[:, :], wblk[:, :], AF.Abs),
              reads=[wblk], writes=[ab2])
            E("gpsimd", lambda e, hfc=hfc: e.dma_start(out=s2s[hfc * 128:(hfc + 1) * 128, :], in_=sgn[:, :]),
              reads=[sgn], writes=[(s2s, hfc)], dma=True)
            def reduce():
                pa = bk[7]
                for fc in range(DC):
                    E("tensor", lambda e, fc=fc, pa=pa: e.matmul(
                        pa[:, fc:fc + 1], ab2[:, fc * 128:(fc + 1) * 128], ones4k[:, :],
                        start=True, stop=True),
                        reads=[ab2, ones4k], writes=[pa])
                E("vector", lambda e, pa=pa: e.tensor_add(a2c[:, :], a2c[:, :], pa[:, 0:8]),
                  reads=[pa, a2c], writes=[a2c])
            return reduce



    p1_iter = iter(range(32))
    pending_p1 = []

    for rc in range(2):
        for ntile in range(S // 512):
            pb = bk[ntile % 2]
            for dc in range(DC):
                E("tensor", lambda e, rc=rc, nt=ntile, dc=dc, pb=pb: e.matmul(
                    pb[:, :], wv[:, dc * 256 + rc * 128: dc * 256 + rc * 128 + 128],
                    xb[:, dc * S + nt * 512: dc * S + nt * 512 + 512],
                    start=(dc == 0), stop=(dc == DC - 1)),
                    reads=[wv, xb], writes=[pb])
            E("vector", lambda e, rc=rc, nt=ntile, pb=pb: e.tensor_copy(
                t[:, rc * S + nt * 512: rc * S + nt * 512 + 512], pb[:, :]),
                reads=[pb], writes=[t])

    pending_evac = [None]
    pending_avtail = [None]
    for g in range(8):
        if pending_avtail[0] is not None:
            pending_avtail[0]()
            pending_avtail[0] = None
        if pending_evac[0] is not None:
            pending_evac[0]()
            pending_evac[0] = None
        fq = D + g * 128
        fk = g * 128
        fv = 2 * D + g * 128
        scr = r_sc[0:1, 0:512]
        mk2 = r_sc[0:1, 512:513]
        mq2 = r_sc[0:1, 513:514]
        prd = r_sc[0:1, 514:515]
        sqr = r_sc[0:1, 515:516]
        ngc = r_sc[0:1, 516:517]
        cur = r_sc[0:1, 517:518]
        E("vector", lambda e, a=mk2: e.memset(a, 0.0), writes=[r_sc])
        E("vector", lambda e, a=mq2: e.memset(a, 0.0), writes=[r_sc])
        def csq_batch(src_, ntiles, mrow):
            nb = ntiles * 512 * 2
            E("scalar", lambda e, n=ntiles, src_=src_: e.activation(
                sq[:, 0:n * 512], src_[:, 0:n * 512], AF.Square),
                reads=[src_], writes=[(sq, ("B", 0, nb))])
            for nt in range(ntiles):
                pb = bk[6] if nt % 2 == 0 else bk[7]
                bs = ("B", nt * 1024, (nt + 1) * 1024)
                E("tensor", lambda e, nt=nt, pb=pb: e.matmul(
                    pb[0:1, :], cm.ones_colb[:, :], sq[:, nt * 512: nt * 512 + 512],
                    start=True, stop=True),
                    reads=[(sq, bs), cm.ones_colb], writes=[pb])
                E("vector", lambda e, pb=pb, a=cur: e.reduce_max(a, pb[0:1, :], axis=AX.X),
                    reads=[pb], writes=[r_sc])
                E("vector", lambda e, mrow=mrow, b=cur: e.tensor_max(mrow, mrow, b),
                    reads=[r_sc], writes=[r_sc])
        for ntile in range(S // 512):
            pb = bk[ntile % 2]
            for rc in range(2):
                E("tensor", lambda e, nt=ntile, rc=rc, pb=pb, fq=fq: e.matmul(
                    pb[:, :], wu[:, rc * 3 * D + fq: rc * 3 * D + fq + 128],
                    t[:, rc * S + nt * 512: rc * S + nt * 512 + 512],
                    start=(rc == 0), stop=(rc == 1)),
                    reads=[wu, t], writes=[pb])
            E("vector", lambda e, nt=ntile, pb=pb: e.tensor_copy(
                Qg[:, nt * 512: nt * 512 + 512], pb[:, :]),
                reads=[pb], writes=[Qg])
        csq_batch(Qg, S // 512, mq2)
        for ntile in range(TOK // 512):
            pb = bk[2 + ntile % 2]
            for rc in range(2):
                E("tensor", lambda e, nt=ntile, rc=rc, pb=pb, fk=fk: e.matmul(
                    pb[:, :], wu[:, rc * 3 * D + fk: rc * 3 * D + fk + 128],
                    t[:, rc * S + nt * 512: rc * S + nt * 512 + 512],
                    start=(rc == 0), stop=(rc == 1)),
                    reads=[wu, t], writes=[pb])
            E("vector", lambda e, nt=ntile, pb=pb: e.tensor_copy(
                Kg[:, nt * 512: nt * 512 + 512], pb[:, :]),
                reads=[pb], writes=[Kg])
        csq_batch(Kg, TOK // 512, mk2)
        for jc in range(32):
            pb = bk[4 + jc % 2]
            for rc in range(2):
                E("tensor", lambda e, jc=jc, rc=rc, pb=pb, fv=fv: e.matmul(
                    pb[:, 0:128], t[:, rc * S + jc * 128: rc * S + jc * 128 + 128],
                    wu[:, rc * 3 * D + fv: rc * 3 * D + fv + 128],
                    start=(rc == 0), stop=(rc == 1)),
                    reads=[wu, t], writes=[pb])
            for h in range(2):
                E("vector", lambda e, jc=jc, h=h, pb=pb: e.tensor_copy(
                    Vg[:, jc * 130 + h * 65: jc * 130 + h * 65 + 64],
                    pb[:, h * 64: h * 64 + 64]),
                    reads=[pb], writes=[Vg])
        E("vector", lambda e: e.memset(
            Vg[:, :].rearrange("p (j h c) -> p j h c", j=32, h=2)[:, :, :, 64:65], 1.0),
            writes=[Vg])

        E("vector", lambda e, a=prd, b=mk2, c=mq2: e.tensor_mul(a, b, c), reads=[r_sc], writes=[r_sc])
        E("scalar", lambda e, a=sqr, b=prd: e.activation(a, b, AF.Sqrt), reads=[r_sc], writes=[r_sc])
        E("vector", lambda e, a=ngc, b=sqr: e.tensor_scalar(a, b, -0.125, None, ALU.mult),
          reads=[r_sc], writes=[r_sc])
        pb = bk[6]
        E("tensor", lambda e, pb=pb, a=ngc: e.matmul(pb[:, 0:1], cm.ones_row[:, :], a,
                                                     start=True, stop=True),
          reads=[r_sc, cm.ones_row], writes=[pb])
        E("vector", lambda e, pb=pb: e.tensor_copy(negc[:, :], pb[:, 0:1]),
          reads=[pb], writes=[negc])

        for it in range(NSLOT):
            def sprime(jc, h):
                sb_ = bk[2 * (jc % 2) + h]
                pbase = h * 64
                E("tensor", lambda e, jc=jc, it=it, pbase=pbase, sb_=sb_: e.matmul(
                    sb_[:, :], Qg[pbase:pbase + 64, jc * 128: jc * 128 + 128],
                    Kg[pbase:pbase + 64, it * 512: it * 512 + 512],
                    start=True, stop=True),
                    reads=[Qg, Kg], writes=[sb_])
            def expw(jc):
                c0 = (jc % 2) * 2 * SL
                p0 = (((jc // 2) % 2) * 4 + (jc % 2) * 2) * SL
                E("scalar", lambda e, c0=c0, p0=p0: e.activation(
                    Pp[:, p0:p0 + 2 * SL], S4[:, c0:c0 + 2 * SL], AF.Exp,
                    bias=negc[:, 0:1], scale=0.125),
                    reads=[(S4, ("B", c0 * 4, (c0 + 2 * SL) * 4)), negc],
                    writes=[(Pp, ("B", p0 * 2, (p0 + 2 * SL) * 2))])
            def av(jc, h):
                s0 = (((jc // 2) % 2) * 4 + (jc % 2) * 2 + h) * SL
                nbank = bk[4 + h]
                E("tensor", lambda e, jc=jc, h=h, s0=s0, nbank=nbank: e.matmul(
                    nbank[0:65, :], Vg[:, jc * 130 + h * 65: jc * 130 + h * 65 + 65],
                    Pp[:, s0:s0 + SL], start=(jc == 0), stop=(jc == 31)),
                    reads=[Vg, (Pp, ("B", s0 * 2, (s0 + SL) * 2))], writes=[nbank])
            def mk_evac(g_, it_):
                def run():
                    for h in range(2):
                        nbank = bk[4 + h]
                        E("vector", lambda e, h=h, nbank=nbank: e.reciprocal(
                            rl[0:1, h * SL:(h + 1) * SL], nbank[64:65, :]),
                            reads=[nbank], writes=[(rl, ("B", h * SL * 4, (h + 1) * SL * 4))])
                    for h in range(2):
                        pbx = bk[6] if h == 0 else bk[7]
                        E("tensor", lambda e, h=h, pbx=pbx: e.matmul(
                            pbx[0:64, :], cm.ones_row[0:1, 0:64],
                            rl[0:1, h * SL:(h + 1) * SL], start=True, stop=True),
                            reads=[(rl, ("B", h * SL * 4, (h + 1) * SL * 4)), cm.ones_row],
                            writes=[pbx])
                    for h in range(2):
                        pbx = bk[6] if h == 0 else bk[7]
                        E("vector", lambda e, h=h, pbx=pbx: e.tensor_copy(
                            rlb[h * 64:h * 64 + 64, :], pbx[0:64, :]),
                            reads=[pbx], writes=[rlb])
                    for h in range(2):
                        nbank = bk[4 + h]
                        hh = 2 * g_ + h
                        fc_, pb_ = hh // 2, (hh % 2) * 64
                        E("vector", lambda e, h=h, it_=it_, fc_=fc_, pb_=pb_, nbank=nbank: e.tensor_mul(
                            y[pb_:pb_ + 64, fc_ * TOK + it_ * 512: fc_ * TOK + it_ * 512 + 512],
                            nbank[0:64, :], rlb[h * 64:h * 64 + 64, :]),
                            reads=[nbank, rlb], writes=[y])
                return run
            sprime(0, 0); sprime(0, 1); expw(0)
            if pending_avtail[0] is not None:
                pending_avtail[0]()
                pending_avtail[0] = None
            if pending_evac[0] is not None:
                pending_evac[0]()
                pending_evac[0] = None
            for f in pending_p1:
                f()
            pending_p1.clear()
            sprime(1, 0); sprime(1, 1); expw(1)
            for jc in range(2, 32):
                sprime(jc, 0); sprime(jc, 1)
                expw(jc)
                av(jc - 2, 0); av(jc - 2, 1)
            k_ = next(p1_iter, None)
            if k_ is not None:
                pending_p1.append(p1_w1_step(k_))
                pending_p1.append(p1_w2_step(k_))
            pending_avtail[0] = (lambda avf=av: (avf(30, 0), avf(30, 1),
                                                avf(31, 0), avf(31, 1)))
            pending_evac[0] = mk_evac(g, it)
    pending_avtail[0]()
    pending_avtail[0] = None
    pending_evac[0]()
    pending_evac[0] = None

    for f in pending_p1:
        f()
    pending_p1.clear()

    for rc in range(2):
        for ntile in range(NSLOT):
            pb = bk[ntile % 2]
            for dc in range(DC):
                E("tensor", lambda e, rc=rc, nt=ntile, dc=dc, pb=pb: e.matmul(
                    pb[:, :], wov[:, dc * 256 + rc * 128: dc * 256 + rc * 128 + 128],
                    y[:, dc * TOK + nt * 512: dc * TOK + nt * 512 + 512],
                    start=(dc == 0), stop=(dc == DC - 1)),
                    reads=[wov, y], writes=[pb])
            E("vector", lambda e, rc=rc, nt=ntile, pb=pb: e.tensor_copy(
                z1[:, rc * TOK + nt * 512: rc * TOK + nt * 512 + 512], pb[:, :]),
                reads=[pb], writes=[z1])

    for ntile in range(NSLOT):
        xsl = xseq.rearrange("(c p) x -> p c x", p=128)[:, :, ntile * 512:(ntile + 1) * 512]
        E("sync", lambda e, xsl=xsl: e.dma_start(
            out=xotA[:, :].rearrange("p (c x) -> p c x", c=DC), in_=xsl),
            writes=[xotA], dma=True)
        for fc in range(DC):
            pb = bk[2 + fc % 2]
            for rc in range(2):
                E("tensor", lambda e, fc=fc, nt=ntile, rc=rc, pb=pb: e.matmul(
                    pb[:, :], wou[:, rc * D + fc * 128: rc * D + fc * 128 + 128],
                    z1[:, rc * TOK + nt * 512: rc * TOK + nt * 512 + 512],
                    start=(rc == 0), stop=(rc == 1)),
                    reads=[wou, z1], writes=[pb])
            E("vector", lambda e, fc=fc, pb=pb: e.tensor_add(
                xresA[:, fc * SL:(fc + 1) * SL], pb[:, :], xotA[:, fc * SL:(fc + 1) * SL]),
                reads=[pb, xotA], writes=[xresA])
        cm.layernorm_nt(xresA, bk[6], bk[7], out_f32=xresA)
        dstv = x1d.rearrange("(c p) x -> p c x", p=128)[:, :, ntile * 512:(ntile + 1) * 512]
        E("sync", lambda e, dstv=dstv: e.dma_start(
            out=dstv, in_=xresA[:, :].rearrange("p (c x) -> p c x", c=DC)),
            reads=[xresA], writes=[(x1d, ntile)], dma=True)

    xsbB = ar.at("xsbB", 0, [128, DC * TOK], BF16)
    hB = ar.at("hB", 32768, [128, 32 * SL], BF16)
    xotB = ar.at("xotB", 65536, [128, DC * SL], F32)
    xresB = ar.at("xresB", 81920, [128, DC * SL], F32)
    s1buf = [ar.at(f"s1b{i}", 106496 + i * 2048, [128, 1024], BF16) for i in range(4)]
    s2buf = [ar.at(f"s2b{i}", 143360 + i * 2048, [128, 1024], BF16) for i in range(4)]
    otB = ar.at("otB", 115712, [128, SL], F32)
    xrbB = ar.at("xrbB", 118784, [128, DC * SL], BF16)
    x2bB = ar.at("x2bB", 126976, [128, DC * SL], BF16)
    lrowsB = ar.at("lrowsB", 135168, [1, 3 * SL], F32)
    cm.attach_ln_scratch(xrbB, x2bB, lrowsB)
    load_fm(em, xsbB, x1d, DC, TOK, eng="gpsimd", tag="ldx1")
    em.ops[-1].reads.extend([em._keys((x1d, nt))[0] for nt in range(NSLOT)])
    for nt in range(NSLOT):
        w = em.last_writer.get(em._keys((x1d, nt))[0])
        if w is not None:
            ch = ("d", w.eng, w.dma_slot) if w.is_dma else ("e", w.eng)
            cur = em.ops[-1].waits.get(ch)
            if cur is None or w.seq > cur.seq:
                em.ops[-1].waits[ch] = w

    for st in range(4):
        xsl = x1d.rearrange("(c p) x -> p c x", p=128)[:, :, st * 512:(st + 1) * 512]
        E("sync", lambda e, xsl=xsl: e.dma_start(
            out=xotB[:, :].rearrange("p (c x) -> p c x", c=DC), in_=xsl),
            reads=[(x1d, st)], writes=[xotB], dma=True)
        for hfc in range(32):
            sb_ = s1buf[hfc % 4]
            E("sync", lambda e, hfc=hfc, sb_=sb_: e.dma_start(out=sb_[:, :], in_=s1s[hfc]),
              reads=[(s1s, hfc)], writes=[sb_], dma=True)
            pb = bk[hfc % 2]
            for dc in range(DC):
                E("tensor", lambda e, dc=dc, st=st, sb_=sb_, pb=pb: e.matmul(
                    pb[:, :], sb_[:, dc * 128:(dc + 1) * 128],
                    xsbB[:, dc * TOK + st * 512: dc * TOK + st * 512 + 512],
                    start=(dc == 0), stop=(dc == DC - 1)),
                    reads=[sb_, xsbB], writes=[pb])
            E("scalar", lambda e, hfc=hfc, pb=pb: e.activation(
                hB[:, hfc * SL:(hfc + 1) * SL], pb[:, :], AF.Gelu,
                bias=b1s[:, hfc:hfc + 1], scale=a1c[:, hfc:hfc + 1]),
                reads=[pb, b1s, a1c], writes=[hB])
        for hfc in range(32):
            sb_ = s2buf[hfc % 4]
            E("scalar", lambda e, hfc=hfc, sb_=sb_: e.dma_start(
                out=sb_[:, :], in_=s2s[hfc * 128:(hfc + 1) * 128, :]),
                reads=[(s2s, hfc)], writes=[sb_], dma=True)
            for fc in range(DC):
                E("tensor", lambda e, fc=fc, hfc=hfc, sb_=sb_: e.matmul(
                    bk[fc][:, :], sb_[:, fc * 128:(fc + 1) * 128],
                    hB[:, hfc * SL:(hfc + 1) * SL],
                    start=(hfc == 0), stop=(hfc == 31)),
                    reads=[sb_, hB], writes=[bk[fc]])
        for fc in range(DC):
            E("vector", lambda e, fc=fc: e.tensor_scalar(
                otB[:, :], bk[fc][:, :], a2c[:, fc:fc + 1], b2s[:, fc:fc + 1],
                ALU.mult, ALU.add),
                reads=[bk[fc], b2s, a2c], writes=[otB])
            E("vector", lambda e, fc=fc: e.tensor_add(
                xresB[:, fc * SL:(fc + 1) * SL], otB[:, :], xotB[:, fc * SL:(fc + 1) * SL]),
                reads=[otB, xotB], writes=[xresB])
        cm.layernorm_nt(xresB, bk[6], bk[7], out_f32=xresB)
        dstv = x2d.rearrange("(c p) x -> p c x", p=128)[:, :, st * 512:(st + 1) * 512]
        E("sync", lambda e, dstv=dstv: e.dma_start(
            out=dstv, in_=xresB[:, :].rearrange("p (c x) -> p c x", c=DC)),
            reads=[xresB], writes=[(x2d, st)], dma=True)

    xsbC = ar.at("xsbC", 0, [128, DC * TOK], BF16)
    wTC = ar.at("wTC", 32768, [128, DC * D], BF16)
    qtC = ar.at("qtC", 49152, [128, DC * SL], BF16)
    sqvC = ar.at("sqvC", 57344, [128, SL], BF16)
    q2rC = ar.at("q2rC", 58368, [1, SL], F32)
    KoC = ar.at("KoC", 160768, [128, DC * TOK], BF16)

    load_fm(em, xsbC, x2d, DC, TOK, eng="gpsimd", tag="ldx2")
    em.ops[-1].reads.extend([em._keys((x2d, nt))[0] for nt in range(NSLOT)])
    for nt in range(NSLOT):
        w = em.last_writer.get(em._keys((x2d, nt))[0])
        if w is not None:
            ch = ("d", w.eng, w.dma_slot) if w.is_dma else ("e", w.eng)
            cur = em.ops[-1].waits.get(ch)
            if cur is None or w.seq > cur.seq:
                em.ops[-1].waits[ch] = w

    load_fm(em, wTC, wvT, DC, D, eng="gpsimd", tag="ldw")
    for tc in range(TOK // 128):
        for half in range(2):
            pb = bk[half]
            for dc in range(DC):
                E("tensor", lambda e, tc=tc, half=half, dc=dc, pb=pb: e.matmul(
                    pb[:, :], xsbC[:, dc * TOK + tc * 128: dc * TOK + tc * 128 + 128],
                    wTC[:, dc * D + half * 512: dc * D + half * 512 + 512],
                    start=(dc == 0), stop=(dc == DC - 1)),
                    reads=[wTC, xsbC], writes=[pb])
            E("scalar", lambda e, half=half, pb=pb: e.activation(
                qtC[:, half * 512: half * 512 + 512], pb[:, :], AF.Copy),
                reads=[pb], writes=[qtC])
        E("sync", lambda e, tc=tc: e.dma_start(
            out=eV.ap()[tc * 128:(tc + 1) * 128, :], in_=qtC[:, 0:D]),
            reads=[qtC], writes=[(eV, tc)], dma=True)
    grp = [[0, 1], [2, 3], [4, 5], [6, 7]]
    E("gpsimd", lambda e: e.collective_compute(
        "AllGather", ALU.bypass, replica_groups=grp,
        ins=[eV.ap().opt()], outs=[gV.ap().opt()]),
        reads=[(eV, tc) for tc in range(TOK // 128)], writes=[gV], tag="cc")

    for wsrc, with_q2 in ((wqT, True), (wkT, False)):
        load_fm(em, wTC, wsrc, DC, D, eng="gpsimd", tag="ldw")
        for ntile in range(NSLOT):
            for fc in range(DC):
                pb = bk[fc % 2]
                for dc in range(DC):
                    E("tensor", lambda e, fc=fc, nt=ntile, dc=dc, pb=pb: e.matmul(
                        pb[:, :], wTC[:, dc * D + fc * 128: dc * D + fc * 128 + 128],
                        xsbC[:, dc * TOK + nt * 512: dc * TOK + nt * 512 + 512],
                        start=(dc == 0), stop=(dc == DC - 1)),
                        reads=[wTC, xsbC], writes=[pb])
                if with_q2:
                    E("scalar", lambda e, fc=fc, pb=pb: e.activation(
                        qtC[:, fc * SL:(fc + 1) * SL], pb[:, :], AF.Copy),
                        reads=[pb], writes=[qtC])
                else:
                    E("scalar", lambda e, fc=fc, nt=ntile, pb=pb: e.activation(
                        KoC[:, fc * TOK + nt * 512: fc * TOK + nt * 512 + 512], pb[:, :], AF.Copy),
                        reads=[pb], writes=[KoC])
            if with_q2:
                for j4 in range(4):
                    r0 = (ntile * 4 + j4) * 128
                    dstv = eQ.ap()[r0:r0 + 128, :].rearrange("p (dc c) -> p dc c", dc=DC)
                    srcv = qtC[:, :].rearrange("p (dc j c) -> p j dc c", dc=DC, j=4)[:, j4]
                    E("sync", lambda e, dstv=dstv, srcv=srcv: e.dma_start(out=dstv, in_=srcv),
                        reads=[qtC], writes=[(eQ, ntile)], dma=True)
                pq = bk[2]
                for fc in range(DC):
                    E("scalar", lambda e, fc=fc: e.activation(
                        sqvC[:, :], qtC[:, fc * SL:(fc + 1) * SL], AF.Square),
                        reads=[qtC], writes=[sqvC])
                    E("tensor", lambda e, fc=fc, pq=pq: e.matmul(
                        pq[0:1, :], cm.ones_colb[:, :], sqvC[:, :],
                        start=(fc == 0), stop=(fc == DC - 1)),
                        reads=[sqvC, cm.ones_colb], writes=[pq])
                E("vector", lambda e, pq=pq: e.tensor_copy(q2rC[0:1, :], pq[0:1, :]),
                  reads=[pq], writes=[q2rC])
                E("sync", lambda e, nt=ntile: e.dma_start(
                    out=eq2.ap()[0:1, nt * 512:(nt + 1) * 512], in_=q2rC[0:1, :]),
                    reads=[q2rC], writes=[(eq2, ntile)], dma=True)

        if with_q2:
            grp = [[0, 1], [2, 3], [4, 5], [6, 7]]
            E("gpsimd", lambda e: e.collective_compute(
                "AllGather", ALU.bypass, replica_groups=grp,
                ins=[eQ.ap().opt()], outs=[gQ.ap().opt()]),
                reads=[(eQ, nt) for nt in range(NSLOT)], writes=[gQ], tag="cc")
            E("gpsimd", lambda e: e.collective_compute(
                "AllGather", ALU.bypass, replica_groups=grp,
                ins=[eq2.ap().opt()], outs=[gq2.ap().opt()]),
                reads=[(eq2, nt) for nt in range(NSLOT)], writes=[gq2], tag="cc")



    w1e = ar.at("w1e", 115712, [128, DC * 8 * 256], BF16)
    load_fm(em, w1e, W1f, DC, 8 * 256, eng="gpsimd")
    qbufC = [ar.at(f"qbC{i}", i * 2048, [128, DC * 128], BF16) for i in range(4)]
    vbufC = [ar.at(f"vbC{i}", 8192 + i * 2048, [128, D], BF16) for i in range(4)]
    mbufC = [ar.at(f"mbC{i}", 16384 + i * 1024, [128, SL], BF16) for i in range(2)]
    PpC = ar.at("PpC", 18432, [128, 32 * SL], BF16)
    xotC = ar.at("xotC", 51200, [128, DC * SL], F32)
    xresC = ar.at("xresC", 67584, [128, DC * SL], F32)
    xrbC = ar.at("xrbC", 83968, [128, DC * SL], BF16)
    x2bC = ar.at("x2bC", 92160, [128, DC * SL], BF16)
    lrowsC = ar.at("lrowsC", 100352, [1, 3 * SL], F32)
    sqv2 = ar.at("sqv2", 106496, [128, SL], BF16)
    scrC = ar.at("scrC", 107520, [1, SL], F32)
    scaC = ar.at("scaC", 109568, [1, 16], F32)
    negcC = ar.at("negcC", 109632, [128, 1], F32)
    rlC = ar.at("rlC", 109696, [1, SL], F32)
    rlbC = ar.at("rlbC", 111744, [128, SL], F32)
    cm.attach_ln_scratch(xrbC, x2bC, lrowsC)

    mk2, mq2 = scaC[0:1, 0:1], scaC[0:1, 1:2]
    prd, sqr = scaC[0:1, 2:3], scaC[0:1, 3:4]
    ngc, cur = scaC[0:1, 4:5], scaC[0:1, 5:6]
    E("vector", lambda e, a=mk2: e.memset(a, 0.0), writes=[scaC])
    E("vector", lambda e, a=mq2: e.memset(a, 0.0), writes=[scaC])
    for ntile in range(NSLOT):
        pq = bk[0]
        for fc in range(DC):
            E("scalar", lambda e, fc=fc, nt=ntile: e.activation(
                sqv2[:, :], KoC[:, fc * TOK + nt * 512: fc * TOK + nt * 512 + 512], AF.Square),
                reads=[KoC], writes=[sqv2])
            E("tensor", lambda e, fc=fc, pq=pq: e.matmul(
                pq[0:1, :], cm.ones_colb[:, :], sqv2[:, :],
                start=(fc == 0), stop=(fc == DC - 1)),
                reads=[sqv2, cm.ones_colb], writes=[pq])
        E("vector", lambda e, pq=pq: e.tensor_copy(scrC[0:1, :], pq[0:1, :]),
          reads=[pq], writes=[scrC])
        E("vector", lambda e, a=cur: e.reduce_max(a, scrC[0:1, :], axis=AX.X),
          reads=[scrC], writes=[scaC])
        E("vector", lambda e, a=mk2, b=cur: e.tensor_max(a, a, b), reads=[scaC], writes=[scaC])
    for ntile in range(S // 512):
        gsl = gq2.ap()[ntile // 4: ntile // 4 + 1, (ntile % 4) * 512:(ntile % 4) * 512 + 512]
        E("sync", lambda e, gsl=gsl: e.dma_start(out=scrC[0:1, :], in_=gsl),
          reads=[gq2], writes=[scrC], dma=True)
        E("vector", lambda e, a=cur: e.reduce_max(a, scrC[0:1, :], axis=AX.X),
          reads=[scrC], writes=[scaC])
        E("vector", lambda e, a=mq2, b=cur: e.tensor_max(a, a, b), reads=[scaC], writes=[scaC])
    E("vector", lambda e, a=prd, b=mk2, c=mq2: e.tensor_mul(a, b, c), reads=[scaC], writes=[scaC])
    E("scalar", lambda e, a=sqr, b=prd: e.activation(a, b, AF.Sqrt), reads=[scaC], writes=[scaC])
    E("vector", lambda e, a=ngc, b=sqr: e.tensor_scalar(a, b, -1.0 / 32.0, None, ALU.mult),
      reads=[scaC], writes=[scaC])
    E("tensor", lambda e, a=ngc: e.matmul(bk[3][:, 0:1], cm.ones_row[:, :], a, start=True, stop=True),
      reads=[scaC, cm.ones_row], writes=[bk[3]])
    E("vector", lambda e: e.tensor_copy(negcC[:, :], bk[3][:, 0:1]),
      reads=[bk[3]], writes=[negcC])

    for it in range(NSLOT):
        def qload(jc):
            qb_ = qbufC[jc % 4]
            r0 = jc * 128 if jc < 16 else TOK + (jc - 16) * 128
            E("sync", lambda e, r0=r0, qb_=qb_: e.dma_start(
                out=qb_[:, :], in_=gQ.ap()[r0:r0 + 128, :]),
                reads=[gQ], writes=[qb_], dma=True)
        def sgen(jc):
            qb_ = qbufC[jc % 4]
            sb_ = bk[jc % 2]
            for dc in range(DC):
                E("tensor", lambda e, jc=jc, it=it, dc=dc, sb_=sb_, qb_=qb_: e.matmul(
                    sb_[:, :], qb_[:, dc * 128:(dc + 1) * 128],
                    KoC[:, dc * TOK + it * 512: dc * TOK + it * 512 + 512],
                    start=(dc == 0), stop=(dc == DC - 1)),
                    reads=[qb_, KoC], writes=[sb_])
            E("scalar", lambda e, jc=jc, sb_=sb_: e.activation(
                PpC[:, jc * SL:(jc + 1) * SL], sb_[:, :], AF.Exp,
                bias=negcC[:, 0:1], scale=1.0 / 32.0),
                reads=[sb_, negcC], writes=[(PpC, ("B", jc * SL * 2, (jc + 1) * SL * 2))])
        def lacc(jc):
            E("tensor", lambda e, jc=jc: e.matmul(
                bk[2][0:1, :], cm.ones_colb[:, :], PpC[:, jc * SL:(jc + 1) * SL],
                start=(jc == 0), stop=(jc == 31)),
                reads=[(PpC, ("B", jc * SL * 2, (jc + 1) * SL * 2)), cm.ones_colb], writes=[bk[2]])
        inc_list = [g_ * 4 + cl for g_ in INC_TILES[it] for cl in range(4)]
        inc_pos = {jc: k for k, jc in enumerate(inc_list)}
        msk = {}
        for mi, g_ in enumerate(MSK_TILES[it]):
            for cl in range(4):
                msk[g_ * 4 + cl] = mi * 4 + cl
        nI = len(inc_list)
        def vload(k, half):
            if k >= nI:
                return
            jc = inc_list[k]
            vb_ = vbufC[k % 4]
            r0 = jc * 128 if jc < 16 else TOK + (jc - 16) * 128
            E("scalar", lambda e, r0=r0, vb_=vb_: e.dma_start(
                out=vb_[:, :], in_=gV.ap()[r0:r0 + 128, :]),
                reads=[gV], writes=[vb_], dma=True)
        def do_mask(jc):
            mb_ = mbufC[msk[jc] % 2]
            E("sync", lambda e, it=it, m=msk[jc], mb_=mb_: e.dma_start(
                out=mb_[:, :], in_=cmask[it, m]),
                writes=[mb_], dma=True)
            E("vector", lambda e, jc=jc, mb_=mb_: e.tensor_mul(
                PpC[:, jc * SL:(jc + 1) * SL], PpC[:, jc * SL:(jc + 1) * SL], mb_[:, :]),
                reads=[(PpC, ("B", jc * SL * 2, (jc + 1) * SL * 2)), mb_],
                writes=[(PpC, ("B", jc * SL * 2, (jc + 1) * SL * 2))])
        def avmm(jc, half, k):
            vb_ = vbufC[k % 4]
            for dvl in range(4):
                dv = half * 4 + dvl
                E("tensor", lambda e, jc=jc, dv=dv, dvl=dvl, vb_=vb_, k=k, nI=nI: e.matmul(
                    bk[4 + dvl][:, :], vb_[:, dv * 128:(dv + 1) * 128],
                    PpC[:, jc * SL:(jc + 1) * SL],
                    start=(k == 0), stop=(k == nI - 1)),
                    reads=[vb_, (PpC, ("B", jc * SL * 2, (jc + 1) * SL * 2))], writes=[bk[4 + dvl]])
        xsl = x2d.rearrange("(c p) x -> p c x", p=128)[:, :, it * 512:(it + 1) * 512]
        E("sync", lambda e, xsl=xsl: e.dma_start(
            out=xotC[:, :].rearrange("p (c x) -> p c x", c=DC), in_=xsl),
            reads=[(x2d, it)], writes=[xotC], dma=True)
        qload(0); qload(1); qload(2)
        vload(0, 0); vload(1, 0)
        sgen(0)
        for jc in range(1, 32):
            if jc + 2 < 32:
                qload(jc + 2)
            sgen(jc)
            lacc(jc - 1)
            d = jc - 2
            if d in inc_pos:
                if d in msk:
                    do_mask(d)
                k = inc_pos[d]
                vload(k + 2, 0)
                avmm(d, 0, k)
        lacc(31)
        for d in (30, 31):
            if d in inc_pos:
                if d in msk:
                    do_mask(d)
                avmm(d, 0, inc_pos[d])
        E("vector", lambda e: e.tensor_copy(scrC[0:1, :], bk[2][0:1, :]),
          reads=[bk[2]], writes=[scrC])
        E("vector", lambda e: e.reciprocal(rlC[0:1, :], scrC[0:1, :]),
          reads=[scrC], writes=[rlC])
        E("tensor", lambda e: e.matmul(bk[3][:, :], cm.ones_row[:, :], rlC[0:1, :],
                                       start=True, stop=True),
          reads=[rlC, cm.ones_row], writes=[bk[3]])
        E("vector", lambda e: e.tensor_copy(rlbC[:, :], bk[3][:, :]),
          reads=[bk[3]], writes=[rlbC])
        for dvl in range(4):
            E("vector", lambda e, dvl=dvl: e.tensor_mul(
                cm.tmp[:, :], bk[4 + dvl][:, :], rlbC[:, :]),
                reads=[bk[4 + dvl], rlbC], writes=[cm.tmp])
            E("vector", lambda e, dv=dvl: e.tensor_add(
                xresC[:, dv * SL:(dv + 1) * SL], cm.tmp[:, :], xotC[:, dv * SL:(dv + 1) * SL]),
                reads=[cm.tmp, xotC], writes=[xresC])
        vload(0, 1); vload(1, 1)
        for k, jc in enumerate(inc_list):
            vload(k + 2, 1)
            avmm(jc, 1, k)
        for dvl in range(4):
            dv = 4 + dvl
            E("vector", lambda e, dvl=dvl: e.tensor_mul(
                cm.tmp[:, :], bk[4 + dvl][:, :], rlbC[:, :]),
                reads=[bk[4 + dvl], rlbC], writes=[cm.tmp])
            E("vector", lambda e, dv=dv: e.tensor_add(
                xresC[:, dv * SL:(dv + 1) * SL], cm.tmp[:, :], xotC[:, dv * SL:(dv + 1) * SL]),
                reads=[cm.tmp, xotC], writes=[xresC])
        cm.layernorm_nt(xresC, bk[2], bk[3], out_f32=xresC)
        dstv = x3d.rearrange("(c p) x -> p c x", p=128)[:, :, it * 512:(it + 1) * 512]
        E("sync", lambda e, dstv=dstv: e.dma_start(
            out=dstv, in_=xresC[:, :].rearrange("p (c x) -> p c x", c=DC)),
            reads=[xresC], writes=[(x3d, it)], dma=True)

    w2e = ar.at("w2e", 32768, [128, 2 * 8 * D], BF16)
    xotD = ar.at("xotD", 65536, [128, DC * SL], F32)
    xresD = ar.at("xresD", 81920, [128, DC * SL], F32)
    xebD2 = ar.at("xebD2", 98304, [128, DC * SL], BF16)
    xhatD = ar.at("xhatD", 24576, [128, DC * SL], BF16)
    xebD = ar.at("xebD", 16384, [128, DC * SL], BF16)
    hallD = ar.at("hallD", 0, [128, 16 * SL], BF16)
    xrbD = ar.at("xrbD", 168960, [128, DC * SL], BF16)
    x2bD = ar.at("x2bD", 177152, [128, DC * SL], BF16)
    lrowsD = ar.at("lrowsD", 106496, [1, 3 * SL], F32)
    wgD = ar.at("wgD", 161792, [128, DC * 8], F32)
    lnsD = ar.at("lnsD", 162048, [128, DC * 8], F32)
    lnbD = ar.at("lnbD", 162304, [128, DC * 8], F32)
    b1sD = ar.at("b1sD", 162560, [128, 2 * 8], F32)
    b2bD = ar.at("b2bD", 162688, [8, D], BF16)
    idtD = ar.at("idtD", 164736, [128, 128], F32)
    indbD = ar.at("indbD", 165248, [8, 1024], BF16)
    rmxD = ar.at("rmxD", 167296, [128, 1], F32)
    mtmD = ar.at("mtmD", 167360, [128, 8], F32)
    maskbD = ar.at("maskbD", 167424, [8, SL], BF16)
    cm.attach_ln_scratch(xrbD, x2bD, lrowsD)

    load_fm(em, wgD, WgT, DC, 8, eng="sync")
    load_fm(em, lnsD, lnsI, DC, 8, eng="sync")
    load_fm(em, lnbD, lnbI, DC, 8, eng="sync")
    load_fm(em, b1sD, b1f, 2, 8, eng="sync")
    load_fm(em, w2e, W2f, 2, 8 * D, eng="gpsimd")
    E("gpsimd", lambda e: e.dma_start(out=b2bD[:, :], in_=b2I), writes=[b2bD], dma=True)
    E("sync", lambda e: e.dma_start(out=idtD[:, :], in_=idI), writes=[idtD], dma=True)
    E("sync", lambda e: e.dma_start(out=indbD[:, :], in_=indI), writes=[indbD], dma=True)

    lnbBD = ar.at("lnbBD", 168448, [128, DC * 8], BF16)
    bcD = ar.at("bcD", 168640, [128, 2 * 8], F32)
    E("vector", lambda e: e.tensor_copy(lnbBD[:, :], lnbD[:, :]),
      reads=[lnbD], writes=[lnbBD])
    for ex in range(8):
        for hfc in range(2):
            pb = bk[1 + hfc]
            for dc in range(DC):
                E("tensor", lambda e, ex=ex, hfc=hfc, dc=dc, pb=pb: e.matmul(
                    pb[:, 0:1],
                    w1e[:, dc * 2048 + ex * 256 + hfc * 128: dc * 2048 + ex * 256 + hfc * 128 + 128],
                    lnbBD[:, dc * 8 + ex: dc * 8 + ex + 1],
                    start=(dc == 0), stop=(dc == DC - 1)),
                    reads=[w1e, lnbBD], writes=[pb])
            E("vector", lambda e, ex=ex, hfc=hfc, pb=pb: e.tensor_add(
                bcD[:, hfc * 8 + ex: hfc * 8 + ex + 1], pb[:, 0:1],
                b1sD[:, hfc * 8 + ex: hfc * 8 + ex + 1]),
                reads=[pb, b1sD], writes=[bcD])
    for ex in range(8):
        for dc in range(DC):
            E("vector", lambda e, ex=ex, dc=dc: e.tensor_scalar(
                w1e[:, dc * 2048 + ex * 256: dc * 2048 + ex * 256 + 256],
                w1e[:, dc * 2048 + ex * 256: dc * 2048 + ex * 256 + 256],
                lnsD[:, dc * 8 + ex: dc * 8 + ex + 1], None, ALU.mult),
                reads=[w1e, lnsD], writes=[w1e])

    for tt in range(NSLOT):
        xsl = x3d.rearrange("(c p) x -> p c x", p=128)[:, :, tt * 512:(tt + 1) * 512]
        E("sync", lambda e, xsl=xsl: e.dma_start(
            out=xotD[:, :].rearrange("p (c x) -> p c x", c=DC), in_=xsl),
            reads=[(x3d, tt)], writes=[xotD], dma=True)
        for m4 in range(4):
            pg = bk[0]
            for dc in range(DC):
                E("tensor", lambda e, m4=m4, dc=dc, pg=pg: e.matmul(
                    pg[:, 0:8], xotD[:, dc * SL + m4 * 128: dc * SL + m4 * 128 + 128],
                    wgD[:, dc * 8:(dc + 1) * 8],
                    start=(dc == 0), stop=(dc == DC - 1)),
                    reads=[xotD, wgD], writes=[pg])
            E("vector", lambda e, pg=pg: e.reduce_max(rmxD[:, :], pg[:, 0:8], axis=AX.X),
              reads=[pg], writes=[rmxD])
            E("vector", lambda e, pg=pg: e.tensor_scalar(
                mtmD[:, :], pg[:, 0:8], rmxD[:, 0:1], None, ALU.is_ge),
                reads=[pg, rmxD], writes=[mtmD])
            E("tensor", lambda e: e.transpose(bk[3][0:8, 0:128], mtmD[:, :], idtD[:, :]),
              reads=[mtmD, idtD], writes=[bk[3]])
            E("vector", lambda e, m4=m4: e.tensor_copy(
                maskbD[0:8, m4 * 128:(m4 + 1) * 128], bk[3][0:8, 0:128]),
                reads=[bk[3]], writes=[maskbD])
        cm.layernorm_nt(xotD, bk[6], bk[7], out_f32=None, out_bf16=xhatD)
        for ex in range(8):
            mbk = bk[3] if ex % 2 == 0 else bk[0]
            E("tensor", lambda e, ex=ex, mbk=mbk: e.matmul(
                mbk[:, :], indbD[0:8, ex * 128:(ex + 1) * 128], maskbD[0:8, :],
                start=True, stop=True),
                reads=[indbD, maskbD], writes=[mbk])
            for hfc in range(2):
                ph = bk[1 + hfc]
                for dc in range(DC):
                    E("tensor", lambda e, ex=ex, hfc=hfc, dc=dc, ph=ph: e.matmul(
                        ph[:, :], w1e[:, dc * 2048 + ex * 256 + hfc * 128: dc * 2048 + ex * 256 + hfc * 128 + 128],
                        xhatD[:, dc * SL:(dc + 1) * SL],
                        start=(dc == 0), stop=(dc == DC - 1)),
                        reads=[w1e, xhatD], writes=[ph])
                E("scalar", lambda e, ex=ex, hfc=hfc, ph=ph: e.activation(
                    hallD[:, (ex * 2 + hfc) * SL:(ex * 2 + hfc + 1) * SL], ph[:, :], AF.Gelu,
                    bias=bcD[:, hfc * 8 + ex: hfc * 8 + ex + 1]),
                    reads=[ph, bcD], writes=[hallD])
                E("vector", lambda e, ex=ex, hfc=hfc, mbk=mbk: e.tensor_mul(
                    hallD[:, (ex * 2 + hfc) * SL:(ex * 2 + hfc + 1) * SL],
                    hallD[:, (ex * 2 + hfc) * SL:(ex * 2 + hfc + 1) * SL], mbk[:, :]),
                    reads=[hallD, mbk], writes=[hallD])
        for fc in range(DC):
            py = bk[4 + fc % 2]
            for ex in range(8):
                for hfc in range(2):
                    E("tensor", lambda e, fc=fc, ex=ex, hfc=hfc, py=py: e.matmul(
                        py[:, :], w2e[:, hfc * 8 * D + ex * D + fc * 128: hfc * 8 * D + ex * D + fc * 128 + 128],
                        hallD[:, (ex * 2 + hfc) * SL:(ex * 2 + hfc + 1) * SL],
                        start=(ex == 0 and hfc == 0), stop=False),
                        reads=[w2e, hallD], writes=[py])
            E("tensor", lambda e, fc=fc, py=py: e.matmul(
                py[:, :], b2bD[0:8, fc * 128:(fc + 1) * 128], maskbD[0:8, :],
                start=False, stop=True),
                reads=[b2bD, maskbD], writes=[py])
            E("vector", lambda e, fc=fc, py=py: e.tensor_add(
                xresD[:, fc * SL:(fc + 1) * SL], py[:, :], xotD[:, fc * SL:(fc + 1) * SL]),
                reads=[py, xotD], writes=[xresD])
        cm.layernorm_nt(xresD, bk[6], bk[7], out_f32=xresD)
        dstv = xout.rearrange("(c p) x -> p c x", p=128)[:, :, tt * 512:(tt + 1) * 512]
        E("sync", lambda e, dstv=dstv: e.dma_start(
            out=dstv, in_=xresD[:, :].rearrange("p (c x) -> p c x", c=DC)),
            reads=[xresD], dma=True, tag="out")

    if DEBUG_F:
        deq = _out(nc, "dbg_eQ", [TOK, D], BF16)
        E("sync", lambda e: e.dma_start(out=deq, in_=eQ.ap()),
          reads=[(eQ, nt) for nt in range(NSLOT)], dma=True, tag="out")
        dgq = _out(nc, "dbg_gQ", [2 * TOK, D], BF16)
        E("sync", lambda e: e.dma_start(out=dgq, in_=gQ.ap()), reads=[gQ], dma=True, tag="out")
        for nm, t_ in (("dbg_x1", x1d), ("dbg_x2", x2d), ("dbg_x3", x3d)):
            o_ = _out(nc, nm, [D, TOK])
            E("sync", lambda e, o_=o_, t_=t_: e.dma_start(out=o_, in_=t_),
              reads=[(t_, nt) for nt in range(NSLOT)], dma=True, tag="out")
    outs = [op for op in em.ops if op.tag == "out"]
    em.emit("sync", lambda e: e.nop(), after=outs, tag="final")
    em.build("F")
    return nc


_NC_CACHE = {}


def _get_nc(name):
    if name not in _NC_CACHE:
        _NC_CACHE[name] = {"A": build_A, "B": build_B, "C1": build_C1,
                           "C2": build_C2, "D": build_D, "F": build_F}[name]()
    return _NC_CACHE[name]


def _core_slices(x, c):
    b, par = c // 2, c % 2
    return np.concatenate([x[b, 512 * p:512 * p + 512, :] for p in POS[par]], 0)


def _build_cmask(par):
    m = np.zeros((4, 8, 128, 512), np.float32)
    tri = (np.arange(512)[None, :] >= np.arange(512)[:, None]).astype(np.float32)
    for it in range(4):
        p = POS[par][it]
        for mi, g in enumerate(MSK_TILES[it]):
            q = GPOS[g]
            if q < p:
                m[it, mi * 4:(mi + 1) * 4] = 1.0
            elif q == p:
                for cl in range(4):
                    m[it, mi * 4 + cl] = tri[cl * 128:(cl + 1) * 128, :]
    return m.astype(ml_dtypes.bfloat16)


FUSED = True


def kernel(**inputs):
    if FUSED:
        return kernel_fused(**inputs)
    return kernel_unfused(**inputs)


def kernel_fused(L0_attn_x, L0_attn_W_qkv_v, L0_attn_W_qkv_u, L0_attn_W_o_v,
                 L0_attn_W_o_u, L0_ffn_w1, L0_ffn_b1, L0_ffn_w2, L0_ffn_b2,
                 L1_attn_wq, L1_attn_wk, L1_attn_wv, L1_ffn_Wg, L1_ffn_ln_scale,
                 L1_ffn_ln_bias, L1_ffn_W1, L1_ffn_b1, L1_ffn_W2, L1_ffn_b2):
    f32 = np.float32
    x = np.asarray(L0_attn_x, f32)
    cores = list(range(NCORE))
    Wg = np.asarray(L1_ffn_Wg, f32)
    W1 = np.asarray(L1_ffn_W1, f32)
    W2 = np.asarray(L1_ffn_W2, f32)
    ind = np.zeros((8, 1024), f32)
    for e in range(8):
        ind[e, e * 128:(e + 1) * 128] = 1.0
    shared = {
        "WvT": np.ascontiguousarray(np.asarray(L0_attn_W_qkv_v, f32).T),
        "WuT": np.ascontiguousarray(np.asarray(L0_attn_W_qkv_u, f32).T),
        "WovT": np.ascontiguousarray(np.asarray(L0_attn_W_o_v, f32).T),
        "WouT": np.ascontiguousarray(np.asarray(L0_attn_W_o_u, f32).T),
        "w1T": np.ascontiguousarray(np.asarray(L0_ffn_w1, f32).T),
        "w2T": np.ascontiguousarray(np.asarray(L0_ffn_w2, f32).T),
        "b1c": np.ascontiguousarray(np.asarray(L0_ffn_b1, f32).reshape(32, 128).T),
        "b2c": np.ascontiguousarray(np.asarray(L0_ffn_b2, f32).reshape(8, 128).T),
        "wqT": np.ascontiguousarray(np.asarray(L1_attn_wq, f32).T),
        "wkT": np.ascontiguousarray(np.asarray(L1_attn_wk, f32).T),
        "wvT": np.ascontiguousarray(np.asarray(L1_attn_wv, f32).T),
        "WgT": np.ascontiguousarray(Wg.T),
        "lns": np.ascontiguousarray(np.asarray(L1_ffn_ln_scale, f32).T),
        "lnb": np.ascontiguousarray(np.asarray(L1_ffn_ln_bias, f32).T),
        "W1f": np.ascontiguousarray(W1.transpose(1, 0, 2).reshape(1024, 8 * 256)),
        "b1f": np.ascontiguousarray(np.asarray(L1_ffn_b1, f32).T),
        "W2f": np.ascontiguousarray(W2.transpose(1, 0, 2).reshape(256, 8 * 1024)),
        "b2": np.ascontiguousarray(np.asarray(L1_ffn_b2, f32)),
        "id128": np.eye(128, dtype=f32),
        "ind": ind.astype(ml_dtypes.bfloat16),
    }
    in_maps = []
    for c in cores:
        own = _core_slices(x, c)
        peer = _core_slices(x, c ^ 1)
        xseq = np.ascontiguousarray(np.concatenate([own, peer], 0).T)
        in_maps.append({"xseq": xseq, "cmask": _build_cmask(c % 2), **shared})
    res = run_bass_kernel_spmd(_get_nc("F"), in_maps, core_ids=cores)
    global _LAST_RES
    _LAST_RES = res
    out = np.empty((B, S, D), f32)
    for c in cores:
        b, par = c // 2, c % 2
        got = np.asarray(res.results[c]["xout"]).T
        for si, p in enumerate(POS[par]):
            out[b, 512 * p:512 * p + 512, :] = got[si * 512:(si + 1) * 512, :]
    return out


def kernel_unfused(L0_attn_x, L0_attn_W_qkv_v, L0_attn_W_qkv_u, L0_attn_W_o_v,
           L0_attn_W_o_u, L0_ffn_w1, L0_ffn_b1, L0_ffn_w2, L0_ffn_b2,
           L1_attn_wq, L1_attn_wk, L1_attn_wv, L1_ffn_Wg, L1_ffn_ln_scale,
           L1_ffn_ln_bias, L1_ffn_W1, L1_ffn_b1, L1_ffn_W2, L1_ffn_b2):
    f32 = np.float32
    x = np.asarray(L0_attn_x, f32)
    cores = list(range(NCORE))

    WvT = np.ascontiguousarray(np.asarray(L0_attn_W_qkv_v, f32).T)
    WuT = np.ascontiguousarray(np.asarray(L0_attn_W_qkv_u, f32).T)
    WovT = np.ascontiguousarray(np.asarray(L0_attn_W_o_v, f32).T)
    WouT = np.ascontiguousarray(np.asarray(L0_attn_W_o_u, f32).T)
    in_maps = []
    for c in cores:
        own = _core_slices(x, c)
        peer = _core_slices(x, c ^ 1)
        xseq = np.ascontiguousarray(np.concatenate([own, peer], 0).T)
        in_maps.append({"xseq": xseq, "WvT": WvT, "WuT": WuT, "WovT": WovT, "WouT": WouT})
    rA = run_bass_kernel_spmd(_get_nc("A"), in_maps, core_ids=cores)
    x1c = [np.asarray(rA.results[c]["x1"]) for c in cores]

    w1T = np.ascontiguousarray(np.asarray(L0_ffn_w1, f32).T)
    w2T = np.ascontiguousarray(np.asarray(L0_ffn_w2, f32).T)
    b1c = np.ascontiguousarray(np.asarray(L0_ffn_b1, f32).reshape(32, 128).T)
    b2c = np.ascontiguousarray(np.asarray(L0_ffn_b2, f32).reshape(8, 128).T)
    in_maps = [{"x1": x1c[c], "w1T": w1T, "w2T": w2T, "b1c": b1c, "b2c": b2c}
               for c in cores]
    rB = run_bass_kernel_spmd(_get_nc("B"), in_maps, core_ids=cores)
    x2c = [np.asarray(rB.results[c]["x2"]) for c in cores]

    wqT = np.ascontiguousarray(np.asarray(L1_attn_wq, f32).T)
    wkT = np.ascontiguousarray(np.asarray(L1_attn_wk, f32).T)
    wvT = np.ascontiguousarray(np.asarray(L1_attn_wv, f32).T)
    in_maps = [{"x2": x2c[c], "wqT": wqT, "wkT": wkT, "wvT": wvT} for c in cores]
    rC1 = run_bass_kernel_spmd(_get_nc("C1"), in_maps, core_ids=cores)

    in_maps = []
    for c in cores:
        b, par = c // 2, c % 2
        e, o = 2 * b, 2 * b + 1
        Qf = np.concatenate([rC1.results[e]["Qo"], rC1.results[o]["Qo"]], axis=1)
        Vf = np.concatenate([rC1.results[e]["Vo"], rC1.results[o]["Vo"]], axis=0)
        q2f = np.concatenate([rC1.results[e]["q2o"], rC1.results[o]["q2o"]], axis=1)
        in_maps.append({"x2": x2c[c], "Ko": rC1.results[c]["Ko"], "Qf": Qf,
                        "Vf": Vf, "q2f": q2f, "cmask": _build_cmask(par)})
    rC2 = run_bass_kernel_spmd(_get_nc("C2"), in_maps, core_ids=cores)
    x3c = [np.asarray(rC2.results[c]["x3"]) for c in cores]

    Wg = np.asarray(L1_ffn_Wg, f32)
    W1 = np.asarray(L1_ffn_W1, f32)
    W2 = np.asarray(L1_ffn_W2, f32)
    ind = np.zeros((8, 1024), f32)
    for e in range(8):
        ind[e, e * 128:(e + 1) * 128] = 1.0
    shared = {
        "WgT": np.ascontiguousarray(Wg.T),
        "lns": np.ascontiguousarray(np.asarray(L1_ffn_ln_scale, f32).T),
        "lnb": np.ascontiguousarray(np.asarray(L1_ffn_ln_bias, f32).T),
        "W1f": np.ascontiguousarray(W1.transpose(1, 0, 2).reshape(1024, 8 * 256)),
        "b1f": np.ascontiguousarray(np.asarray(L1_ffn_b1, f32).T),
        "W2f": np.ascontiguousarray(W2.transpose(1, 0, 2).reshape(256, 8 * 1024)),
        "b2": np.ascontiguousarray(np.asarray(L1_ffn_b2, f32)),
        "id128": np.eye(128, dtype=f32),
        "ind": ind.astype(ml_dtypes.bfloat16),
    }
    in_maps = [{"x3": x3c[c], **shared} for c in cores]
    rD = run_bass_kernel_spmd(_get_nc("D"), in_maps, core_ids=cores)

    out = np.empty((B, S, D), f32)
    for c in cores:
        b, par = c // 2, c % 2
        got = np.asarray(rD.results[c]["xout"]).T
        for si, p in enumerate(POS[par]):
            out[b, 512 * p:512 * p + 512, :] = got[si * 512:(si + 1) * 512, :]
    return out
